# version 09
# baseline (speedup 1.0000x reference)
import contextlib
import numpy as np
import ml_dtypes
import bass_rust
import concourse.bass as bass
import concourse.mybir as mybir
from concourse.bass import IndirectOffsetOnAxis
from concourse.bass_utils import run_bass_kernel_spmd

F32 = mybir.dt.float32
BF16 = mybir.dt.bfloat16
U32 = mybir.dt.uint32
AF = mybir.ActivationFunctionType
OP = mybir.AluOpType
AX = mybir.AxisListType

L = 2048
NT = 16
D = 1024
NH = 16
E = 64
CAP = 128
NBLK = 96
SLOTS = NBLK * CAP
DH = 4096
WIN = 224
PADL = WIN + 127
SCALE = 0.125


def fancy(base_ap, axes, extra_off=0):
    ap = base_ap.copy()
    part = list(ap.ap)[0]
    ap.ap = bass_rust.VecI64Pair([list(part)] + [list(a) for a in axes])
    ap.offset = ap.offset + extra_off
    return ap


def _prep_weights(inp):
    f32 = lambda a: np.ascontiguousarray(np.asarray(a), dtype=np.float32)
    w = {}
    Wq, Wk, Wv, Wo = (f32(inp[f"L0_attn_W{ch}"]) for ch in "qkvo")
    sw = lambda wt: np.ascontiguousarray(
        wt.reshape(8, 128, 1024).transpose(1, 0, 2).reshape(128, 8192))
    w["wq_s"] = sw(f32(Wq.T))
    w["wk_s"] = sw(f32(Wk.T))
    w["wv_s"] = sw(f32(Wv.T))
    w["wo_s"] = sw(f32(Wo.T))
    w["wq_r"] = Wq
    w["bq"] = f32(inp["L0_attn_bq"]).reshape(8, 128).T.copy()
    w["bk"] = f32(inp["L0_attn_bk"]).reshape(8, 128).T.copy()
    w["bv"] = f32(inp["L0_attn_bv"]).reshape(1, D)
    w["bo"] = f32(inp["L0_attn_bo"]).reshape(1, D)
    w["demT"] = f32(np.asarray(inp["L0_attn_dist_emb"]).T)

    w["wup_s"] = np.ascontiguousarray(
        f32(np.asarray(inp["L0_ffn_W_up"]).T).reshape(8, 128, 64)
        .transpose(1, 0, 2).reshape(128, 512))
    w["wdn_s"] = np.ascontiguousarray(
        f32(np.asarray(inp["L0_ffn_W_down"]).T).reshape(32, 128, 64)
        .transpose(1, 0, 2).reshape(128, 2048))
    A_up = f32(inp["L0_ffn_A_up"])
    B_up = f32(inp["L0_ffn_B_up"])
    A_dn = f32(inp["L0_ffn_A_down"])
    B_dn = f32(inp["L0_ffn_B_down"])
    w["bupg"] = np.ascontiguousarray(
        np.tile(B_up.transpose(2, 0, 1), (4, 1, 1))
        .transpose(1, 0, 2).reshape(E * 128, 64))
    z = np.zeros((E, 4, 32, 4, 64), np.float32)
    at = A_up.transpose(0, 2, 1)
    for pl in range(4):
        z[:, pl, :, pl, :] = at
    w["aupbd"] = np.ascontiguousarray(z.reshape(E * 128, 256))
    z = np.zeros((E, 2, 64, 2, 32), np.float32)
    bt = B_dn.transpose(0, 2, 1)
    for m in range(2):
        z[:, m, :, m, :] = bt
    w["bdng"] = np.ascontiguousarray(z.reshape(E * 128, 64))
    z = np.zeros((E, 2, 4, 32, 4, 32), np.float32)
    for h in range(2):
        ah = A_dn[:, :, 32 * h:32 * h + 32].transpose(0, 2, 1)
        for pl in range(4):
            z[:, h, pl, :, pl, :] = ah
    w["adnbd"] = np.ascontiguousarray(
        z.reshape(E, 2, 128, 128).transpose(0, 2, 1, 3).reshape(E * 128, 256))
    w["scup"] = np.tile(f32(inp["L0_ffn_scale_up"]).reshape(1, 1), (128, 1))
    w["scdn"] = np.tile(f32(inp["L0_ffn_scale_down"]).reshape(1, 1), (128, 1))
    w["biasup"] = f32(inp["L0_ffn_bias_up"]).reshape(1, DH)
    w["biasdn"] = f32(inp["L0_ffn_bias_down"]).reshape(1, D)
    w["ident"] = np.eye(128, dtype=np.float32)
    w["onesr"] = np.ones((1, 128), np.float32)
    w["halfr"] = np.full((1, 128), 0.5, np.float32)
    w["iotaE"] = np.tile(np.arange(E, dtype=np.float32), (128, 1))
    w["sltri"] = np.triu(np.ones((E, E), np.float32), 1)
    w["iotaB"] = np.tile(np.arange(NBLK, dtype=np.float32), (E, 1))
    w["eiota"] = np.arange(E, dtype=np.float32).reshape(E, 1)
    w["onesf"] = np.ones((1, 128), np.float32)
    w["iota128"] = np.arange(128, dtype=np.float32).reshape(128, 1)
    return w


_W_SHAPES = {
    "wq_s": [128, 8192], "wk_s": [128, 8192], "wv_s": [128, 8192],
    "wo_s": [128, 8192],
    "wq_r": [D, D], "bq": [128, 8], "bk": [128, 8], "bv": [1, D],
    "bo": [1, D], "demT": [64, 73], "wup_s": [128, 512],
    "wdn_s": [128, 2048],
    "bupg": [E * 128, 64], "aupbd": [E * 128, 256], "bdng": [E * 128, 64],
    "adnbd": [E * 128, 256], "scup": [128, 1], "scdn": [128, 1],
    "biasup": [1, DH], "biasdn": [1, D], "ident": [128, 128],
    "onesr": [1, 128], "halfr": [1, 128], "iotaE": [128, E],
    "sltri": [E, E], "iotaB": [E, NBLK], "eiota": [E, 1],
    "onesf": [1, 128], "iota128": [128, 1],
}


class Ctx:
    pass


class Rec:

    def __init__(self, eng):
        object.__setattr__(self, "_e", eng)
        object.__setattr__(self, "last", None)

    def __getattr__(self, name):
        a = getattr(object.__getattribute__(self, "_e"), name)
        if callable(a) and name not in ("wait_ge", "nop"):
            def call(*args, **kw):
                r = a(*args, **kw)
                if r is not None and hasattr(r, "then_inc"):
                    object.__setattr__(self, "last", r)
                return r
            return call
        return a


class Bld:
    def __init__(self, nc, c, blk):
        self.nc, self.c, self.blk = nc, c, blk
        self.barv = 0

    def on(self, eng, fn):
        getattr(self.blk, eng)(lambda e: fn(Rec(e)))

    def barrier(self):
        c = self.c
        self.barv += 5
        for eng in ("sync", "tensor", "vector", "scalar", "gpsimd"):
            def body(e, _v=self.barv, _eng=eng):
                if _eng == "vector":
                    e.memset(c.dmy[:, :], 0.0).then_inc(c.sem["bar"], 1)
                elif _eng == "scalar":
                    e.copy(c.dmy2[:, :], c.dmy2[:, :]).then_inc(c.sem["bar"], 1)
                else:
                    e.nop().then_inc(c.sem["bar"], 1)
                e.wait_ge(c.sem["bar"], _v)
            self.on(eng, body)


def _inc(c, instr, name, k):
    instr.then_inc(c.sem[name], k)
    c.cnt[name] += k


def vsync(c, v, instr):
    _inc(c, instr, "vv", 1)
    v.wait_ge(c.sem["vv"], c.cnt["vv"])


def build_program():
    nc = bass.Bass("TRN2", target_bir_lowering=False)
    c = Ctx()
    c.nc = nc
    c.i_x = nc.dram_tensor("x", [L, D], F32, kind="ExternalInput").ap()
    c.i = {k: nc.dram_tensor(k, shp, F32, kind="ExternalInput").ap()
           for k, shp in _W_SHAPES.items()}
    c.o_out = nc.dram_tensor("out", [L, D], F32, kind="ExternalOutput").ap()

    c.d_xg_up = nc.dram_tensor("d_xg_up", [SLOTS, D], BF16,
                               kind="ExternalInput")
    c.d_yp_up = nc.dram_tensor("d_yp_up", [SLOTS, DH], BF16, kind="Internal")
    c.d_xg_dn = nc.dram_tensor("d_xg_dn", [SLOTS, DH], BF16,
                               kind="ExternalInput")
    c.d_yp_dn = nc.dram_tensor("d_yp_dn", [SLOTS, D], BF16, kind="Internal")
    c.d_h = nc.dram_tensor("d_h", [L, DH], BF16, kind="Internal")
    c.d_skew = nc.dram_tensor("d_skew", [2 * 128 * PADL], BF16, kind="Internal")
    c.d_xr = nc.dram_tensor("d_xr", [L, D], F32, kind="Internal")

    es = contextlib.ExitStack()
    c.es = es
    sb = lambda name, shp, dt: es.enter_context(nc.sbuf_tensor(name, shp, dt))

    c.xb = sb("xb", [128, 2 * D], F32)
    c.xnb = sb("xnb", [128, 2 * D], BF16)
    c.wuphl = sb("swuphl", [128, 8 * 128], BF16)
    c.wdnhl = sb("swdnhl", [128, 32 * 128], BF16)
    c.demT = sb("sdemT", [64, 73], BF16)
    pass
    c.ident = sb("sident", [128, 128], BF16)
    c.identF = sb("sidentF", [128, 128], F32)
    c.onesr = sb("sonesr", [1, 128], BF16)
    c.halfr = sb("shalfr", [1, 128], BF16)
    c.iotaE = sb("siotaE", [128, E], F32)
    c.bq = sb("sbq", [128, 8], F32)
    c.bq16 = sb("sbq16", [128, 8], BF16)
    c.bk = sb("sbk", [128, 8], F32)
    c.bvr = sb("sbvr", [1, D], BF16)
    c.bor = sb("sbor", [1, D], BF16)
    c.scupc = sb("sscupc", [128, 1], F32)
    c.scdnc = sb("sscdnc", [128, 1], F32)
    c.bdnbc = sb("sbdnbc", [128, D], BF16)
    c.ddbd = sb("sddbd", [128, 8 * 16], BF16)
    c.ddbq = sb("sddbq", [16, 1], F32)
    c.dd = sb("sdd", [64, 1], F32)

    c.ssum = sb("ssum", [128, NT], F32)
    c.ssq = sb("ssq", [128, NT], F32)
    c.smean = sb("smean", [128, NT], F32)
    c.srstd = sb("srstd", [128, NT], F32)
    c.logL = sb("logL", [128, E], F32)
    c.mxv = sb("mxv", [128, 8], F32)
    c.mxi = sb("mxi", [128, 8], U32)
    c.dstL = sb("dstL", [128, E], F32)
    c.p1 = sb("p1", [128, NT], F32)
    c.p2 = sb("p2", [128, NT], F32)
    c.pd = sb("pd", [128, NT], F32)
    c.dst1 = [sb(f"dst1_{t}", [128, 1], U32) for t in range(NT)]
    c.dst2 = [sb(f"dst2_{t}", [128, 1], U32) for t in range(NT)]
    c.idxf = sb("idxf", [128, 2], F32)

    c.hTc = sb("hTc", [128, 32 * 128], BF16)
    c.sqdump = c.hTc[:, 0:1024]
    c.bupr16 = c.hTc[0:1, 0:DH]
    c.bdnr16 = c.hTc[0:1, 0:D]
    c.m1L = c.hTc[:, 2048:2048 + NT * E]
    c.m2L = c.hTc[:, 3072:3072 + NT * E]
    c.wksl = c.hTc[:, 0:2048]
    c.wvsl = None
    c.augT = c.hTc[0:16, 2048:4096]

    c.ar_a = sb("ar_a", [128, 8192], BF16)
    c.xgtg = c.ar_a[:, :]
    c.wstr = c.ar_a[:, 2048:4096]
    c.xg2 = c.ar_a[:, :]
    c.bupbc = c.ar_a[:, 0:DH]
    c.ar_b = sb("ar_b", [128, 16384], BF16)
    c.xnT = c.ar_b[:, :]
    _bf = c.ar_b[:, :].bitcast(F32)
    c.t1f = _bf[:, 0:DH]
    c.t2f = _bf[:, DH:2 * DH]
    c.t3f = _bf[:, 0:D]

    c.ar_c = sb("ar_c", [128, 16384], BF16)
    c.outT = c.ar_c[:, :]
    c.demTf = c.ar_c[:, :].bitcast(F32)[0:64, 0:73]
    c.gb1 = c.ar_c[:, 0:DH]
    c.gb2 = c.ar_c[:, DH:2 * DH]
    c.hbuf = c.ar_c[:, 2 * DH:4 * DH]
    c.waug = c.ar_c[0:16, 0:1024]
    c.ar_d = sb("ar_d", [128, NT * 1040], BF16)
    _fd = c.ar_d[:, :].bitcast(F32)
    c.sltri = _fd[0:64, 6144:6208]
    c.iotaB96 = _fd[0:64, 6208:6208 + NBLK]
    c.eiotac = _fd[0:64, 6306:6307]
    c.onesrF = _fd[0:1, 6307:6435]
    c.iota128c = _fd[:, 6435:6436]
    c.rsc = _fd[0:64, 6440:6456]
    c.indb = _fd[0:64, 6456:6456 + NBLK]
    c.indb2 = _fd[0:64, 6552:6552 + NBLK]
    c.ebr = _fd[0:1, 6648:6648 + NBLK]
    c.widxu = c.ar_d[:, :].bitcast(U32)[:, 6744:6744 + NBLK]
    c.rscU = c.ar_d[:, :].bitcast(U32)[0:64, 6440:6456]
    c.vfull = c.ar_d[:, :]
    c.yp = c.ar_d[:, 0:2 * DH]
    c.yp2 = c.ar_d[:, 2 * DH:2 * DH + 2 * D]
    c.y1b = c.ar_d[:, 2 * DH + 2 * D:2 * DH + 3 * D]
    c.y2b = c.ar_d[:, 2 * DH + 3 * D:2 * DH + 4 * D]
    c.ar_e = sb("ar_e", [128, 16384], BF16)
    c.sa = c.ar_e[:, 0:4096]
    c.xlb = c.ar_e[:, 4096:6144]
    c.br = c.ar_e[:, 4096:8192]
    c.sa2 = c.ar_e[:, 8192:12288]
    c.br2 = c.ar_e[:, 12288:16384]
    c.qsl = [c.ar_e[0:65, jj * 2048:(jj + 1) * 2048] for jj in range(4)]
    c.ksl = [c.ar_e[0:65, 8192 + jj * 2048:8192 + (jj + 1) * 2048]
             for jj in range(4)]
    c.wosl = c.ar_e[:, 0:8192]
    c.ar_f = sb("ar_f", [128, 10240], BF16)
    _ff = c.ar_f[:, :].bitcast(F32)
    c.logT = _ff[0:64, 0:L]
    c.scanT = _ff[0:64, L:2 * L]
    c.dstT = c.scanT
    c.mT = c.ar_f[0:64, 8192:10240]
    c.zrow = c.ar_f[0:1, 8192:10240]
    c.zbc = c.hTc[64:128, 2048:4096]
    c.eT = c.ar_f[:, 0:2048]
    c.bandl = c.ar_f[:, 2048:2048 + NT * WIN]
    c.bandTB = c.ar_f[:, 5632:7680]
    c.bandTAC = c.ar_f[:, 7680:9728]

    c.aupbd_s = sb("aupbd_s", [128, 2 * 320], BF16)
    c.adnbd_s = sb("adnbd_s", [128, 2 * 320], BF16)

    c.waugT = sb("waugT", [128, 16 * 8], BF16)
    c.tpad = c.ar_a[:, 4096:4096 + 2 * PADL]
    c.avl = c.ar_a[:, 4800:4864]
    c.zrec = sb("zrec", [128, 1], F32)
    c.tcol = sb("tcol", [128, 1], F32)
    c.acol = sb("acol", [128, 1], F32)
    c.dmy = sb("dmy", [128, 1], F32)
    c.dmy2 = sb("dmy2", [128, 1], F32)
    c.PS = es.enter_context(nc.psum_tensor("PS", [128, 4096], F32))
    c.PSb = c.PS.ap().bitcast(BF16)

    sems = ["ld", "pe", "dv", "ac", "gp", "st", "bar", "sk", "ig", "rb", "vv",
            "iz", "s0", "s1", "h0", "h1", "g0", "g1", "r0", "r1"]
    c.sem = {s: es.enter_context(nc.semaphore(s)) for s in sems}
    c.cnt = {s: 0 for s in sems}
    c.up_st_war = {0: 0, 1: 0}
    c.up_rb_war = {0: 0, 1: 0}
    c.dn_rb_war = {0: 0, 1: 0}
    c.up_w_war = {0: 0, 1: 0}
    c.dn_w_war = {0: 0, 1: 0}
    c.dn_st_war = {0: 0, 1: 0}
    c.cu_st_war = {0: 0, 1: 0}
    c.cu_pe_war = {0: 0, 1: 0}
    return nc, c


def emit_p0(b):
    nc, c = b.nc, b.c

    def sync(s):
        _inc(c, s.dma_start(out=c.d_xr.ap(), in_=c.i_x), "rb", 16)
        arbF = c.ar_b[:, :].bitcast(F32)
        _inc(c, s.dma_start(out=arbF[:, 0:512], in_=c.i["wup_s"]), "rb", 16)
        _inc(c, s.dma_start(out=arbF[:, 512:2560], in_=c.i["wdn_s"]),
             "rb", 16)
    b.on("sync", sync)
    c.x_rb = c.cnt["rb"]

    def gp(g):
        i = c.i
        _inc(c, g.memset(c.dmy2[:, :], 0.0), "iz", 1)
        loads = [
            (c.demT[:, :], i["demT"]),
            (c.demTf[:, :], i["demT"]),
            (c.ident[:, :], i["ident"]),
            (c.identF[:, :], i["ident"]),
            (c.onesr[:, :], i["onesr"]),
            (c.halfr[:, :], i["halfr"]),
            (c.iotaE[:, :], i["iotaE"]),
            (c.bq[:, :], i["bq"]),
            (c.bq16[:, :], i["bq"]),
            (c.bk[:, :], i["bk"]),
            (c.bvr[:, :], i["bv"]),
            (c.bor[:, :], i["bo"]),
            (c.scupc[:, :], i["scup"]),
            (c.scdnc[:, :], i["scdn"]),
            (c.bdnr16, i["biasdn"]),
        ]
        for dst, srcap in loads:
            _inc(c, g.dma_start(out=dst, in_=srcap), "ld", 16)
    b.on("gpsimd", gp)

    def sc0(s):
        s.wait_ge(c.sem["iz"], 1)
    b.on("scalar", sc0)
    ldv = c.cnt["ld"]

    def te(t):
        t.wait_ge(c.sem["ld"], ldv)
        for kk in range(2):
            t.matmul(c.PS[:, 512 + kk * 512:1024 + kk * 512], c.halfr[:, :],
                     c.bdnr16[:, kk * 512:(kk + 1) * 512], start=True, stop=True)
        _inc(c, t.last, "pe", 1)
    b.on("tensor", te)
    pev = c.cnt["pe"]

    def ve(v):
        v.wait_ge(c.sem["pe"], pev)
        v.wait_ge(c.sem["rb"], c.x_rb)
        arbF = c.ar_b[:, :].bitcast(F32)
        wupf, wdnf = arbF[:, 0:512], arbF[:, 512:2560]
        vsync(c, v, v.tensor_copy(fancy(c.wuphl[:, :], [[128, 8], [1, 64]]), wupf))
        vsync(c, v, v.tensor_tensor(
            fancy(c.wuphl[:, :], [[128, 8], [1, 64]], 64), wupf,
            fancy(c.wuphl[:, :], [[128, 8], [1, 64]]), op=OP.subtract))
        vsync(c, v, v.tensor_copy(fancy(c.wdnhl[:, :], [[128, 32], [1, 64]]), wdnf))
        vsync(c, v, v.tensor_tensor(
            fancy(c.wdnhl[:, :], [[128, 32], [1, 64]], 64), wdnf,
            fancy(c.wdnhl[:, :], [[128, 32], [1, 64]]), op=OP.subtract))
        vsync(c, v, v.tensor_copy(c.bdnbc[:, :], c.PS[:, 512:1536]))
        vsync(c, v, v.tensor_sub(c.dd[:, :], c.demTf[:, 72:73],
                                 c.demTf[:, 0:1]))
        vsync(c, v, v.memset(c.ddbd[:, :], 0.0))
        last = None
        for h in range(16):
            rb0 = 0 if (h % 2 == 0) else 64
            ch = h // 2
            last = v.tensor_copy(
                c.ddbd[rb0:rb0 + 64, ch * 16 + h:ch * 16 + h + 1], c.dd[:, :])
        _inc(c, last, "dv", 1)
    b.on("vector", ve)
    dvv = c.cnt["dv"]

    def te3(t):
        t.wait_ge(c.sem["dv"], dvv)
        for kk in range(8):
            t.matmul(c.PS[0:16, 1536:1537],
                     c.ddbd[:, kk * 16:(kk + 1) * 16], c.bq16[:, kk:kk + 1],
                     start=(kk == 0), stop=(kk == 7))
        _inc(c, t.last, "pe", 1)
    b.on("tensor", te3)
    pev = c.cnt["pe"]

    def ve3(v):
        v.wait_ge(c.sem["pe"], pev)
        _inc(c, v.tensor_copy(c.ddbq[:, :], c.PS[0:16, 1536:1537]), "dv", 1)
    b.on("vector", ve3)


def emit_ln(b):
    nc, c = b.nc, b.c

    st_pre0 = c.cnt["st"]
    rb_pre0 = c.cnt["rb"]
    for t in range(NT):
        par = t % 2
        dv_pre, ac_pre = c.cnt["dv"], c.cnt["ac"]

        def sl(s, _t=t, _p=par):
            if _t == 0:
                s.wait_ge(c.sem["rb"], rb_pre0)
                s.wait_ge(c.sem["st"], st_pre0)
            s.wait_ge(c.sem["dv"], dv_pre)
            s.wait_ge(c.sem["ac"], ac_pre)
            _inc(c, s.dma_start(out=c.xb[:, _p * D:(_p + 1) * D],
                                in_=c.d_xr.ap()[_t * 128:(_t + 1) * 128, :]),
                 "rb", 16)
        b.on("sync", sl)
        rbv = c.cnt["rb"]

        def ve(v, _t=t, _p=par):
            v.wait_ge(c.sem["rb"], rbv)
            _inc(c, v.tensor_reduce(c.ssum[:, _t:_t + 1],
                                    c.xb[:, _p * D:(_p + 1) * D],
                                    axis=AX.X, op=OP.add), "dv", 1)
        b.on("vector", ve)

        def sc(s, _t=t, _p=par):
            s.wait_ge(c.sem["rb"], rbv)
            _inc(c, s.activation(c.sqdump[:, :], c.xb[:, _p * D:(_p + 1) * D],
                                 AF.Square, accum_out=c.ssq[:, _t:_t + 1]),
                 "ac", 1)
        b.on("scalar", sc)

    dvv, acv = c.cnt["dv"], c.cnt["ac"]

    def ve2(v):
        v.wait_ge(c.sem["dv"], dvv)
        v.wait_ge(c.sem["ac"], acv)
        vsync(c, v, v.tensor_scalar(c.smean[:, :], c.ssum[:, :], 1.0 / D, None,
                                    op0=OP.mult))
        vsync(c, v, v.tensor_scalar(c.srstd[:, :], c.ssq[:, :], 1.0 / D, None,
                                    op0=OP.mult))
        vsync(c, v, v.tensor_tensor(c.ssum[:, :], c.smean[:, :], c.smean[:, :],
                                    op=OP.mult))
        vsync(c, v, v.tensor_tensor(c.srstd[:, :], c.srstd[:, :],
                                    c.ssum[:, :], op=OP.subtract))
        vsync(c, v, v.tensor_scalar(c.srstd[:, :], c.srstd[:, :], 1e-5, None,
                                    op0=OP.add))
        _inc(c, v.nop(), "dv", 1)
    b.on("vector", ve2)
    dvv = c.cnt["dv"]

    def sc2(s):
        s.wait_ge(c.sem["dv"], dvv)
        _inc(c, s.activation(c.srstd[:, :], c.srstd[:, :], AF.Sqrt), "ac", 1)
    b.on("scalar", sc2)
    acv = c.cnt["ac"]

    def ve3(v):
        v.wait_ge(c.sem["ac"], acv)
        _inc(c, v.reciprocal(c.srstd[:, :], c.srstd[:, :]), "dv", 1)
    b.on("vector", ve3)


def emit_xn_tile(b, t, war_sems=(), want_lo=False):
    nc, c = b.nc, b.c
    par = t % 2
    waits = list(war_sems)

    def sl(s, _t=t, _p=par, _w=tuple(waits)):
        for nm, val in _w:
            s.wait_ge(c.sem[nm], val)
        _inc(c, s.dma_start(out=c.xb[:, _p * D:(_p + 1) * D],
                            in_=c.d_xr.ap()[_t * 128:(_t + 1) * 128, :]),
             "rb", 16)
    b.on("sync", sl)
    rbv = c.cnt["rb"]

    def vn(v, _t=t, _p=par, _lo=want_lo):
        v.wait_ge(c.sem["rb"], rbv)
        if not _lo:
            _inc(c, v.scalar_tensor_tensor(
                c.xnb[:, _p * D:(_p + 1) * D], c.xb[:, _p * D:(_p + 1) * D],
                c.smean[:, _t:_t + 1],
                c.srstd[:, _t:_t + 1].to_broadcast((128, D)),
                op0=OP.subtract, op1=OP.mult), "dv", 1)
        else:
            xnf = c.ar_b[:, :].bitcast(F32)[:, 0:D]
            vsync(c, v, v.scalar_tensor_tensor(
                xnf, c.xb[:, _p * D:(_p + 1) * D],
                c.smean[:, _t:_t + 1],
                c.srstd[:, _t:_t + 1].to_broadcast((128, D)),
                op0=OP.subtract, op1=OP.mult))
            vsync(c, v, v.tensor_copy(c.xnb[:, _p * D:(_p + 1) * D], xnf))
            _inc(c, v.tensor_tensor(
                c.xlb[:, _p * D:(_p + 1) * D], xnf,
                c.xnb[:, _p * D:(_p + 1) * D], op=OP.subtract), "dv", 1)
    b.on("vector", vn)
    return c.cnt["dv"]


def emit_router(b, width, wslab, logits_ready):
    nc, c = b.nc, b.c
    nch = width // 128

    if not logits_ready:
        for t in range(NT):
            xn_dv = emit_xn_tile(b, t, war_sems=(("pe", c.cnt["pe"]),),
                                 want_lo=True)
            par = t % 2

            def teT(tt, _t=t, _p=par, _x=xn_dv):
                tt.wait_ge(c.sem["dv"], _x)
                if _t > 0:
                    tt.wait_ge(c.sem["dv"], c.cnt["dv"])
                for ch in range(8):
                    tt.transpose(c.PSb[:, 4096 + ch * 128:4096 + ch * 128 + 128],
                                 c.xnb[:, _p * D + ch * 128:_p * D + ch * 128 + 128],
                                 c.ident[:, :])
                for ch in range(8):
                    tt.transpose(c.PSb[:, 5120 + ch * 128:5120 + ch * 128 + 128],
                                 c.xlb[:, _p * D + ch * 128:_p * D + ch * 128 + 128],
                                 c.ident[:, :])
                _inc(c, tt.last, "pe", 1)
            b.on("tensor", teT)
            pev = c.cnt["pe"]

            def veT(v):
                v.wait_ge(c.sem["pe"], pev)
                v.tensor_copy(c.hTc[:, 0:1024], c.PSb[:, 4096:5120])
                _inc(c, v.tensor_copy(c.hTc[:, 1024:2048],
                                      c.PSb[:, 5120:6144]), "dv", 1)
            b.on("vector", veT)
            dvv = c.cnt["dv"]

            def teM(tt, _t=t):
                tt.wait_ge(c.sem["dv"], dvv)
                out128 = c.PS[0:128, _t * 128:(_t + 1) * 128]
                out64 = c.PS[0:64, _t * 128:(_t + 1) * 128]
                for ch in range(8):
                    if ch < 7:
                        tt.matmul(out128,
                                  c.wuphl[:, ch * 128:(ch + 1) * 128],
                                  c.hTc[:, ch * 128:(ch + 1) * 128],
                                  start=(ch == 0), stop=False)
                        tt.matmul(out64,
                                  c.wuphl[:, ch * 128:ch * 128 + 64],
                                  c.hTc[:, 1024 + ch * 128:1024 + (ch + 1) * 128],
                                  start=False, stop=False)
                    else:
                        tt.matmul(out64,
                                  c.wuphl[:, ch * 128:ch * 128 + 64],
                                  c.hTc[:, 1024 + ch * 128:1024 + (ch + 1) * 128],
                                  start=False, stop=False)
                        tt.matmul(out128,
                                  c.wuphl[:, ch * 128:(ch + 1) * 128],
                                  c.hTc[:, ch * 128:(ch + 1) * 128],
                                  start=False, stop=True)
                _inc(c, tt.last, "pe", 1)
            b.on("tensor", teM)
        pev = c.cnt["pe"]

        def vel(v):
            v.wait_ge(c.sem["pe"], pev)
            vsync(c, v, v.tensor_copy(c.logT[:, :], c.PS[64:128, 0:L]))
            _inc(c, v.tensor_tensor(c.logT[:, :], c.logT[:, :],
                                    c.PS[0:64, 0:L], op=OP.add), "dv", 1)
        b.on("vector", vel)

    dvv = c.cnt["dv"]

    vv_mark = [0]

    def teL(tt):
        tt.wait_ge(c.sem["dv"], dvv)
        for t in range(8):
            tt.transpose(c.PS[:, 2048 + t * 64:2048 + t * 64 + 64],
                         c.logT[:, t * 128:(t + 1) * 128],
                         c.identF[0:64, 0:64])
        _inc(c, tt.last, "pe", 1)
    b.on("tensor", teL)
    pe_base = c.cnt["pe"] - 1

    def vtop(v):
        for half in range(2):
            v.wait_ge(c.sem["pe"], pe_base + 1 + half)
            for t8 in range(8):
                t = half * 8 + t8
                vsync(c, v, v.tensor_copy(
                    c.logL[:, :], c.PS[:, 2048 + t8 * 64:2048 + t8 * 64 + 64]))
                vsync(c, v, v.max(out=c.mxv[:, :], in_=c.logL[:, :]))
                vsync(c, v, v.max_index(out=c.mxi[:, :], in_max=c.mxv[:, :],
                                        in_values=c.logL[:, :]))
                vsync(c, v, v.tensor_copy(c.idxf[:, 0:2], c.mxi[:, 0:2]))
                vsync(c, v, v.tensor_scalar(
                    c.m1L[:, t * E:(t + 1) * E], c.iotaE[:, :],
                    c.idxf[:, 0:1], None, op0=OP.is_equal))
                vsync(c, v, v.tensor_scalar(
                    c.m2L[:, t * E:(t + 1) * E], c.iotaE[:, :],
                    c.idxf[:, 1:2], None, op0=OP.is_equal))
                vsync(c, v, v.tensor_sub(c.pd[:, t:t + 1], c.mxv[:, 1:2],
                                         c.mxv[:, 0:1]))
            if half == 0:
                _inc(c, v.nop(), "dv", 1)
        _inc(c, v.nop(), "dv", 1)
    b.on("vector", vtop)
    dv_half = c.cnt["dv"] - 1

    def teL2(tt):
        tt.wait_ge(c.sem["dv"], dv_half)
        for t in range(8, NT):
            tt.transpose(c.PS[:, 2048 + (t % 8) * 64:2048 + (t % 8) * 64 + 64],
                         c.logT[:, t * 128:(t + 1) * 128],
                         c.identF[0:64, 0:64])
        _inc(c, tt.last, "pe", 1)
    b.on("tensor", teL2)
    dvv = c.cnt["dv"]

    def sca(s):
        s.wait_ge(c.sem["dv"], dvv)
        _inc(c, s.activation(c.pd[:, :], c.pd[:, :], AF.Exp), "ac", 1)
    b.on("scalar", sca)
    acv = c.cnt["ac"]

    def vp(v):
        v.wait_ge(c.sem["ac"], acv)
        vsync(c, v, v.tensor_scalar(c.p1[:, :], c.pd[:, :], 1.0, None,
                                    op0=OP.add))
        vsync(c, v, v.reciprocal(c.p1[:, :], c.p1[:, :]))
        vsync(c, v, v.tensor_scalar(c.p2[:, :], c.p1[:, :], -1.0, 1.0,
                                    op0=OP.mult, op1=OP.add))
        _inc(c, v.nop(), "dv", 1)
    b.on("vector", vp)
    dvv = c.cnt["dv"]

    def teQ(tt):
        tt.wait_ge(c.sem["dv"], dvv)
        for t in range(NT):
            tt.transpose(c.PSb[0:64, 4096 + t * 128:4096 + t * 128 + 128],
                         c.m1L[:, t * E:(t + 1) * E], c.ident[:, :])
            tt.transpose(c.PSb[0:64, 6144 + t * 128:6144 + t * 128 + 128],
                         c.m2L[:, t * E:(t + 1) * E], c.ident[:, :])
        _inc(c, tt.last, "pe", 1)
    b.on("tensor", teQ)
    pev = c.cnt["pe"]

    R = c.rsc

    def veQ(v):
        v.wait_ge(c.sem["pe"], pev)
        v.wait_ge(c.sem["ld"], c.ffn_ld)
        vsync(c, v, v.tensor_copy(c.mT[0:64, :], c.PSb[0:64, 4096:6144]))
        vsync(c, v, v.tensor_tensor(c.mT[0:64, :], c.mT[0:64, :],
                                    c.PSb[0:64, 6144:8192], op=OP.add))
        vsync(c, v, v.tensor_tensor_scan(c.scanT[:, :], c.mT[0:64, :],
                                         c.mT[0:64, :], 0.0, op0=OP.add,
                                         op1=OP.bypass))
        U = c.rscU
        vsync(c, v, v.tensor_copy(R[:, 0:1], c.scanT[:, L - 1:L]))
        vsync(c, v, v.tensor_scalar(R[:, 1:2], R[:, 0:1], 127.0, None,
                                    op0=OP.add))
        vsync(c, v, v.tensor_copy(U[:, 2:3], R[:, 1:2]))
        vsync(c, v, v.tensor_scalar(U[:, 2:3], U[:, 2:3], 7, None,
                                    op0=OP.logical_shift_right))
        vsync(c, v, v.tensor_scalar(U[:, 2:3], U[:, 2:3], 7, None,
                                    op0=OP.logical_shift_left))
        vsync(c, v, v.tensor_copy(R[:, 3:4], U[:, 2:3]))
        _inc(c, v.nop(), "dv", 1)
    b.on("vector", veQ)
    dvv = c.cnt["dv"]

    def teP(tt):
        tt.wait_ge(c.sem["dv"], dvv)
        tt.matmul(c.PS[0:64, 2048:2049], c.sltri, R[:, 3:4],
                  start=True, stop=True)
        _inc(c, tt.last, "pe", 1)
    b.on("tensor", teP)
    pev = c.cnt["pe"]

    def veP(v):
        v.wait_ge(c.sem["pe"], pev)
        vsync(c, v, v.tensor_copy(R[:, 4:5], c.PS[0:64, 2048:2049]))
        vsync(c, v, v.tensor_scalar(c.scanT[:, :], c.scanT[:, :], -1.0, None,
                                    op0=OP.add))
        vsync(c, v, v.tensor_scalar(c.dstT[:, :], c.scanT[:, :],
                                    R[:, 4:5], None, op0=OP.add))
        vsync(c, v, v.tensor_scalar(R[:, 5:6], R[:, 4:5], 1.0 / 128, None,
                                    op0=OP.mult))
        vsync(c, v, v.tensor_scalar(R[:, 6:7], R[:, 3:4], 1.0 / 128, None,
                                    op0=OP.mult))
        vsync(c, v, v.tensor_tensor(R[:, 6:7], R[:, 5:6], R[:, 6:7],
                                    op=OP.add))
        vsync(c, v, v.tensor_scalar(c.indb[:, :], c.iotaB96, R[:, 5:6],
                                    None, op0=OP.is_ge))
        vsync(c, v, v.tensor_scalar(c.indb2[:, :], c.iotaB96, R[:, 6:7],
                                    None, op0=OP.is_lt))
        vsync(c, v, v.tensor_tensor(c.indb[:, :], c.indb[:, :],
                                    c.indb2[:, :], op=OP.mult))
        _inc(c, v.nop(), "dv", 1)
    b.on("vector", veP)
    dvv = c.cnt["dv"]

    def teW(tt):
        tt.wait_ge(c.sem["dv"], dvv)
        tt.matmul(c.PS[0:1, 2064:2064 + NBLK], c.eiotac, c.indb[:, :],
                  start=True, stop=True)
        _inc(c, tt.last, "pe", 1)
    b.on("tensor", teW)
    pev = c.cnt["pe"]

    def veW(v):
        v.wait_ge(c.sem["pe"], pev)
        _inc(c, v.tensor_scalar(c.ebr[:, :], c.PS[0:1, 2064:2064 + NBLK],
                                128.0, None, op0=OP.mult), "dv", 1)
    b.on("vector", veW)
    dvv = c.cnt["dv"]

    def teW2(tt):
        tt.wait_ge(c.sem["dv"], dvv)
        tt.matmul(c.PS[0:128, 2176:2176 + NBLK], c.onesrF, c.ebr[:, :],
                  start=True, stop=True)
        _inc(c, tt.last, "pe", 1)
    b.on("tensor", teW2)
    pev = c.cnt["pe"]

    def veW2(v):
        v.wait_ge(c.sem["pe"], pev)
        _inc(c, v.tensor_scalar(c.widxu[:, :], c.PS[:, 2176:2176 + NBLK],
                                c.iota128c, None, op0=OP.add), "dv", 1)
    b.on("vector", veW2)
    c.widx_dv = c.cnt["dv"]
    dvv = c.cnt["dv"]

    def teD(tt):
        tt.wait_ge(c.sem["dv"], dvv)
        for t in range(8):
            tt.transpose(c.PS[:, 2048 + t * 64:2048 + t * 64 + 64],
                         c.dstT[0:64, t * 128:(t + 1) * 128],
                         c.identF[0:64, 0:64])
        _inc(c, tt.last, "pe", 1)
    b.on("tensor", teD)
    pe_base = c.cnt["pe"] - 1

    def veD(v):
        for half in range(2):
            v.wait_ge(c.sem["pe"], pe_base + 1 + half)
            for t8 in range(8):
                t = half * 8 + t8
                vsync(c, v, v.tensor_copy(
                    c.dstL[:, :], c.PS[:, 2048 + t8 * 64:2048 + t8 * 64 + 64]))
                for mk, dk in ((c.m1L, c.dst1), (c.m2L, c.dst2)):
                    vsync(c, v, v.tensor_tensor(
                        c.logL[:, :], mk[:, t * E:(t + 1) * E], c.dstL[:, :],
                        op=OP.mult))
                    vsync(c, v, v.tensor_reduce(
                        c.idxf[:, 0:1], c.logL[:, :], axis=AX.X, op=OP.add))
                    vsync(c, v, v.tensor_copy(dk[t][:, :], c.idxf[:, 0:1]))
            if half == 0:
                _inc(c, v.nop(), "dv", 1)
        _inc(c, v.nop(), "dv", 1)
    b.on("vector", veD)
    dv_half2 = c.cnt["dv"] - 1

    def teD2(tt):
        tt.wait_ge(c.sem["dv"], dv_half2)
        for t in range(8, NT):
            tt.transpose(c.PS[:, 2048 + (t % 8) * 64:2048 + (t % 8) * 64 + 64],
                         c.dstT[0:64, t * 128:(t + 1) * 128],
                         c.identF[0:64, 0:64])
        _inc(c, tt.last, "pe", 1)
    b.on("tensor", teD2)


def emit_ffn(b):
    nc, c = b.nc, b.c

    def gcl(g):
        g.wait_ge(c.sem["pe"], c.cnt["pe"])
        g.wait_ge(c.sem["dv"], c.cnt["dv"])
        for dst, key in ((c.sltri, "sltri"), (c.iotaB96, "iotaB"),
                         (c.eiotac, "eiota"), (c.onesrF, "onesf"),
                         (c.iota128c, "iota128")):
            _inc(c, g.dma_start(out=dst, in_=c.i[key]), "ld", 16)
    b.on("gpsimd", gcl)
    c.ffn_ld = c.cnt["ld"]
    emit_ln(b)
    b.barrier()
    emit_router(b, D, None, logits_ready=False)
    b.barrier()

    dvv = c.cnt["dv"]
    for t in range(NT):
        xn_dv = emit_xn_tile(b, t, war_sems=(("ig", c.cnt["ig"]), ("dv", dvv)))
        par = t % 2

        def gsc(g, _t=t, _p=par, _x=xn_dv):
            g.wait_ge(c.sem["dv"], _x)
            for dk in (c.dst1, c.dst2):
                g.wait_ge(c.sem["ig"], c.cnt["ig"])
                _inc(c, g.indirect_dma_start(
                    out=c.d_xg_up.ap(),
                    out_offset=IndirectOffsetOnAxis(ap=dk[_t][:, :], axis=0),
                    in_=c.xnb[:, _p * D:(_p + 1) * D], in_offset=None),
                    "ig", 16)
        b.on("gpsimd", gsc)
    igv_disp = c.cnt["ig"]

    for grp in range(NBLK // 4):
        half = grp % 2
        pe_war = c.up_rb_war[half]

        rsem = "r%d" % half

        def mk_srd(c0, c1):
            def srd(s, _g=grp, _h=half, _pw=pe_war, _rs=rsem, _c0=c0, _c1=c1):
                s.wait_ge(c.sem["ig"], igv_disp)
                s.wait_ge(c.sem["pe"], _pw)
                for ch in range(_c0, _c1):
                    _inc(c, s.dma_start(
                        out=c.ar_a[:, _h * 4096 + ch * 512:
                                   _h * 4096 + ch * 512 + 512],
                        in_=c.d_xg_up.ap()[_g * 512:(_g + 1) * 512,
                                           ch * 128:(ch + 1) * 128],
                        transpose=True), _rs, 16)
            return srd
        b.on("sync", mk_srd(0, 4))
        b.on("scalar", mk_srd(4, 8))
        rbv = (rsem, c.cnt[rsem])
        for bi in range(4):
            emit_up_block(b, grp * 4 + bi, half, bi, rbv)
        c.up_rb_war[half] = c.cnt["pe"]
    b.barrier()

    emit_combine_up(b)
    b.barrier()
    emit_router(b, DH, None, logits_ready=True)
    b.barrier()
    emit_down_dispatch(b)
    igv2 = c.cnt["ig"]
    b.barrier()

    for grp in range(NBLK):
        half = grp % 2
        pe_war = c.dn_rb_war[half]

        rsem = "r%d" % half

        def mk_srd2(c0, c1):
            def srd2(s, _g=grp, _h=half, _pw=pe_war, _rs=rsem, _c0=c0, _c1=c1):
                s.wait_ge(c.sem["ig"], igv2)
                s.wait_ge(c.sem["pe"], _pw)
                for ch in range(_c0, _c1):
                    _inc(c, s.dma_start(
                        out=c.ar_a[:, _h * 4096 + ch * 128:
                                   _h * 4096 + ch * 128 + 128],
                        in_=c.d_xg_dn.ap()[_g * 128:(_g + 1) * 128,
                                           ch * 128:(ch + 1) * 128],
                        transpose=True), _rs, 16)
            return srd2
        b.on("sync", mk_srd2(0, 16))
        b.on("scalar", mk_srd2(16, 32))
        rbv = (rsem, c.cnt[rsem])
        emit_dn_block(b, grp, half, 0, rbv)
        c.dn_rb_war[half] = c.cnt["pe"]
    b.barrier()
    emit_combine_dn(b)


def emit_up_block(b, blkid, buf, bi, rbv):
    nc, c = b.nc, b.c
    e = blkid
    psA = c.PS[:, 0:2048]
    psBb = c.PSb[:, 4096:6144]
    wbuf = blkid % 2

    gsem = "g%d" % (blkid % 2)

    def gw(g, _e=e, _w=wbuf, _pw=c.up_w_war[wbuf], _gs=gsem):
        g.wait_ge(c.sem["pe"], _pw)
        g.wait_ge(c.sem["dv"], c.widx_dv)
        _inc(c, g.indirect_dma_start(
            out=c.aupbd_s[:, _w * 320:_w * 320 + 256], out_offset=None,
            in_=c.i["aupbd"],
            in_offset=IndirectOffsetOnAxis(ap=c.widxu[:, _e:_e + 1], axis=0)),
            _gs, 16)
        _inc(c, g.indirect_dma_start(
            out=c.aupbd_s[:, _w * 320 + 256:_w * 320 + 320], out_offset=None,
            in_=c.i["bupg"],
            in_offset=IndirectOffsetOnAxis(ap=c.widxu[:, _e:_e + 1], axis=0)),
            _gs, 16)
    b.on("gpsimd", gw)
    gpv = c.cnt[gsem]

    dv_bef, ac_bef = c.cnt["dv"], c.cnt["ac"]

    def teA(t, _buf=buf, _bi=bi, _e=e, _w=wbuf, _gs=gsem):
        t.wait_ge(c.sem[rbv[0]], rbv[1])
        t.wait_ge(c.sem[_gs], gpv)
        t.wait_ge(c.sem["dv"], dv_bef)
        t.wait_ge(c.sem["ac"], ac_bef)
        for i in range(32):
            u, a = i & 3, i >> 2
            lhsT = c.ar_a[32 * u:32 * u + 32,
                          _buf * 4096 + a * 512 + _bi * 128:
                          _buf * 4096 + a * 512 + _bi * 128 + 128]
            rhs = c.aupbd_s[32 * u:32 * u + 32,
                            _w * 320 + 256:_w * 320 + 320]
            out = psA[:, u * 512 + a * 64:u * 512 + a * 64 + 64]
            t.matmul(out, lhsT, rhs, start=True, stop=True,
                     tile_position=(32 * u, 0))
        _inc(c, t.last, "pe", 1)
    b.on("tensor", teA)
    pev = c.cnt["pe"]
    sbuf_i = blkid % 2

    def veA(v, _s=sbuf_i):
        v.wait_ge(c.sem["pe"], pev)
        src = fancy(psA, [[1, 64], [64, 8], [512, 4]])
        dst = fancy(c.sa[:, _s * 2048:(_s + 1) * 2048],
                    [[32, 64], [4, 8], [1, 4]])
        _inc(c, v.tensor_copy(dst, src), "dv", 1)
    b.on("vector", veA)
    dvv = c.cnt["dv"]

    def teB(t, _s=sbuf_i):
        t.wait_ge(c.sem["dv"], dvv)
        src = c.sa[:, _s * 2048:(_s + 1) * 2048]
        for q in range(16):
            t.transpose(psBb[:, q * 128:(q + 1) * 128],
                        src[:, q * 128:(q + 1) * 128], c.ident[:, :])
        _inc(c, t.last, "pe", 1)
    b.on("tensor", teB)
    pev = c.cnt["pe"]

    def veB(v, _s=sbuf_i):
        v.wait_ge(c.sem["pe"], pev)
        _inc(c, v.tensor_copy(c.br[:, _s * 2048:(_s + 1) * 2048], psBb),
             "dv", 1)
    b.on("vector", veB)
    dvv = c.cnt["dv"]

    for rnd in range(2):
        def teC(t, _s=sbuf_i, _w=wbuf, _r=rnd, _gs=gsem):
            t.wait_ge(c.sem["dv"], dvv if _r == 0 else c.cnt["dv"])
            t.wait_ge(c.sem[_gs], gpv)
            for q8 in range(8):
                q = _r * 8 + q8
                lhsT = c.br[:, _s * 2048 + q * 128:_s * 2048 + q * 128 + 128]
                rhs = c.aupbd_s[:, _w * 320:_w * 320 + 256]
                t.matmul(psA[:, q8 * 256:(q8 + 1) * 256], lhsT, rhs,
                         start=True, stop=True)
            _inc(c, t.last, "pe", 1)
        b.on("tensor", teC)
        pev2 = c.cnt["pe"]

        st_war = c.up_st_war[blkid % 2]

        def veC(v, _r=rnd, _sw=st_war):
            v.wait_ge(c.sem["pe"], pev2)
            if _r == 0:
                v.wait_ge(c.sem["s%d" % (blkid % 2)], _sw)
            src = fancy(psA, [[256, 8], [64, 4], [1, 64]])
            ybase = c.yp[:, (blkid % 2) * DH:(blkid % 2) * DH + DH]
            dst = fancy(ybase, [[4, 8], [1, 4], [64, 64]], 32 * _r)
            _inc(c, v.tensor_copy(dst, src), "dv", 1)
        b.on("vector", veC)

    dvv2 = c.cnt["dv"]

    def veS(v):
        v.wait_ge(c.sem["dv"], dvv2)
        y = c.yp[:, (blkid % 2) * DH:(blkid % 2) * DH + DH]
        _inc(c, v.tensor_scalar(y, y, c.scupc[:, :], None, op0=OP.mult),
             "dv", 1)
    b.on("vector", veS)
    dvv3 = c.cnt["dv"]

    def sst(s, _b=blkid):
        s.wait_ge(c.sem["dv"], dvv3)
        _inc(c, s.dma_start(out=c.d_yp_up.ap()[_b * 128:(_b + 1) * 128, :],
                            in_=c.yp[:, (_b % 2) * DH:(_b % 2) * DH + DH]),
             "s%d" % (_b % 2), 16)
    b.on("scalar" if blkid % 2 == 0 else "sync", sst)
    c.up_st_war[blkid % 2] = c.cnt["s%d" % (blkid % 2)]
    c.up_w_war[blkid % 2] = c.cnt["pe"]


def emit_combine_up(b):
    nc, c = b.nc, b.c
    s0_all, s1_all = c.cnt["s0"], c.cnt["s1"]

    rb_pre = c.cnt["rb"]

    def gb_(g):
        g.wait_ge(c.sem["pe"], c.cnt["pe"])
        _inc(c, g.dma_start(out=c.bupr16, in_=c.i["biasup"]), "gp", 16)
    b.on("gpsimd", gb_)
    gpv = c.cnt["gp"]

    def tb_(t):
        t.wait_ge(c.sem["gp"], gpv)
        for kk in range(8):
            t.matmul(c.PS[:, kk * 512:(kk + 1) * 512], c.onesr[:, :],
                     c.bupr16[:, kk * 512:(kk + 1) * 512],
                     start=True, stop=True)
        _inc(c, t.last, "pe", 1)
    b.on("tensor", tb_)
    pev_b = c.cnt["pe"]

    def vb_(v):
        v.wait_ge(c.sem["pe"], pev_b)
        _inc(c, v.tensor_copy(c.bupbc[:, :], c.PS[:, 0:DH]), "dv", 1)
    b.on("vector", vb_)

    for t in range(NT):
        def gg(g, _t=t):
            g.wait_ge(c.sem["s0"], s0_all)
            g.wait_ge(c.sem["s1"], s1_all)
            g.wait_ge(c.sem["dv"], c.cnt["dv"])
            _inc(c, g.indirect_dma_start(
                out=c.gb1[:, :], out_offset=None, in_=c.d_yp_up.ap(),
                in_offset=IndirectOffsetOnAxis(ap=c.dst1[_t][:, :], axis=0)),
                "ig", 16)
            _inc(c, g.indirect_dma_start(
                out=c.gb2[:, :], out_offset=None, in_=c.d_yp_up.ap(),
                in_offset=IndirectOffsetOnAxis(ap=c.dst2[_t][:, :], axis=0)),
                "ig", 16)
        b.on("gpsimd", gg)
        igv = c.cnt["ig"]

        def vc(v, _t=t):
            v.wait_ge(c.sem["ig"], igv)
            vsync(c, v, v.tensor_scalar(c.t1f[:, :], c.gb1[:, :],
                                        c.p1[:, _t:_t + 1], None, op0=OP.mult))
            vsync(c, v, v.scalar_tensor_tensor(
                c.t2f[:, :], c.gb2[:, :], c.p2[:, _t:_t + 1], c.t1f[:, :],
                op0=OP.mult, op1=OP.add))
            vsync(c, v, v.tensor_tensor(c.t1f[:, :], c.t2f[:, :],
                                        c.bupbc[:, :], op=OP.add))
            _inc(c, v.nop(), "dv", 1)
        b.on("vector", vc)
        dvv = c.cnt["dv"]
        hb = t % 2

        def sg(s, _hb=hb, _sw=c.cu_st_war[hb], _pw=c.cu_pe_war[hb]):
            s.wait_ge(c.sem["dv"], dvv)
            s.wait_ge(c.sem["h%d" % hb], _sw)
            s.wait_ge(c.sem["pe"], _pw)
            _inc(c, s.activation(c.t2f[:, :], c.t1f[:, :], AF.Gelu), "ac", 1)
            s.wait_ge(c.sem["ac"], c.cnt["ac"])
            _inc(c, s.copy(c.hbuf[:, _hb * DH:(_hb + 1) * DH], c.t2f[:, :]),
                 "ac", 1)
        b.on("scalar", sg)
        acv = c.cnt["ac"]

        def shw(s, _t=t, _hb=hb):
            s.wait_ge(c.sem["ac"], acv)
            _inc(c, s.dma_start(out=c.d_h.ap()[_t * 128:(_t + 1) * 128, :],
                                in_=c.hbuf[:, _hb * DH:(_hb + 1) * DH]),
                 "h%d" % _hb, 16)
        b.on("scalar", shw)
        c.cu_st_war[hb] = c.cnt["h%d" % hb]

        pe_prev_teM = c.cnt["pe"]
        for half in range(2):
            def vlo(v, _hb=hb, _half=half, _pp=pe_prev_teM):
                v.wait_ge(c.sem["ac"], acv)
                v.wait_ge(c.sem["pe"], c.cnt["pe"])
                _inc(c, v.tensor_tensor(
                    c.hTc[:, 2048:4096],
                    c.t2f[:, _half * 2048:(_half + 1) * 2048],
                    c.hbuf[:, _hb * DH + _half * 2048:
                           _hb * DH + (_half + 1) * 2048],
                    op=OP.subtract), "dv", 1)
            b.on("vector", vlo)
            dv_lo = c.cnt["dv"]

            def teL(tt, _hb=hb, _half=half):
                tt.wait_ge(c.sem["ac"], acv)
                tt.wait_ge(c.sem["dv"], dv_lo)
                for ch16 in range(16):
                    ch = _half * 16 + ch16
                    tt.transpose(
                        c.PSb[:, 4096 + ch16 * 128:4096 + ch16 * 128 + 128],
                        c.hbuf[:, _hb * DH + ch * 128:_hb * DH + ch * 128 + 128],
                        c.ident[:, :])
                for ch16 in range(16):
                    tt.transpose(
                        c.PSb[:, 6144 + ch16 * 128:6144 + ch16 * 128 + 128],
                        c.hTc[:, 2048 + ch16 * 128:2048 + ch16 * 128 + 128],
                        c.ident[:, :])
                _inc(c, tt.last, "pe", 1)
            b.on("tensor", teL)
            pev = c.cnt["pe"]

            def veL(v):
                v.wait_ge(c.sem["pe"], pev)
                v.tensor_copy(c.hTc[:, 0:2048], c.PSb[:, 4096:6144])
                _inc(c, v.tensor_copy(c.hTc[:, 2048:4096],
                                      c.PSb[:, 6144:8192]), "dv", 1)
            b.on("vector", veL)
            dvv2 = c.cnt["dv"]

            def teM(tt, _t=t, _half=half):
                tt.wait_ge(c.sem["dv"], dvv2)
                out128 = c.PS[0:128, _t * 128:(_t + 1) * 128]
                out64 = c.PS[0:64, _t * 128:(_t + 1) * 128]
                for ch16 in range(16):
                    ch = _half * 16 + ch16
                    last_mm = (_half == 1 and ch16 == 15)
                    if not last_mm:
                        tt.matmul(out128,
                                  c.wdnhl[:, ch * 128:(ch + 1) * 128],
                                  c.hTc[:, ch16 * 128:(ch16 + 1) * 128],
                                  start=(_half == 0 and ch16 == 0), stop=False)
                        tt.matmul(out64,
                                  c.wdnhl[:, ch * 128:ch * 128 + 64],
                                  c.hTc[:, 2048 + ch16 * 128:
                                        2048 + (ch16 + 1) * 128],
                                  start=False, stop=False)
                    else:
                        tt.matmul(out64,
                                  c.wdnhl[:, ch * 128:ch * 128 + 64],
                                  c.hTc[:, 2048 + ch16 * 128:
                                        2048 + (ch16 + 1) * 128],
                                  start=False, stop=False)
                        tt.matmul(out128,
                                  c.wdnhl[:, ch * 128:(ch + 1) * 128],
                                  c.hTc[:, ch16 * 128:(ch16 + 1) * 128],
                                  start=False, stop=True)
                _inc(c, tt.last, "pe", 1)
            b.on("tensor", teM)
        c.cu_pe_war[hb] = c.cnt["pe"]

    pev = c.cnt["pe"]

    def vf(v):
        v.wait_ge(c.sem["pe"], pev)
        vsync(c, v, v.tensor_copy(c.logT[:, :], c.PS[64:128, 0:L]))
        _inc(c, v.tensor_tensor(c.logT[:, :], c.logT[:, :],
                                c.PS[0:64, 0:L], op=OP.add), "dv", 1)
    b.on("vector", vf)


def emit_down_dispatch(b):
    nc, c = b.nc, b.c
    dvv = c.cnt["dv"]
    h0_dh, h1_dh = c.cnt["h0"], c.cnt["h1"]
    for t in range(NT):
        hb = t % 2
        ig_prev = c.cnt["ig"]

        def sr(s, _t=t, _hb=hb):
            s.wait_ge(c.sem["h0"], h0_dh)
            s.wait_ge(c.sem["h1"], h1_dh)
            s.wait_ge(c.sem["dv"], dvv)
            s.wait_ge(c.sem["ig"], ig_prev)
            _inc(c, s.dma_start(out=c.hbuf[:, _hb * DH:(_hb + 1) * DH],
                                in_=c.d_h.ap()[_t * 128:(_t + 1) * 128, :]),
                 "rb", 16)
        b.on("sync", sr)
        rbv = c.cnt["rb"]

        def gs(g, _t=t, _hb=hb):
            g.wait_ge(c.sem["rb"], rbv)
            for dk in (c.dst1, c.dst2):
                g.wait_ge(c.sem["ig"], c.cnt["ig"])
                _inc(c, g.indirect_dma_start(
                    out=c.d_xg_dn.ap(),
                    out_offset=IndirectOffsetOnAxis(ap=dk[_t][:, :], axis=0),
                    in_=c.hbuf[:, _hb * DH:(_hb + 1) * DH], in_offset=None),
                    "ig", 16)
        b.on("gpsimd", gs)


def emit_dn_block(b, blkid, buf, bi, rbv):
    nc, c = b.nc, b.c
    e = blkid
    psA = c.PS[:, 0:2048]
    psBb = c.PSb[:, 4096:6144]
    psC = c.PS[:, 3072:4096]
    wbuf = blkid % 2

    gsem = "g%d" % (blkid % 2)

    def gw(g, _e=e, _w=wbuf, _pw=c.dn_w_war[wbuf], _gs=gsem):
        g.wait_ge(c.sem["pe"], _pw)
        g.wait_ge(c.sem["dv"], c.widx_dv)
        _inc(c, g.indirect_dma_start(
            out=c.adnbd_s[:, _w * 320:_w * 320 + 256], out_offset=None,
            in_=c.i["adnbd"],
            in_offset=IndirectOffsetOnAxis(ap=c.widxu[:, _e:_e + 1], axis=0)),
            _gs, 16)
        _inc(c, g.indirect_dma_start(
            out=c.adnbd_s[:, _w * 320 + 256:_w * 320 + 320], out_offset=None,
            in_=c.i["bdng"],
            in_offset=IndirectOffsetOnAxis(ap=c.widxu[:, _e:_e + 1], axis=0)),
            _gs, 16)
    b.on("gpsimd", gw)
    gpv = c.cnt[gsem]
    dv_bef, ac_bef = c.cnt["dv"], c.cnt["ac"]

    def teA(t, _buf=buf, _bi=bi, _e=e, _w=wbuf, _gs=gsem):
        t.wait_ge(c.sem[rbv[0]], rbv[1])
        t.wait_ge(c.sem[_gs], gpv)
        t.wait_ge(c.sem["dv"], dv_bef)
        t.wait_ge(c.sem["ac"], ac_bef)
        for fp in range(32):
            lhsT = c.ar_a[:, _buf * 4096 + fp * 128:_buf * 4096 + (fp + 1) * 128]
            rhs = c.adnbd_s[:, _w * 320 + 256:_w * 320 + 320]
            out = psA[:, (fp & 3) * 512 + (fp >> 2) * 64:
                      (fp & 3) * 512 + (fp >> 2) * 64 + 64]
            t.matmul(out, lhsT, rhs, start=True, stop=True)
        _inc(c, t.last, "pe", 1)
    b.on("tensor", teA)
    pev = c.cnt["pe"]
    sbuf_i = blkid % 2

    def veA(v, _s=sbuf_i):
        v.wait_ge(c.sem["pe"], pev)
        for h in range(2):
            for d in range(2):
                src = fancy(psA, [[1, 32], [64, 4], [512, 4]],
                            256 * h + 32 * d)
                dst = fancy(c.sa2[:, _s * 2048:(_s + 1) * 2048],
                            [[32, 32], [8, 4], [2, 4]], 1024 * h + d)
                last = (h == 1 and d == 1)
                if last:
                    _inc(c, v.tensor_copy(dst, src), "dv", 1)
                else:
                    v.tensor_copy(dst, src)
    b.on("vector", veA)
    dvv = c.cnt["dv"]

    def teB(t, _s=sbuf_i):
        t.wait_ge(c.sem["dv"], dvv)
        src = c.sa2[:, _s * 2048:(_s + 1) * 2048]
        for u in range(16):
            t.transpose(psBb[:, u * 128:(u + 1) * 128],
                        src[:, u * 128:(u + 1) * 128], c.ident[:, :])
        _inc(c, t.last, "pe", 1)
    b.on("tensor", teB)
    pev = c.cnt["pe"]

    def veB(v, _s=sbuf_i):
        v.wait_ge(c.sem["pe"], pev)
        _inc(c, v.tensor_copy(c.br2[:, _s * 2048:(_s + 1) * 2048], psBb),
             "dv", 1)
    b.on("vector", veB)
    dvv = c.cnt["dv"]

    def teC(t, _s=sbuf_i, _w=wbuf, _gs=gsem):
        t.wait_ge(c.sem["dv"], dvv)
        t.wait_ge(c.sem[_gs], gpv)
        for q in range(8):
            for h in range(2):
                lhsT = c.br2[:, _s * 2048 + (h * 8 + q) * 128:
                             _s * 2048 + (h * 8 + q) * 128 + 128]
                rhs = c.adnbd_s[:, _w * 320 + h * 128:_w * 320 + h * 128 + 128]
                t.matmul(psC[:, q * 128:(q + 1) * 128], lhsT, rhs,
                         start=(h == 0), stop=(h == 1))
        _inc(c, t.last, "pe", 1)
    b.on("tensor", teC)
    pev = c.cnt["pe"]

    st_war = c.dn_st_war[blkid % 2]

    def veC(v, _b=blkid, _sw=st_war):
        v.wait_ge(c.sem["pe"], pev)
        v.wait_ge(c.sem["s%d" % (blkid % 2)], _sw)
        src = fancy(psC, [[128, 8], [32, 4], [1, 32]])
        ybase = c.yp2[:, (_b % 2) * D:(_b % 2) * D + D]
        dst = fancy(ybase, [[4, 8], [1, 4], [32, 32]])
        vsync(c, v, v.tensor_copy(dst, src))
        y = c.yp2[:, (_b % 2) * D:(_b % 2) * D + D]
        _inc(c, v.tensor_scalar(y, y, c.scdnc[:, :], None, op0=OP.mult),
             "dv", 1)
    b.on("vector", veC)
    dvv2 = c.cnt["dv"]

    def sst(s, _b=blkid):
        s.wait_ge(c.sem["dv"], dvv2)
        _inc(c, s.dma_start(out=c.d_yp_dn.ap()[_b * 128:(_b + 1) * 128, :],
                            in_=c.yp2[:, (_b % 2) * D:(_b % 2) * D + D]),
             "s%d" % (_b % 2), 16)
    b.on("scalar" if blkid % 2 == 0 else "sync", sst)
    c.dn_st_war[blkid % 2] = c.cnt["s%d" % (blkid % 2)]
    c.dn_w_war[blkid % 2] = c.cnt["pe"]


def emit_combine_dn(b):
    nc, c = b.nc, b.c
    s0_all, s1_all = c.cnt["s0"], c.cnt["s1"]
    for t in range(NT):
        def gg(g, _t=t):
            g.wait_ge(c.sem["s0"], s0_all)
            g.wait_ge(c.sem["s1"], s1_all)
            g.wait_ge(c.sem["dv"], c.cnt["dv"])
            _inc(c, g.indirect_dma_start(
                out=c.y1b[:, :], out_offset=None, in_=c.d_yp_dn.ap(),
                in_offset=IndirectOffsetOnAxis(ap=c.dst1[_t][:, :], axis=0)),
                "ig", 16)
            _inc(c, g.indirect_dma_start(
                out=c.y2b[:, :], out_offset=None, in_=c.d_yp_dn.ap(),
                in_offset=IndirectOffsetOnAxis(ap=c.dst2[_t][:, :], axis=0)),
                "ig", 16)
        b.on("gpsimd", gg)
        igv = c.cnt["ig"]

        par = t % 2
        st_pre = c.cnt["st"]

        def sx2(s, _t=t, _p=par):
            s.wait_ge(c.sem["st"], st_pre)
            _inc(c, s.dma_start(out=c.xb[:, _p * D:(_p + 1) * D],
                                in_=c.d_xr.ap()[_t * 128:(_t + 1) * 128, :]),
                 "rb", 16)
        b.on("sync", sx2)
        rbv2 = c.cnt["rb"]

        def vc(v, _t=t, _p=par):
            v.wait_ge(c.sem["ig"], igv)
            v.wait_ge(c.sem["rb"], rbv2)
            vsync(c, v, v.tensor_scalar(c.t3f[:, :], c.y1b[:, :],
                                        c.p1[:, _t:_t + 1], None, op0=OP.mult))
            vsync(c, v, v.scalar_tensor_tensor(
                c.t3f[:, :], c.y2b[:, :], c.p2[:, _t:_t + 1], c.t3f[:, :],
                op0=OP.mult, op1=OP.add))
            vsync(c, v, v.scalar_tensor_tensor(
                c.t3f[:, :], c.t3f[:, :], 0.5, c.bdnbc[:, :],
                op0=OP.mult, op1=OP.add))
            vsync(c, v, v.tensor_tensor(
                c.xb[:, _p * D:(_p + 1) * D], c.xb[:, _p * D:(_p + 1) * D],
                c.t3f[:, :], op=OP.add))
            _inc(c, v.nop(), "dv", 1)
        b.on("vector", vc)
        dvx = c.cnt["dv"]

        def sx3(s, _t=t, _p=par):
            s.wait_ge(c.sem["dv"], dvx)
            _inc(c, s.dma_start(out=c.d_xr.ap()[_t * 128:(_t + 1) * 128, :],
                                in_=c.xb[:, _p * D:(_p + 1) * D]), "st", 16)
        b.on("sync", sx3)


def emit_attn(b):
    nc, c = b.nc, b.c

    def vtp(v):
        v.wait_ge(c.sem["pe"], c.cnt["pe"])
        v.wait_ge(c.sem["ig"], c.cnt["ig"])
        v.memset(c.tpad[:, :], 0.0)
    b.on("vector", vtp)
    emit_ln(b)
    b.barrier()

    for t in range(NT):
        xn_dv = emit_xn_tile(b, t, war_sems=(("pe", c.cnt["pe"]),))
        par = t % 2

        def teT(tt, _t=t, _p=par, _x=xn_dv):
            tt.wait_ge(c.sem["dv"], _x)
            if _t > 0:
                tt.wait_ge(c.sem["dv"], c.cnt["dv"])
            for ch in range(8):
                tt.transpose(c.PSb[:, 4096 + ch * 128:4096 + ch * 128 + 128],
                             c.xnb[:, _p * D + ch * 128:_p * D + ch * 128 + 128],
                             c.ident[:, :])
            _inc(c, tt.last, "pe", 1)
        b.on("tensor", teT)
        pev = c.cnt["pe"]

        def veT(v, _t=t):
            v.wait_ge(c.sem["pe"], pev)
            last = None
            for ch in range(8):
                last = v.tensor_copy(
                    c.xnT[:, ch * L + _t * 128:ch * L + _t * 128 + 128],
                    c.PSb[:, 4096 + ch * 128:4096 + ch * 128 + 128])
            _inc(c, last, "dv", 1)
        b.on("vector", veT)
    b.barrier()

    emit_aug(b)
    b.barrier()

    for grp in range(4):
        emit_proj_group(b, grp)
        b.barrier()
        for j in range(4):
            emit_head(b, grp, j)
        b.barrier()

    emit_oproj(b)


def emit_aug(b):
    nc, c = b.nc, b.c
    for ch in range(8):
        pe_prev = c.cnt["pe"]
        gsem = "g%d" % (ch % 2)

        def gw(g, _ch=ch, _pp=pe_prev, _gs=gsem):
            if _ch >= 2:
                g.wait_ge(c.sem["pe"], _pp)
            _inc(c, g.dma_start(
                out=c.wstr[:, (_ch % 2) * 1024:(_ch % 2) * 1024 + 1024],
                in_=c.i["wq_r"][_ch * 128:(_ch + 1) * 128, :]), _gs, 16)
        b.on("gpsimd", gw)
        gpv = c.cnt[gsem]

        def tw(t, _ch=ch, _gs=gsem):
            t.wait_ge(c.sem[_gs], gpv)
            for half in range(2):
                t.matmul(c.PS[0:16, half * 512:(half + 1) * 512],
                         c.ddbd[:, _ch * 16:(_ch + 1) * 16],
                         c.wstr[:, (_ch % 2) * 1024 + half * 512:
                                (_ch % 2) * 1024 + half * 512 + 512],
                         start=(_ch == 0), stop=(_ch == 7),
                         skip_group_check=True)
            _inc(c, t.last, "pe", 1)
        b.on("tensor", tw)
    pev = c.cnt["pe"]

    def ve(v):
        v.wait_ge(c.sem["pe"], pev)
        _inc(c, v.tensor_copy(c.waug[:, :], c.PS[0:16, 0:1024]), "dv", 1)
    b.on("vector", ve)
    dvv = c.cnt["dv"]

    def tT(t):
        t.wait_ge(c.sem["dv"], dvv)
        for ch in range(8):
            t.transpose(c.PSb[:, 4096 + ch * 16:4096 + ch * 16 + 16],
                        c.waug[:, ch * 128:(ch + 1) * 128],
                        c.ident[0:16, 0:16])
        _inc(c, t.last, "pe", 1)
    b.on("tensor", tT)
    pev = c.cnt["pe"]

    def ve2(v):
        v.wait_ge(c.sem["pe"], pev)
        _inc(c, v.tensor_copy(c.waugT[:, :], c.PSb[:, 4096:4224]), "dv", 1)
    b.on("vector", ve2)
    dvv = c.cnt["dv"]

    def tA(t):
        t.wait_ge(c.sem["dv"], dvv)
        for lc in range(4):
            for ch in range(8):
                t.matmul(c.PS[0:16, lc * 512:(lc + 1) * 512],
                         c.waugT[:, ch * 16:(ch + 1) * 16],
                         c.xnT[:, ch * L + lc * 512:ch * L + lc * 512 + 512],
                         start=(ch == 0), stop=(ch == 7),
                         skip_group_check=True)
        _inc(c, t.last, "pe", 1)
    b.on("tensor", tA)
    pev = c.cnt["pe"]

    def vA(v):
        v.wait_ge(c.sem["pe"], pev)
        _inc(c, v.tensor_scalar(c.augT[:, :], c.PS[0:16, 0:L],
                                c.ddbq[:, :], None, op0=OP.add), "dv", 1)
    b.on("vector", vA)
    c.aug_dv = c.cnt["dv"]


def emit_proj_group(b, grp):
    nc, c = b.nc, b.c
    col0 = 256 * grp
    wvsl = c.ar_a[:, 0:2048]

    def gw(g):
        g.wait_ge(c.sem["pe"], c.cnt["pe"])
        _inc(c, g.dma_start(
            out=c.wstr[:, 0:2048],
            in_=fancy(c.i["wq_s"], [[1024, 8], [1, 256]], col0)), "gp", 16)
        _inc(c, g.dma_start(
            out=c.wksl,
            in_=fancy(c.i["wk_s"], [[1024, 8], [1, 256]], col0)), "gp", 16)
        _inc(c, g.dma_start(
            out=wvsl,
            in_=fancy(c.i["wv_s"], [[1024, 8], [1, 256]], col0)), "gp", 16)
    b.on("gpsimd", gw)
    gpv = c.cnt["gp"]

    for wslab, slabs, bias in ((c.wstr[:, 0:2048], c.qsl, c.bq),
                               (c.wksl, c.ksl, c.bk)):
        ac_bef = c.cnt["ac"]

        def tp(t, _w=wslab, _ab=ac_bef):
            t.wait_ge(c.sem["gp"], gpv)
            t.wait_ge(c.sem["ac"], _ab)
            for dqt in range(2):
                for lc in range(4):
                    reg = dqt * 4 + lc
                    for ch in range(8):
                        t.matmul(c.PS[:, reg * 512:(reg + 1) * 512],
                                 _w[:, ch * 256 + dqt * 128:
                                    ch * 256 + dqt * 128 + 128],
                                 c.xnT[:, ch * L + lc * 512:
                                       ch * L + lc * 512 + 512],
                                 start=(ch == 0), stop=(ch == 7))
            _inc(c, t.last, "pe", 1)
        b.on("tensor", tp)
        pev = c.cnt["pe"]

        def vev(v, _sl=slabs, _bias=bias):
            v.wait_ge(c.sem["pe"], pev)
            last = None
            for dqt in range(2):
                for lc in range(4):
                    reg = dqt * 4 + lc
                    for par in range(2):
                        j = 2 * dqt + par
                        last = v.tensor_scalar(
                            _sl[j][0:64, lc * 512:(lc + 1) * 512],
                            c.PS[64 * par:64 * par + 64,
                                 reg * 512:(reg + 1) * 512],
                            _bias[64 * par:64 * par + 64,
                                  2 * grp + dqt:2 * grp + dqt + 1],
                            None, op0=OP.add)
            _inc(c, last, "ac", 1)
        b.on("vector", vev)

    dv_bef = c.cnt["dv"]

    ac_aux = c.cnt["ac"]

    def vaux(v):
        v.wait_ge(c.sem["ac"], ac_aux)
        last = None
        for j in range(4):
            last = v.memset(c.ksl[j][64:65, :], 1.0)
        _inc(c, last, "dv", 1)
    b.on("vector", vaux)

    def saux(s):
        s.wait_ge(c.sem["ac"], ac_aux)
        s.wait_ge(c.sem["dv"], c.aug_dv)
        for j in range(4):
            _inc(c, s.dma_start(
                out=c.qsl[j][64:65, :],
                in_=c.augT[4 * grp + j:4 * grp + j + 1, :]), "sk", 16)
    b.on("sync", saux)

    ac_bef2 = c.cnt["ac"]

    def vround(t, rt):
        reg = rt % 8
        for ch in range(8):
            t.matmul(c.PS[:, reg * 512:reg * 512 + 256],
                     c.xnT[:, ch * L + rt * 128:ch * L + rt * 128 + 128],
                     wvsl[:, ch * 256:(ch + 1) * 256],
                     start=(ch == 0), stop=False)
        t.matmul(c.PS[:, reg * 512:reg * 512 + 256],
                 c.onesr[0:1, 0:128],
                 c.bvr[0:1, col0:col0 + 256],
                 start=False, stop=True)

    def tv1(t):
        t.wait_ge(c.sem["gp"], gpv)
        t.wait_ge(c.sem["ac"], ac_bef2)
        t.wait_ge(c.sem["dv"], c.cnt["dv"])
        for rt in range(8):
            vround(t, rt)
        _inc(c, t.last, "pe", 1)
    b.on("tensor", tv1)
    pev1 = c.cnt["pe"]

    def vevh(v, half):
        for r8 in range(8):
            rt = half * 8 + r8
            src = fancy(c.PS[:, r8 * 512:r8 * 512 + 256],
                        [[64, 4], [1, 64]])
            dst = fancy(c.vfull[:, rt * 1040:(rt + 1) * 1040],
                        [[65, 4], [1, 64]], 65 * 4 * grp)
            last = v.tensor_copy(dst, src)
        _inc(c, last, "dv", 1)

    def vh0(v):
        v.wait_ge(c.sem["pe"], pev1)
        vevh(v, 0)
    b.on("vector", vh0)
    dv_h0 = c.cnt["dv"]

    def tv2(t):
        t.wait_ge(c.sem["dv"], dv_h0)
        for rt in range(8, NT):
            vround(t, rt)
        _inc(c, t.last, "pe", 1)
    b.on("tensor", tv2)
    pev2v = c.cnt["pe"]

    def vh1(v):
        v.wait_ge(c.sem["pe"], pev2v)
        vevh(v, 1)
    b.on("vector", vh1)

    if grp == 0:
        def vone(v):
            v.wait_ge(c.sem["dv"], c.cnt["dv"])
            last = None
            for rt in range(NT):
                ap = fancy(c.vfull[:, rt * 1040:(rt + 1) * 1040],
                           [[65, 16], [1, 1]], 64)
                last = v.memset(ap, 1.0)
            _inc(c, last, "dv", 1)
        b.on("vector", vone)


def emit_head(b, grp, j):
    nc, c = b.nc, b.c
    h = 4 * grp + j

    for lt in range(NT):
        buf = lt % 2
        treg = 128 * buf
        sk_bef = c.cnt["sk"]

        dv_pre0 = c.cnt["dv"]

        def tt_mm(t, _lt=lt, _buf=buf, _tr=treg):
            t.wait_ge(c.sem["vv"], c.cnt["vv"])
            if _lt == 0:
                t.wait_ge(c.sem["dv"], dv_pre0)
            t.matmul(c.PS[:, _tr:_tr + 73],
                     c.qsl[j][0:64, _lt * 128:(_lt + 1) * 128],
                     c.demT[:, :], start=True, stop=True)
            _inc(c, t.last, "pe", 1)
        b.on("tensor", tt_mm)
        pev = c.cnt["pe"]

        def vt(v, _lt=lt, _buf=buf, _tr=treg):
            v.wait_ge(c.sem["pe"], pev)
            v.wait_ge(c.sem["sk"], sk_bef)
            tp = c.tpad[:, _buf * PADL:(_buf + 1) * PADL]
            vsync(c, v, v.tensor_scalar(
                tp[:, 127:200], c.PS[:, _tr:_tr + 73],
                c.PS[:, _tr:_tr + 1], None, op0=OP.subtract))
            vsync(c, v, v.tensor_scalar(
                c.acol[:, :], c.PS[:, _tr + 72:_tr + 73],
                c.PS[:, _tr:_tr + 1], None, op0=OP.subtract))
            vsync(c, v, v.tensor_scalar(
                tp[:, 200:PADL], tp[:, 200:PADL], 0.0, c.acol[:, :],
                op0=OP.mult, op1=OP.add))
            _inc(c, v.nop(), "vv", 1)
        b.on("vector", vt)
        vvv = c.cnt["vv"]

        def swr(s, _lt=lt, _buf=buf):
            s.wait_ge(c.sem["vv"], vvv)
            dst = c.d_skew.ap().copy()
            dst.ap = bass_rust.VecI64Pair([[PADL, 128], [1, PADL]])
            dst.offset = _buf * 128 * PADL
            _inc(c, s.dma_start(out=dst,
                                in_=c.tpad[:, _buf * PADL:(_buf + 1) * PADL]),
                 "sk", 16)
        b.on("sync", swr)
        skv = c.cnt["sk"]

        def srd(s, _lt=lt, _buf=buf):
            s.wait_ge(c.sem["sk"], skv)
            src = c.d_skew.ap().copy()
            src.ap = bass_rust.VecI64Pair([[PADL - 1, 128], [1, WIN]])
            src.offset = _buf * 128 * PADL + 127
            _inc(c, s.dma_start(out=c.bandl[:, _lt * WIN:(_lt + 1) * WIN],
                                in_=src), "sk", 16)
        b.on("sync", srd)

    sk_all = c.cnt["sk"]

    def tbt(t):
        t.wait_ge(c.sem["sk"], sk_all)
        t.wait_ge(c.sem["dv"], c.cnt["dv"])
        for lt in range(NT):
            s0 = lt * WIN
            t.transpose(c.PSb[:, 4096 + lt * 128:4096 + lt * 128 + 128],
                        c.bandl[:, s0 + 64:s0 + 192], c.ident[:, :])
            if lt >= 1:
                t.transpose(
                    c.PSb[64:128, 6144 + (lt - 1) * 128:6144 + (lt - 1) * 128 + 128],
                    c.bandl[:, s0:s0 + 64], c.ident[:, :],
                    tile_position=(0, 64))
            if lt <= 14:
                t.transpose(
                    c.PSb[0:32, 6144 + (lt + 1) * 128:6144 + (lt + 1) * 128 + 128],
                    c.bandl[:, s0 + 192:s0 + 224], c.ident[:, :])
        _inc(c, t.last, "pe", 1)
    b.on("tensor", tbt)
    pev = c.cnt["pe"]

    def vbt(v):
        v.wait_ge(c.sem["pe"], pev)
        v.tensor_copy(c.bandTB[:, :], c.PSb[:, 4096:6144])
        v.tensor_copy(c.bandTAC[0:32, 128:2048], c.PSb[0:32, 6272:8192])
        _inc(c, v.tensor_copy(c.bandTAC[64:128, 0:1920],
                              c.PSb[64:128, 6144:8064]), "dv", 1)
    b.on("vector", vbt)
    dv_band = c.cnt["dv"]

    for rt in range(NT):
        ac_bef = c.cnt["ac"]

        def ts(t, _rt=rt):
            t.wait_ge(c.sem["dv"], dv_band)
            t.wait_ge(c.sem["ac"], ac_bef)
            r0 = _rt * 128
            aug_end = max(0, (_rt - 1) * 128)

            mms = []

            def add(bank, kw):
                mms.append([bank, kw])

            w0 = 0
            while w0 < aug_end:
                w1 = min(w0 + 512 - (w0 % 512), aug_end)
                add(w0 // 512, dict(
                    out=c.PS[:, w0:w1], lhsT=c.ksl[j][0:65, r0:r0 + 128],
                    rhs=c.qsl[j][0:65, w0:w1]))
                w0 = w1
            w0 = r0
            while w0 < L:
                w1 = min(w0 + 512 - (w0 % 512), L)
                add(w0 // 512, dict(
                    out=c.PS[:, w0:w1], lhsT=c.ksl[j][0:64, r0:r0 + 128],
                    rhs=c.qsl[j][0:64, w0:w1]))
                w0 = w1
            if _rt >= 1:
                w0, w1 = aug_end, aug_end + 128
                mb = w0 // 512
                add(mb, dict(out=c.PS[0:32, w0:w1],
                             lhsT=c.ksl[j][0:64, r0:r0 + 32],
                             rhs=c.qsl[j][0:64, w0:w1]))
                add(mb, dict(out=c.PS[32:64, w0:w1],
                             lhsT=c.ksl[j][0:65, r0 + 32:r0 + 64],
                             rhs=c.qsl[j][0:65, w0:w1],
                             tile_position=(0, 32)))
                add(mb, dict(out=c.PS[64:128, w0:w1],
                             lhsT=c.ksl[j][0:65, r0 + 64:r0 + 128],
                             rhs=c.qsl[j][0:65, w0:w1],
                             tile_position=(0, 64)))
            add(_rt // 4, dict(out=c.PS[:, r0:r0 + 128], lhsT=c.ident[:, :],
                               rhs=c.bandTB[:, r0:r0 + 128]))
            if _rt <= 14:
                add((_rt + 1) // 4, dict(
                    out=c.PS[:, (_rt + 1) * 128:(_rt + 1) * 128 + 128],
                    lhsT=c.ident[64:128, :],
                    rhs=c.bandTAC[64:128, r0:r0 + 128],
                    tile_position=(64, 0)))
            if _rt >= 1:
                add((_rt - 1) // 4, dict(
                    out=c.PS[:, (_rt - 1) * 128:(_rt - 1) * 128 + 128],
                    lhsT=c.ident[0:32, :],
                    rhs=c.bandTAC[0:32, r0:r0 + 128]))

            first_i = {}
            last_i = {}
            for i, (bank, _) in enumerate(mms):
                first_i.setdefault(bank, i)
                last_i[bank] = i
            for i, (bank, kw) in enumerate(mms):
                t.matmul(kw.pop("out"), kw.pop("lhsT"), kw.pop("rhs"),
                         start=(first_i[bank] == i), stop=(last_i[bank] == i),
                         skip_group_check=True, **kw)
            _inc(c, t.last, "pe", 1)
        b.on("tensor", ts)
        pev = c.cnt["pe"]

        def sx(s):
            s.wait_ge(c.sem["pe"], pev)
            _inc(c, s.activation(c.eT[:, 0:L],
                                 c.PS[:, 0:L], AF.Exp, scale=SCALE), "ac", 1)
        b.on("scalar", sx)
        acv = c.cnt["ac"]

        def tav(t, _rt=rt):
            t.wait_ge(c.sem["ac"], acv)
            for nb in range(4):
                t.matmul(c.PS[0:65, 2048 + nb * 512:2048 + (nb + 1) * 512],
                         c.vfull[:, _rt * 1040 + 65 * h:_rt * 1040 + 65 * h + 65],
                         c.eT[:, nb * 512:(nb + 1) * 512],
                         start=(_rt == 0), stop=(_rt == 15),
                         skip_group_check=True)
            _inc(c, t.last, "pe", 1)
        b.on("tensor", tav)

    pe_av = c.cnt["pe"]

    def vzr(v):
        v.wait_ge(c.sem["pe"], pe_av)
        with nc.allow_low_precision(reason="1/Z bf16: 2^-9 rel on softmax "
                                    "normalizer, well inside tolerance"):
            _inc(c, v.reciprocal(c.zrow[:, :], c.PS[64:65, 2048:4096]),
                 "dv", 1)
    b.on("vector", vzr)
    dvv = c.cnt["dv"]

    def tbz(t):
        t.wait_ge(c.sem["dv"], dvv)
        for nb in range(4):
            t.matmul(c.PS[:, nb * 512:(nb + 1) * 512], c.onesr[0:1, 0:128],
                     c.zrow[:, nb * 512:(nb + 1) * 512],
                     start=True, stop=True)
        _inc(c, t.last, "pe", 1)
    b.on("tensor", tbz)
    pev = c.cnt["pe"]

    def vo(v):
        v.wait_ge(c.sem["pe"], pev)
        vsync(c, v, v.tensor_copy(c.zbc[:, :], c.PS[0:64, 0:L]))
        _inc(c, v.tensor_tensor(
            c.outT[64 * (h & 1):64 * (h & 1) + 64,
                   (h >> 1) * L:(h >> 1) * L + L],
            c.PS[0:64, 2048:4096], c.zbc[:, :], op=OP.mult), "dv", 1)
        v.wait_ge(c.sem["dv"], c.cnt["dv"])
        _inc(c, v.nop(), "ac", 1)
    b.on("vector", vo)


def emit_oproj(b):
    nc, c = b.nc, b.c
    wo = c.wosl

    def gw(g):
        g.wait_ge(c.sem["pe"], c.cnt["pe"])
        _inc(c, g.dma_start(out=wo, in_=c.i["wo_s"]), "gp", 16)
    b.on("gpsimd", gw)
    gpv = c.cnt["gp"]
    dv_all = c.cnt["dv"]

    for lt in range(NT):
        reg = lt % 2
        dv_bef = c.cnt["dv"]

        def to(t, _lt=lt, _reg=reg):
            t.wait_ge(c.sem["gp"], gpv)
            t.wait_ge(c.sem["dv"], dv_all if _lt < 2 else c.cnt["dv"])
            for half in range(2):
                out = c.PS[:, _reg * 2048 + half * 512:
                           _reg * 2048 + half * 512 + 512]
                for ch in range(8):
                    t.matmul(out,
                             c.outT[:, ch * L + _lt * 128:ch * L + _lt * 128 + 128],
                             wo[:, ch * 1024 + half * 512:
                                ch * 1024 + half * 512 + 512],
                             start=(ch == 0), stop=False)
                t.matmul(out, c.onesr[0:1, 0:128],
                         c.bor[0:1, half * 512:half * 512 + 512],
                         start=False, stop=True)
            _inc(c, t.last, "pe", 1)
        b.on("tensor", to)
        pev = c.cnt["pe"]

        par = lt % 2
        st_pre = c.cnt["st"]

        def sxl(s, _lt=lt, _p=par):
            s.wait_ge(c.sem["st"], st_pre)
            _inc(c, s.dma_start(out=c.xb[:, _p * D:(_p + 1) * D],
                                in_=c.d_xr.ap()[_lt * 128:(_lt + 1) * 128, :]),
                 "rb", 16)
        b.on("sync", sxl)
        rbx = c.cnt["rb"]

        def vr(v, _lt=lt, _reg=reg, _p=par):
            v.wait_ge(c.sem["pe"], pev)
            v.wait_ge(c.sem["rb"], rbx)
            _inc(c, v.scalar_tensor_tensor(
                c.xb[:, _p * D:(_p + 1) * D],
                c.PS[:, _reg * 2048:_reg * 2048 + 1024], 1.0,
                c.xb[:, _p * D:(_p + 1) * D], op0=OP.mult, op1=OP.add),
                "dv", 1)
        b.on("vector", vr)
        dvx = c.cnt["dv"]

        def sxs(s, _lt=lt, _p=par):
            s.wait_ge(c.sem["dv"], dvx)
            _inc(c, s.dma_start(out=c.d_xr.ap()[_lt * 128:(_lt + 1) * 128, :],
                                in_=c.xb[:, _p * D:(_p + 1) * D]), "st", 16)
        b.on("sync", sxs)


def emit_out(b):
    nc, c = b.nc, b.c

    def so(s):
        s.wait_ge(c.sem["st"], c.cnt["st"])
        _inc(c, s.dma_start(out=c.o_out, in_=c.d_xr.ap()), "st", 16)
        s.wait_ge(c.sem["st"], c.cnt["st"])
    b.on("sync", so)


def emit_all(b, upto=3):
    emit_p0(b)
    b.barrier()
    if upto == -2:
        emit_attn(b)
        b.barrier()
        emit_out(b)
        return
    if upto >= 1:
        emit_ffn(b)
        b.barrier()
    if upto >= 2:
        emit_attn(b)
        b.barrier()
    if upto >= 3:
        emit_ffn(b)
        b.barrier()
    emit_out(b)


def build_full(upto=3):
    nc, c = build_program()
    with c.es:
        with nc.Block() as block:
            b = Bld(nc, c, block)
            emit_all(b, upto)
    return nc


def kernel(**inputs):
    w = _prep_weights(inputs)
    x_full = np.ascontiguousarray(
        np.asarray(inputs["L0_attn_hidden_states"], dtype=np.float32))
    nc = build_full()
    z_up = np.zeros((SLOTS, D), ml_dtypes.bfloat16)
    z_dn = np.zeros((SLOTS, DH), ml_dtypes.bfloat16)
    in_maps = []
    for core in range(8):
        m = {"x": x_full[core], "d_xg_up": z_up, "d_xg_dn": z_dn}
        m.update(w)
        in_maps.append(m)
    res = run_bass_kernel_spmd(nc, in_maps, core_ids=list(range(8)))
    out = np.stack([np.asarray(r["out"], dtype=np.float32)
                    for r in res.results], axis=0)
    return out


# version 10
# speedup vs baseline: 1.0982x; 1.0982x over previous
import contextlib
import numpy as np
import ml_dtypes
import bass_rust
import concourse.bass as bass
import concourse.mybir as mybir
from concourse.bass import IndirectOffsetOnAxis
from concourse.bass_utils import run_bass_kernel_spmd

F32 = mybir.dt.float32
BF16 = mybir.dt.bfloat16
U32 = mybir.dt.uint32
AF = mybir.ActivationFunctionType
OP = mybir.AluOpType
AX = mybir.AxisListType

L = 2048
NT = 16
D = 1024
NH = 16
E = 64
CAP = 128
NBLK = 96
SLOTS = NBLK * CAP
DH = 4096
WIN = 224
PADL = WIN + 127
SCALE = 0.125


def fancy(base_ap, axes, extra_off=0):
    ap = base_ap.copy()
    part = list(ap.ap)[0]
    ap.ap = bass_rust.VecI64Pair([list(part)] + [list(a) for a in axes])
    ap.offset = ap.offset + extra_off
    return ap


def _prep_weights(inp):
    f32 = lambda a: np.ascontiguousarray(np.asarray(a), dtype=np.float32)
    w = {}
    Wq, Wk, Wv, Wo = (f32(inp[f"L0_attn_W{ch}"]) for ch in "qkvo")
    sw = lambda wt: np.ascontiguousarray(
        wt.reshape(8, 128, 1024).transpose(1, 0, 2).reshape(128, 8192))
    w["wq_s"] = sw(f32(Wq.T))
    w["wk_s"] = sw(f32(Wk.T))
    w["wv_s"] = sw(f32(Wv.T))
    w["wo_s"] = sw(f32(Wo.T))
    w["wq_r"] = Wq
    w["bq"] = f32(inp["L0_attn_bq"]).reshape(8, 128).T.copy()
    w["bk"] = f32(inp["L0_attn_bk"]).reshape(8, 128).T.copy()
    w["bv"] = f32(inp["L0_attn_bv"]).reshape(1, D)
    w["bo"] = f32(inp["L0_attn_bo"]).reshape(1, D)
    w["demT"] = f32(np.asarray(inp["L0_attn_dist_emb"]).T)

    w["wup_s"] = np.ascontiguousarray(
        f32(np.asarray(inp["L0_ffn_W_up"]).T).reshape(8, 128, 64)
        .transpose(1, 0, 2).reshape(128, 512))
    w["wdn_s"] = np.ascontiguousarray(
        f32(np.asarray(inp["L0_ffn_W_down"]).T).reshape(32, 128, 64)
        .transpose(1, 0, 2).reshape(128, 2048))
    A_up = f32(inp["L0_ffn_A_up"])
    B_up = f32(inp["L0_ffn_B_up"])
    A_dn = f32(inp["L0_ffn_A_down"])
    B_dn = f32(inp["L0_ffn_B_down"])
    w["bupg"] = np.ascontiguousarray(
        np.tile(B_up.transpose(2, 0, 1), (4, 1, 1))
        .transpose(1, 0, 2).reshape(E * 128, 64))
    z = np.zeros((E, 4, 32, 4, 64), np.float32)
    at = A_up.transpose(0, 2, 1)
    for pl in range(4):
        z[:, pl, :, pl, :] = at
    w["aupbd"] = np.ascontiguousarray(z.reshape(E * 128, 256))
    z = np.zeros((E, 2, 64, 2, 32), np.float32)
    bt = B_dn.transpose(0, 2, 1)
    for m in range(2):
        z[:, m, :, m, :] = bt
    w["bdng"] = np.ascontiguousarray(z.reshape(E * 128, 64))
    z = np.zeros((E, 2, 4, 32, 4, 32), np.float32)
    for h in range(2):
        ah = A_dn[:, :, 32 * h:32 * h + 32].transpose(0, 2, 1)
        for pl in range(4):
            z[:, h, pl, :, pl, :] = ah
    w["adnbd"] = np.ascontiguousarray(
        z.reshape(E, 2, 128, 128).transpose(0, 2, 1, 3).reshape(E * 128, 256))
    w["scup"] = np.tile(f32(inp["L0_ffn_scale_up"]).reshape(1, 1), (128, 1))
    w["scdn"] = np.tile(f32(inp["L0_ffn_scale_down"]).reshape(1, 1), (128, 1))
    w["biasup"] = f32(inp["L0_ffn_bias_up"]).reshape(1, DH)
    w["biasdn"] = f32(inp["L0_ffn_bias_down"]).reshape(1, D)
    w["ident"] = np.eye(128, dtype=np.float32)
    w["onesr"] = np.ones((1, 128), np.float32)
    w["halfr"] = np.full((1, 128), 0.5, np.float32)
    w["iotaE"] = np.tile(np.arange(E, dtype=np.float32), (128, 1))
    w["sltri"] = np.triu(np.ones((E, E), np.float32), 1)
    w["iotaB"] = np.tile(np.arange(NBLK, dtype=np.float32), (E, 1))
    w["eiota"] = np.arange(E, dtype=np.float32).reshape(E, 1)
    w["onesf"] = np.ones((1, 128), np.float32)
    w["iota128"] = np.arange(128, dtype=np.float32).reshape(128, 1)
    return w


_W_SHAPES = {
    "wq_s": [128, 8192], "wk_s": [128, 8192], "wv_s": [128, 8192],
    "wo_s": [128, 8192],
    "wq_r": [D, D], "bq": [128, 8], "bk": [128, 8], "bv": [1, D],
    "bo": [1, D], "demT": [64, 73], "wup_s": [128, 512],
    "wdn_s": [128, 2048],
    "bupg": [E * 128, 64], "aupbd": [E * 128, 256], "bdng": [E * 128, 64],
    "adnbd": [E * 128, 256], "scup": [128, 1], "scdn": [128, 1],
    "biasup": [1, DH], "biasdn": [1, D], "ident": [128, 128],
    "onesr": [1, 128], "halfr": [1, 128], "iotaE": [128, E],
    "sltri": [E, E], "iotaB": [E, NBLK], "eiota": [E, 1],
    "onesf": [1, 128], "iota128": [128, 1],
}


class Ctx:
    pass


class Rec:

    def __init__(self, eng):
        object.__setattr__(self, "_e", eng)
        object.__setattr__(self, "last", None)

    def __getattr__(self, name):
        a = getattr(object.__getattribute__(self, "_e"), name)
        if callable(a) and name not in ("wait_ge", "nop"):
            def call(*args, **kw):
                r = a(*args, **kw)
                if r is not None and hasattr(r, "then_inc"):
                    object.__setattr__(self, "last", r)
                return r
            return call
        return a


class Bld:
    def __init__(self, nc, c, blk):
        self.nc, self.c, self.blk = nc, c, blk
        self.barv = 0

    def on(self, eng, fn):
        getattr(self.blk, eng)(lambda e: fn(Rec(e)))

    def barrier(self):
        c = self.c
        self.barv += 5
        for eng in ("sync", "tensor", "vector", "scalar", "gpsimd"):
            def body(e, _v=self.barv, _eng=eng):
                if _eng == "vector":
                    e.memset(c.dmy[:, :], 0.0).then_inc(c.sem["bar"], 1)
                elif _eng == "scalar":
                    e.copy(c.dmy2[:, :], c.dmy2[:, :]).then_inc(c.sem["bar"], 1)
                else:
                    e.nop().then_inc(c.sem["bar"], 1)
                e.wait_ge(c.sem["bar"], _v)
            self.on(eng, body)


def _inc(c, instr, name, k):
    instr.then_inc(c.sem[name], k)
    c.cnt[name] += k


def vsync(c, v, instr):
    _inc(c, instr, "vv", 1)
    v.wait_ge(c.sem["vv"], c.cnt["vv"])


def build_program():
    nc = bass.Bass("TRN2", target_bir_lowering=False)
    c = Ctx()
    c.nc = nc
    c.i_x = nc.dram_tensor("x", [L, D], F32, kind="ExternalInput").ap()
    c.i = {k: nc.dram_tensor(k, shp, F32, kind="ExternalInput").ap()
           for k, shp in _W_SHAPES.items()}
    c.o_out = nc.dram_tensor("out", [L, D], F32, kind="ExternalOutput").ap()

    c.d_xg_up = nc.dram_tensor("d_xg_up", [SLOTS, D], BF16,
                               kind="ExternalInput")
    c.d_yp_up = nc.dram_tensor("d_yp_up", [SLOTS, DH], BF16, kind="Internal")
    c.d_xg_dn = nc.dram_tensor("d_xg_dn", [SLOTS, DH], BF16,
                               kind="ExternalInput")
    c.d_yp_dn = nc.dram_tensor("d_yp_dn", [SLOTS, D], BF16, kind="Internal")
    c.d_h = nc.dram_tensor("d_h", [L, DH], BF16, kind="Internal")
    c.d_skew = nc.dram_tensor("d_skew", [2 * 128 * PADL], BF16, kind="Internal")
    c.d_xr = nc.dram_tensor("d_xr", [L, D], F32, kind="Internal")

    es = contextlib.ExitStack()
    c.es = es
    sb = lambda name, shp, dt: es.enter_context(nc.sbuf_tensor(name, shp, dt))

    c.xb = sb("xb", [128, 2 * D], F32)
    c.xnb = sb("xnb", [128, 2 * D], BF16)
    c.wuphl = sb("swuphl", [128, 8 * 128], BF16)
    c.wdnhl = sb("swdnhl", [128, 32 * 128], BF16)
    c.demT = sb("sdemT", [64, 73], BF16)
    pass
    c.ident = sb("sident", [128, 128], BF16)
    c.identF = sb("sidentF", [128, 128], F32)
    c.onesr = sb("sonesr", [1, 128], BF16)
    c.halfr = sb("shalfr", [1, 128], BF16)
    c.iotaE = sb("siotaE", [128, E], F32)
    c.bq = sb("sbq", [128, 8], F32)
    c.bq16 = sb("sbq16", [128, 8], BF16)
    c.bk = sb("sbk", [128, 8], F32)
    c.bvr = sb("sbvr", [1, D], BF16)
    c.bor = sb("sbor", [1, D], BF16)
    c.scupc = sb("sscupc", [128, 1], F32)
    c.scdnc = sb("sscdnc", [128, 1], F32)
    c.bdnbc = sb("sbdnbc", [128, D], BF16)
    c.ddbd = sb("sddbd", [128, 8 * 16], BF16)
    c.ddbq = sb("sddbq", [16, 1], F32)
    c.dd = sb("sdd", [64, 1], F32)

    c.ssum = sb("ssum", [128, NT], F32)
    c.ssq = sb("ssq", [128, NT], F32)
    c.smean = sb("smean", [128, NT], F32)
    c.srstd = sb("srstd", [128, NT], F32)
    c.logL = sb("logL", [128, E], F32)
    c.mxv = sb("mxv", [128, 8], F32)
    c.mxi = sb("mxi", [128, 8], U32)
    c.dstL = sb("dstL", [128, E], F32)
    c.p1 = sb("p1", [128, NT], F32)
    c.p2 = sb("p2", [128, NT], F32)
    c.pd = sb("pd", [128, NT], F32)
    c.dst1 = [sb(f"dst1_{t}", [128, 1], U32) for t in range(NT)]
    c.dst2 = [sb(f"dst2_{t}", [128, 1], U32) for t in range(NT)]
    c.idxf = sb("idxf", [128, 2], F32)

    c.hTc = sb("hTc", [128, 32 * 128], BF16)
    c.sqdump = c.hTc[:, 0:1024]
    c.bupr16 = c.hTc[0:1, 0:DH]
    c.bdnr16 = c.hTc[0:1, 0:D]
    c.m1L = c.hTc[:, 2048:2048 + NT * E]
    c.m2L = c.hTc[:, 3072:3072 + NT * E]
    c.wksl = c.hTc[:, 0:2048]
    c.wvsl = None
    c.augT = c.hTc[0:16, 2048:4096]

    c.ar_a = sb("ar_a", [128, 8192], BF16)
    c.xgtg = c.ar_a[:, :]
    c.wstr = c.ar_a[:, 2048:4096]
    c.xg2 = c.ar_a[:, :]
    c.bupbc = c.ar_a[:, 0:DH]
    c.ar_b = sb("ar_b", [128, 16384], BF16)
    c.xnT = c.ar_b[:, :]
    _bf = c.ar_b[:, :].bitcast(F32)
    c.t1f = _bf[:, 0:DH]
    c.t2f = _bf[:, DH:2 * DH]
    c.t3f = _bf[:, 0:D]

    c.ar_c = sb("ar_c", [128, 16384], BF16)
    c.outT = c.ar_c[:, :]
    c.demTf = c.ar_c[:, :].bitcast(F32)[0:64, 0:73]
    c.gb1 = c.ar_c[:, 0:DH]
    c.gb2 = c.ar_c[:, DH:2 * DH]
    c.hbuf = c.ar_c[:, 2 * DH:4 * DH]
    c.waug = c.ar_c[0:16, 0:1024]
    c.ar_d = sb("ar_d", [128, NT * 1040], BF16)
    _fd = c.ar_d[:, :].bitcast(F32)
    c.sltri = _fd[0:64, 6144:6208]
    c.iotaB96 = _fd[0:64, 6208:6208 + NBLK]
    c.eiotac = _fd[0:64, 6306:6307]
    c.onesrF = _fd[0:1, 6307:6435]
    c.iota128c = _fd[:, 6435:6436]
    c.rsc = _fd[0:64, 6440:6456]
    c.indb = _fd[0:64, 6456:6456 + NBLK]
    c.indb2 = _fd[0:64, 6552:6552 + NBLK]
    c.ebr = _fd[0:1, 6648:6648 + NBLK]
    c.widxu = c.ar_d[:, :].bitcast(U32)[:, 6744:6744 + NBLK]
    c.rscU = c.ar_d[:, :].bitcast(U32)[0:64, 6440:6456]
    c.vfull = c.ar_d[:, :]
    c.yp = c.ar_d[:, 0:2 * DH]
    c.yp2 = c.ar_d[:, 2 * DH:2 * DH + 2 * D]
    c.y1b = c.ar_d[:, 2 * DH + 2 * D:2 * DH + 3 * D]
    c.y2b = c.ar_d[:, 2 * DH + 3 * D:2 * DH + 4 * D]
    c.ar_e = sb("ar_e", [128, 16384], BF16)
    c.sa = c.ar_e[:, 0:4096]
    c.xlb = c.ar_e[:, 4096:6144]
    c.br = c.ar_e[:, 4096:8192]
    c.sa2 = c.ar_e[:, 8192:12288]
    c.br2 = c.ar_e[:, 12288:16384]
    c.qsl = [c.ar_e[0:65, jj * 2048:(jj + 1) * 2048] for jj in range(4)]
    c.ksl = [c.ar_e[0:65, 8192 + jj * 2048:8192 + (jj + 1) * 2048]
             for jj in range(4)]
    c.wosl = c.ar_e[:, 0:8192]
    c.ar_f = sb("ar_f", [128, 10240], BF16)
    _ff = c.ar_f[:, :].bitcast(F32)
    c.logT = _ff[0:64, 0:L]
    c.scanT = _ff[0:64, L:2 * L]
    c.dstT = c.scanT
    c.mT = c.ar_f[0:64, 8192:10240]
    c.zrow = c.ar_f[0:1, 8192:10240]
    c.zbc = c.hTc[64:128, 2048:4096]
    c.eT = c.ar_f[:, 0:2048]
    c.bandl = c.ar_f[:, 2048:2048 + NT * WIN]
    c.bandTB = c.ar_f[:, 5632:7680]
    c.bandTAC = c.ar_f[:, 7680:9728]

    c.aupbd_s = sb("aupbd_s", [128, 2 * 320], BF16)
    c.adnbd_s = sb("adnbd_s", [128, 2 * 320], BF16)

    c.waugT = sb("waugT", [128, 16 * 8], BF16)
    c.tpad = c.ar_a[:, 4096:4096 + 2 * PADL]
    c.avl = c.ar_a[:, 4800:4864]
    c.zrec = sb("zrec", [128, 1], F32)
    c.tcol = sb("tcol", [128, 1], F32)
    c.acol = sb("acol", [128, 1], F32)
    c.dmy = sb("dmy", [128, 1], F32)
    c.dmy2 = sb("dmy2", [128, 1], F32)
    c.PS = es.enter_context(nc.psum_tensor("PS", [128, 4096], F32))
    c.PSb = c.PS.ap().bitcast(BF16)

    sems = ["ld", "pe", "dv", "ac", "gp", "st", "bar", "sk", "ig", "rb", "vv",
            "iz", "s0", "s1", "h0", "h1", "g0", "g1", "r0", "r1"]
    c.sem = {s: es.enter_context(nc.semaphore(s)) for s in sems}
    c.cnt = {s: 0 for s in sems}
    c.up_st_war = {0: 0, 1: 0}
    c.up_rb_war = {0: 0, 1: 0}
    c.dn_rb_war = {0: 0, 1: 0}
    c.up_w_war = {0: 0, 1: 0}
    c.dn_w_war = {0: 0, 1: 0}
    c.dn_st_war = {0: 0, 1: 0}
    c.cu_st_war = {0: 0, 1: 0}
    c.cu_pe_war = {0: 0, 1: 0}
    return nc, c


def emit_p0(b):
    nc, c = b.nc, b.c

    def sync(s):
        _inc(c, s.dma_start(out=c.d_xr.ap(), in_=c.i_x), "rb", 16)
        arbF = c.ar_b[:, :].bitcast(F32)
        _inc(c, s.dma_start(out=arbF[:, 0:512], in_=c.i["wup_s"]), "rb", 16)
        _inc(c, s.dma_start(out=arbF[:, 512:2560], in_=c.i["wdn_s"]),
             "rb", 16)
    b.on("sync", sync)
    c.x_rb = c.cnt["rb"]

    def gp(g):
        i = c.i
        _inc(c, g.memset(c.dmy2[:, :], 0.0), "iz", 1)
        loads = [
            (c.demT[:, :], i["demT"]),
            (c.demTf[:, :], i["demT"]),
            (c.ident[:, :], i["ident"]),
            (c.identF[:, :], i["ident"]),
            (c.onesr[:, :], i["onesr"]),
            (c.halfr[:, :], i["halfr"]),
            (c.iotaE[:, :], i["iotaE"]),
            (c.bq[:, :], i["bq"]),
            (c.bq16[:, :], i["bq"]),
            (c.bk[:, :], i["bk"]),
            (c.bvr[:, :], i["bv"]),
            (c.bor[:, :], i["bo"]),
            (c.scupc[:, :], i["scup"]),
            (c.scdnc[:, :], i["scdn"]),
            (c.bdnr16, i["biasdn"]),
        ]
        for dst, srcap in loads:
            _inc(c, g.dma_start(out=dst, in_=srcap), "ld", 16)
    b.on("gpsimd", gp)

    def sc0(s):
        s.wait_ge(c.sem["iz"], 1)
    b.on("scalar", sc0)
    ldv = c.cnt["ld"]

    def te(t):
        t.wait_ge(c.sem["ld"], ldv)
        for kk in range(2):
            t.matmul(c.PS[:, 512 + kk * 512:1024 + kk * 512], c.halfr[:, :],
                     c.bdnr16[:, kk * 512:(kk + 1) * 512], start=True, stop=True)
        _inc(c, t.last, "pe", 1)
    b.on("tensor", te)
    pev = c.cnt["pe"]

    def ve(v):
        v.wait_ge(c.sem["pe"], pev)
        v.wait_ge(c.sem["rb"], c.x_rb)
        arbF = c.ar_b[:, :].bitcast(F32)
        wupf, wdnf = arbF[:, 0:512], arbF[:, 512:2560]
        vsync(c, v, v.tensor_copy(fancy(c.wuphl[:, :], [[128, 8], [1, 64]]), wupf))
        vsync(c, v, v.tensor_tensor(
            fancy(c.wuphl[:, :], [[128, 8], [1, 64]], 64), wupf,
            fancy(c.wuphl[:, :], [[128, 8], [1, 64]]), op=OP.subtract))
        vsync(c, v, v.tensor_copy(fancy(c.wdnhl[:, :], [[128, 32], [1, 64]]), wdnf))
        vsync(c, v, v.tensor_tensor(
            fancy(c.wdnhl[:, :], [[128, 32], [1, 64]], 64), wdnf,
            fancy(c.wdnhl[:, :], [[128, 32], [1, 64]]), op=OP.subtract))
        vsync(c, v, v.tensor_copy(c.bdnbc[:, :], c.PS[:, 512:1536]))
        vsync(c, v, v.tensor_sub(c.dd[:, :], c.demTf[:, 72:73],
                                 c.demTf[:, 0:1]))
        vsync(c, v, v.memset(c.ddbd[:, :], 0.0))
        last = None
        for h in range(16):
            rb0 = 0 if (h % 2 == 0) else 64
            ch = h // 2
            last = v.tensor_copy(
                c.ddbd[rb0:rb0 + 64, ch * 16 + h:ch * 16 + h + 1], c.dd[:, :])
        _inc(c, last, "dv", 1)
    b.on("vector", ve)
    dvv = c.cnt["dv"]

    def te3(t):
        t.wait_ge(c.sem["dv"], dvv)
        for kk in range(8):
            t.matmul(c.PS[0:16, 1536:1537],
                     c.ddbd[:, kk * 16:(kk + 1) * 16], c.bq16[:, kk:kk + 1],
                     start=(kk == 0), stop=(kk == 7))
        _inc(c, t.last, "pe", 1)
    b.on("tensor", te3)
    pev = c.cnt["pe"]

    def ve3(v):
        v.wait_ge(c.sem["pe"], pev)
        _inc(c, v.tensor_copy(c.ddbq[:, :], c.PS[0:16, 1536:1537]), "dv", 1)
    b.on("vector", ve3)


def emit_ln(b):
    nc, c = b.nc, b.c

    st_pre0 = c.cnt["st"]
    rb_pre0 = c.cnt["rb"]
    for t in range(NT):
        par = t % 2
        dv_pre, ac_pre = c.cnt["dv"], c.cnt["ac"]

        def sl(s, _t=t, _p=par):
            if _t == 0:
                s.wait_ge(c.sem["rb"], rb_pre0)
                s.wait_ge(c.sem["st"], st_pre0)
            s.wait_ge(c.sem["dv"], dv_pre)
            s.wait_ge(c.sem["ac"], ac_pre)
            _inc(c, s.dma_start(out=c.xb[:, _p * D:(_p + 1) * D],
                                in_=c.d_xr.ap()[_t * 128:(_t + 1) * 128, :]),
                 "rb", 16)
        b.on("sync", sl)
        rbv = c.cnt["rb"]

        def ve(v, _t=t, _p=par):
            v.wait_ge(c.sem["rb"], rbv)
            _inc(c, v.tensor_reduce(c.ssum[:, _t:_t + 1],
                                    c.xb[:, _p * D:(_p + 1) * D],
                                    axis=AX.X, op=OP.add), "dv", 1)
        b.on("vector", ve)

        def sc(s, _t=t, _p=par):
            s.wait_ge(c.sem["rb"], rbv)
            _inc(c, s.activation(c.sqdump[:, :], c.xb[:, _p * D:(_p + 1) * D],
                                 AF.Square, accum_out=c.ssq[:, _t:_t + 1]),
                 "ac", 1)
        b.on("scalar", sc)

    dvv, acv = c.cnt["dv"], c.cnt["ac"]

    def ve2(v):
        v.wait_ge(c.sem["dv"], dvv)
        v.wait_ge(c.sem["ac"], acv)
        vsync(c, v, v.tensor_scalar(c.smean[:, :], c.ssum[:, :], 1.0 / D, None,
                                    op0=OP.mult))
        vsync(c, v, v.tensor_scalar(c.srstd[:, :], c.ssq[:, :], 1.0 / D, None,
                                    op0=OP.mult))
        vsync(c, v, v.tensor_tensor(c.ssum[:, :], c.smean[:, :], c.smean[:, :],
                                    op=OP.mult))
        vsync(c, v, v.tensor_tensor(c.srstd[:, :], c.srstd[:, :],
                                    c.ssum[:, :], op=OP.subtract))
        vsync(c, v, v.tensor_scalar(c.srstd[:, :], c.srstd[:, :], 1e-5, None,
                                    op0=OP.add))
        _inc(c, v.nop(), "dv", 1)
    b.on("vector", ve2)
    dvv = c.cnt["dv"]

    def sc2(s):
        s.wait_ge(c.sem["dv"], dvv)
        _inc(c, s.activation(c.srstd[:, :], c.srstd[:, :], AF.Sqrt), "ac", 1)
    b.on("scalar", sc2)
    acv = c.cnt["ac"]

    def ve3(v):
        v.wait_ge(c.sem["ac"], acv)
        _inc(c, v.reciprocal(c.srstd[:, :], c.srstd[:, :]), "dv", 1)
    b.on("vector", ve3)


def emit_xn_tile(b, t, war_sems=(), want_lo=False):
    nc, c = b.nc, b.c
    par = t % 2
    waits = list(war_sems)

    def sl(s, _t=t, _p=par, _w=tuple(waits)):
        for nm, val in _w:
            s.wait_ge(c.sem[nm], val)
        _inc(c, s.dma_start(out=c.xb[:, _p * D:(_p + 1) * D],
                            in_=c.d_xr.ap()[_t * 128:(_t + 1) * 128, :]),
             "rb", 16)
    b.on("sync", sl)
    rbv = c.cnt["rb"]

    def vn(v, _t=t, _p=par, _lo=want_lo):
        v.wait_ge(c.sem["rb"], rbv)
        if not _lo:
            _inc(c, v.scalar_tensor_tensor(
                c.xnb[:, _p * D:(_p + 1) * D], c.xb[:, _p * D:(_p + 1) * D],
                c.smean[:, _t:_t + 1],
                c.srstd[:, _t:_t + 1].to_broadcast((128, D)),
                op0=OP.subtract, op1=OP.mult), "dv", 1)
        else:
            xnf = c.ar_b[:, :].bitcast(F32)[:, 0:D]
            vsync(c, v, v.scalar_tensor_tensor(
                xnf, c.xb[:, _p * D:(_p + 1) * D],
                c.smean[:, _t:_t + 1],
                c.srstd[:, _t:_t + 1].to_broadcast((128, D)),
                op0=OP.subtract, op1=OP.mult))
            vsync(c, v, v.tensor_copy(c.xnb[:, _p * D:(_p + 1) * D], xnf))
            _inc(c, v.tensor_tensor(
                c.xlb[:, _p * D:(_p + 1) * D], xnf,
                c.xnb[:, _p * D:(_p + 1) * D], op=OP.subtract), "dv", 1)
    b.on("vector", vn)
    return c.cnt["dv"]


def emit_router(b, width, wslab, logits_ready):
    nc, c = b.nc, b.c
    nch = width // 128

    if not logits_ready:
        for t in range(NT):
            xn_dv = emit_xn_tile(b, t, war_sems=(("pe", c.cnt["pe"]),),
                                 want_lo=True)
            par = t % 2

            def teT(tt, _t=t, _p=par, _x=xn_dv):
                tt.wait_ge(c.sem["dv"], _x)
                if _t > 0:
                    tt.wait_ge(c.sem["dv"], c.cnt["dv"])
                for ch in range(8):
                    tt.transpose(c.PSb[:, 4096 + ch * 128:4096 + ch * 128 + 128],
                                 c.xnb[:, _p * D + ch * 128:_p * D + ch * 128 + 128],
                                 c.ident[:, :])
                for ch in range(8):
                    tt.transpose(c.PSb[:, 5120 + ch * 128:5120 + ch * 128 + 128],
                                 c.xlb[:, _p * D + ch * 128:_p * D + ch * 128 + 128],
                                 c.ident[:, :])
                _inc(c, tt.last, "pe", 1)
            b.on("tensor", teT)
            pev = c.cnt["pe"]

            def veT(v):
                v.wait_ge(c.sem["pe"], pev)
                v.tensor_copy(c.hTc[:, 0:1024], c.PSb[:, 4096:5120])
                _inc(c, v.tensor_copy(c.hTc[:, 1024:2048],
                                      c.PSb[:, 5120:6144]), "dv", 1)
            b.on("vector", veT)
            dvv = c.cnt["dv"]

            def teM(tt, _t=t):
                tt.wait_ge(c.sem["dv"], dvv)
                out128 = c.PS[0:128, _t * 128:(_t + 1) * 128]
                out64 = c.PS[0:64, _t * 128:(_t + 1) * 128]
                for ch in range(8):
                    if ch < 7:
                        tt.matmul(out128,
                                  c.wuphl[:, ch * 128:(ch + 1) * 128],
                                  c.hTc[:, ch * 128:(ch + 1) * 128],
                                  start=(ch == 0), stop=False)
                        tt.matmul(out64,
                                  c.wuphl[:, ch * 128:ch * 128 + 64],
                                  c.hTc[:, 1024 + ch * 128:1024 + (ch + 1) * 128],
                                  start=False, stop=False)
                    else:
                        tt.matmul(out64,
                                  c.wuphl[:, ch * 128:ch * 128 + 64],
                                  c.hTc[:, 1024 + ch * 128:1024 + (ch + 1) * 128],
                                  start=False, stop=False)
                        tt.matmul(out128,
                                  c.wuphl[:, ch * 128:(ch + 1) * 128],
                                  c.hTc[:, ch * 128:(ch + 1) * 128],
                                  start=False, stop=True)
                _inc(c, tt.last, "pe", 1)
            b.on("tensor", teM)
        pev = c.cnt["pe"]

        def vel(v):
            v.wait_ge(c.sem["pe"], pev)
            vsync(c, v, v.tensor_copy(c.logT[:, :], c.PS[64:128, 0:L]))
            _inc(c, v.tensor_tensor(c.logT[:, :], c.logT[:, :],
                                    c.PS[0:64, 0:L], op=OP.add), "dv", 1)
        b.on("vector", vel)

    dvv = c.cnt["dv"]

    vv_mark = [0]

    def teL(tt):
        tt.wait_ge(c.sem["dv"], dvv)
        for t in range(8):
            tt.transpose(c.PS[:, 2048 + t * 64:2048 + t * 64 + 64],
                         c.logT[:, t * 128:(t + 1) * 128],
                         c.identF[0:64, 0:64])
        _inc(c, tt.last, "pe", 1)
    b.on("tensor", teL)
    pe_base = c.cnt["pe"] - 1

    def vtop(v):
        for half in range(2):
            v.wait_ge(c.sem["pe"], pe_base + 1 + half)
            for t8 in range(8):
                t = half * 8 + t8
                vsync(c, v, v.tensor_copy(
                    c.logL[:, :], c.PS[:, 2048 + t8 * 64:2048 + t8 * 64 + 64]))
                vsync(c, v, v.max(out=c.mxv[:, :], in_=c.logL[:, :]))
                vsync(c, v, v.max_index(out=c.mxi[:, :], in_max=c.mxv[:, :],
                                        in_values=c.logL[:, :]))
                vsync(c, v, v.tensor_copy(c.idxf[:, 0:2], c.mxi[:, 0:2]))
                vsync(c, v, v.tensor_scalar(
                    c.m1L[:, t * E:(t + 1) * E], c.iotaE[:, :],
                    c.idxf[:, 0:1], None, op0=OP.is_equal))
                vsync(c, v, v.tensor_scalar(
                    c.m2L[:, t * E:(t + 1) * E], c.iotaE[:, :],
                    c.idxf[:, 1:2], None, op0=OP.is_equal))
                vsync(c, v, v.tensor_sub(c.pd[:, t:t + 1], c.mxv[:, 1:2],
                                         c.mxv[:, 0:1]))
            if half == 0:
                _inc(c, v.nop(), "dv", 1)
        _inc(c, v.nop(), "dv", 1)
    b.on("vector", vtop)
    dv_half = c.cnt["dv"] - 1

    def teL2(tt):
        tt.wait_ge(c.sem["dv"], dv_half)
        for t in range(8, NT):
            tt.transpose(c.PS[:, 2048 + (t % 8) * 64:2048 + (t % 8) * 64 + 64],
                         c.logT[:, t * 128:(t + 1) * 128],
                         c.identF[0:64, 0:64])
        _inc(c, tt.last, "pe", 1)
    b.on("tensor", teL2)
    dvv = c.cnt["dv"]

    def sca(s):
        s.wait_ge(c.sem["dv"], dvv)
        _inc(c, s.activation(c.pd[:, :], c.pd[:, :], AF.Exp), "ac", 1)
    b.on("scalar", sca)
    acv = c.cnt["ac"]

    def vp(v):
        v.wait_ge(c.sem["ac"], acv)
        vsync(c, v, v.tensor_scalar(c.p1[:, :], c.pd[:, :], 1.0, None,
                                    op0=OP.add))
        vsync(c, v, v.reciprocal(c.p1[:, :], c.p1[:, :]))
        vsync(c, v, v.tensor_scalar(c.p2[:, :], c.p1[:, :], -1.0, 1.0,
                                    op0=OP.mult, op1=OP.add))
        _inc(c, v.nop(), "dv", 1)
    b.on("vector", vp)
    dvv = c.cnt["dv"]

    def teQ(tt):
        tt.wait_ge(c.sem["dv"], dvv)
        for t in range(NT):
            tt.transpose(c.PSb[0:64, 4096 + t * 128:4096 + t * 128 + 128],
                         c.m1L[:, t * E:(t + 1) * E], c.ident[:, :])
            tt.transpose(c.PSb[0:64, 6144 + t * 128:6144 + t * 128 + 128],
                         c.m2L[:, t * E:(t + 1) * E], c.ident[:, :])
        _inc(c, tt.last, "pe", 1)
    b.on("tensor", teQ)
    pev = c.cnt["pe"]

    R = c.rsc

    def veQ(v):
        v.wait_ge(c.sem["pe"], pev)
        v.wait_ge(c.sem["ld"], c.ffn_ld)
        vsync(c, v, v.tensor_copy(c.mT[0:64, :], c.PSb[0:64, 4096:6144]))
        vsync(c, v, v.tensor_tensor(c.mT[0:64, :], c.mT[0:64, :],
                                    c.PSb[0:64, 6144:8192], op=OP.add))
        vsync(c, v, v.tensor_tensor_scan(c.scanT[:, :], c.mT[0:64, :],
                                         c.mT[0:64, :], 0.0, op0=OP.add,
                                         op1=OP.bypass))
        U = c.rscU
        vsync(c, v, v.tensor_copy(R[:, 0:1], c.scanT[:, L - 1:L]))
        vsync(c, v, v.tensor_scalar(R[:, 1:2], R[:, 0:1], 127.0, None,
                                    op0=OP.add))
        vsync(c, v, v.tensor_copy(U[:, 2:3], R[:, 1:2]))
        vsync(c, v, v.tensor_scalar(U[:, 2:3], U[:, 2:3], 7, None,
                                    op0=OP.logical_shift_right))
        vsync(c, v, v.tensor_scalar(U[:, 2:3], U[:, 2:3], 7, None,
                                    op0=OP.logical_shift_left))
        vsync(c, v, v.tensor_copy(R[:, 3:4], U[:, 2:3]))
        _inc(c, v.nop(), "dv", 1)
    b.on("vector", veQ)
    dvv = c.cnt["dv"]

    def teP(tt):
        tt.wait_ge(c.sem["dv"], dvv)
        tt.matmul(c.PS[0:64, 2048:2049], c.sltri, R[:, 3:4],
                  start=True, stop=True)
        _inc(c, tt.last, "pe", 1)
    b.on("tensor", teP)
    pev = c.cnt["pe"]

    def veP(v):
        v.wait_ge(c.sem["pe"], pev)
        vsync(c, v, v.tensor_copy(R[:, 4:5], c.PS[0:64, 2048:2049]))
        vsync(c, v, v.tensor_scalar(c.scanT[:, :], c.scanT[:, :], -1.0, None,
                                    op0=OP.add))
        vsync(c, v, v.tensor_scalar(c.dstT[:, :], c.scanT[:, :],
                                    R[:, 4:5], None, op0=OP.add))
        vsync(c, v, v.tensor_scalar(R[:, 5:6], R[:, 4:5], 1.0 / 128, None,
                                    op0=OP.mult))
        vsync(c, v, v.tensor_scalar(R[:, 6:7], R[:, 3:4], 1.0 / 128, None,
                                    op0=OP.mult))
        vsync(c, v, v.tensor_tensor(R[:, 6:7], R[:, 5:6], R[:, 6:7],
                                    op=OP.add))
        vsync(c, v, v.tensor_scalar(c.indb[:, :], c.iotaB96, R[:, 5:6],
                                    None, op0=OP.is_ge))
        vsync(c, v, v.tensor_scalar(c.indb2[:, :], c.iotaB96, R[:, 6:7],
                                    None, op0=OP.is_lt))
        vsync(c, v, v.tensor_tensor(c.indb[:, :], c.indb[:, :],
                                    c.indb2[:, :], op=OP.mult))
        _inc(c, v.nop(), "dv", 1)
    b.on("vector", veP)
    dvv = c.cnt["dv"]

    def teW(tt):
        tt.wait_ge(c.sem["dv"], dvv)
        tt.matmul(c.PS[0:1, 2064:2064 + NBLK], c.eiotac, c.indb[:, :],
                  start=True, stop=True)
        _inc(c, tt.last, "pe", 1)
    b.on("tensor", teW)
    pev = c.cnt["pe"]

    def veW(v):
        v.wait_ge(c.sem["pe"], pev)
        _inc(c, v.tensor_scalar(c.ebr[:, :], c.PS[0:1, 2064:2064 + NBLK],
                                128.0, None, op0=OP.mult), "dv", 1)
    b.on("vector", veW)
    dvv = c.cnt["dv"]

    def teW2(tt):
        tt.wait_ge(c.sem["dv"], dvv)
        tt.matmul(c.PS[0:128, 2176:2176 + NBLK], c.onesrF, c.ebr[:, :],
                  start=True, stop=True)
        _inc(c, tt.last, "pe", 1)
    b.on("tensor", teW2)
    pev = c.cnt["pe"]

    def veW2(v):
        v.wait_ge(c.sem["pe"], pev)
        _inc(c, v.tensor_scalar(c.widxu[:, :], c.PS[:, 2176:2176 + NBLK],
                                c.iota128c, None, op0=OP.add), "dv", 1)
    b.on("vector", veW2)
    c.widx_dv = c.cnt["dv"]
    dvv = c.cnt["dv"]

    def teD(tt):
        tt.wait_ge(c.sem["dv"], dvv)
        for t in range(8):
            tt.transpose(c.PS[:, 2048 + t * 64:2048 + t * 64 + 64],
                         c.dstT[0:64, t * 128:(t + 1) * 128],
                         c.identF[0:64, 0:64])
        _inc(c, tt.last, "pe", 1)
    b.on("tensor", teD)
    pe_base = c.cnt["pe"] - 1

    def veD(v):
        for half in range(2):
            v.wait_ge(c.sem["pe"], pe_base + 1 + half)
            for t8 in range(8):
                t = half * 8 + t8
                vsync(c, v, v.tensor_copy(
                    c.dstL[:, :], c.PS[:, 2048 + t8 * 64:2048 + t8 * 64 + 64]))
                for mk, dk in ((c.m1L, c.dst1), (c.m2L, c.dst2)):
                    vsync(c, v, v.tensor_tensor(
                        c.logL[:, :], mk[:, t * E:(t + 1) * E], c.dstL[:, :],
                        op=OP.mult))
                    vsync(c, v, v.tensor_reduce(
                        c.idxf[:, 0:1], c.logL[:, :], axis=AX.X, op=OP.add))
                    vsync(c, v, v.tensor_copy(dk[t][:, :], c.idxf[:, 0:1]))
            if half == 0:
                _inc(c, v.nop(), "dv", 1)
        _inc(c, v.nop(), "dv", 1)
    b.on("vector", veD)
    dv_half2 = c.cnt["dv"] - 1

    def teD2(tt):
        tt.wait_ge(c.sem["dv"], dv_half2)
        for t in range(8, NT):
            tt.transpose(c.PS[:, 2048 + (t % 8) * 64:2048 + (t % 8) * 64 + 64],
                         c.dstT[0:64, t * 128:(t + 1) * 128],
                         c.identF[0:64, 0:64])
        _inc(c, tt.last, "pe", 1)
    b.on("tensor", teD2)


def emit_ffn(b):
    nc, c = b.nc, b.c

    def gcl(g):
        g.wait_ge(c.sem["pe"], c.cnt["pe"])
        g.wait_ge(c.sem["dv"], c.cnt["dv"])
        for dst, key in ((c.sltri, "sltri"), (c.iotaB96, "iotaB"),
                         (c.eiotac, "eiota"), (c.onesrF, "onesf"),
                         (c.iota128c, "iota128")):
            _inc(c, g.dma_start(out=dst, in_=c.i[key]), "ld", 16)
    b.on("gpsimd", gcl)
    c.ffn_ld = c.cnt["ld"]
    emit_ln(b)
    b.barrier()
    emit_router(b, D, None, logits_ready=False)
    b.barrier()

    dvv = c.cnt["dv"]
    for t in range(NT):
        xn_dv = emit_xn_tile(b, t, war_sems=(("ig", c.cnt["ig"]), ("dv", dvv)))
        par = t % 2

        def gsc(g, _t=t, _p=par, _x=xn_dv):
            g.wait_ge(c.sem["dv"], _x)
            for dk in (c.dst1, c.dst2):
                g.wait_ge(c.sem["ig"], c.cnt["ig"])
                _inc(c, g.indirect_dma_start(
                    out=c.d_xg_up.ap(),
                    out_offset=IndirectOffsetOnAxis(ap=dk[_t][:, :], axis=0),
                    in_=c.xnb[:, _p * D:(_p + 1) * D], in_offset=None),
                    "ig", 16)
        b.on("gpsimd", gsc)
    igv_disp = c.cnt["ig"]

    for grp in range(NBLK // 4):
        half = grp % 2
        pe_war = c.up_rb_war[half]

        rsem = "r%d" % half

        def mk_srd(c0, c1):
            def srd(s, _g=grp, _h=half, _pw=pe_war, _rs=rsem, _c0=c0, _c1=c1):
                s.wait_ge(c.sem["ig"], igv_disp)
                s.wait_ge(c.sem["pe"], _pw)
                for ch in range(_c0, _c1):
                    _inc(c, s.dma_start(
                        out=c.ar_a[:, _h * 4096 + ch * 512:
                                   _h * 4096 + ch * 512 + 512],
                        in_=c.d_xg_up.ap()[_g * 512:(_g + 1) * 512,
                                           ch * 128:(ch + 1) * 128],
                        transpose=True), _rs, 16)
            return srd
        b.on("sync", mk_srd(0, 4))
        b.on("scalar", mk_srd(4, 8))
        rbv = (rsem, c.cnt[rsem])
        for bi in range(4):
            emit_up_block(b, grp * 4 + bi, half, bi, rbv)
        c.up_rb_war[half] = c.cnt["pe"]
    b.barrier()

    emit_combine_up(b)
    b.barrier()
    emit_router(b, DH, None, logits_ready=True)
    b.barrier()
    emit_down_dispatch(b)
    igv2 = c.cnt["ig"]
    b.barrier()

    for grp in range(NBLK // 2):
        rsem = "r%d" % (grp % 2)
        pe_war = c.cnt["pe"]

        def mk_srd2(c0, c1):
            def srd2(s, _g=grp, _pw=pe_war, _rs=rsem, _c0=c0, _c1=c1):
                s.wait_ge(c.sem["ig"], igv2)
                s.wait_ge(c.sem["pe"], _pw)
                for ch in range(_c0, _c1):
                    _inc(c, s.dma_start(
                        out=c.ar_a[:, ch * 256:ch * 256 + 256],
                        in_=c.d_xg_dn.ap()[_g * 256:(_g + 1) * 256,
                                           ch * 128:(ch + 1) * 128],
                        transpose=True), _rs, 16)
            return srd2
        b.on("sync", mk_srd2(0, 16))
        b.on("scalar", mk_srd2(16, 32))
        rbv = (rsem, c.cnt[rsem])
        for bi in range(2):
            emit_dn_block(b, grp * 2 + bi, 0, bi, rbv)
        c.dn_rb_war[grp % 2] = c.cnt["pe"]
    b.barrier()
    emit_combine_dn(b)


def emit_up_block(b, blkid, buf, bi, rbv):
    nc, c = b.nc, b.c
    e = blkid
    psA = c.PS[:, 0:2048]
    psBb = c.PSb[:, 4096:6144]
    wbuf = blkid % 2

    gsem = "g%d" % (blkid % 2)

    def gw(g, _e=e, _w=wbuf, _pw=c.up_w_war[wbuf], _gs=gsem):
        g.wait_ge(c.sem["pe"], _pw)
        g.wait_ge(c.sem["dv"], c.widx_dv)
        _inc(c, g.indirect_dma_start(
            out=c.aupbd_s[:, _w * 320:_w * 320 + 256], out_offset=None,
            in_=c.i["aupbd"],
            in_offset=IndirectOffsetOnAxis(ap=c.widxu[:, _e:_e + 1], axis=0)),
            _gs, 16)
        _inc(c, g.indirect_dma_start(
            out=c.aupbd_s[:, _w * 320 + 256:_w * 320 + 320], out_offset=None,
            in_=c.i["bupg"],
            in_offset=IndirectOffsetOnAxis(ap=c.widxu[:, _e:_e + 1], axis=0)),
            _gs, 16)
    b.on("gpsimd", gw)
    gpv = c.cnt[gsem]

    dv_bef, ac_bef = c.cnt["dv"], c.cnt["ac"]

    def teA(t, _buf=buf, _bi=bi, _e=e, _w=wbuf, _gs=gsem):
        t.wait_ge(c.sem[rbv[0]], rbv[1])
        t.wait_ge(c.sem[_gs], gpv)
        t.wait_ge(c.sem["dv"], dv_bef)
        t.wait_ge(c.sem["ac"], ac_bef)
        for i in range(32):
            u, a = i & 3, i >> 2
            lhsT = c.ar_a[32 * u:32 * u + 32,
                          _buf * 4096 + a * 512 + _bi * 128:
                          _buf * 4096 + a * 512 + _bi * 128 + 128]
            rhs = c.aupbd_s[32 * u:32 * u + 32,
                            _w * 320 + 256:_w * 320 + 320]
            out = psA[:, u * 512 + a * 64:u * 512 + a * 64 + 64]
            t.matmul(out, lhsT, rhs, start=True, stop=True,
                     tile_position=(32 * u, 0))
        _inc(c, t.last, "pe", 1)
    b.on("tensor", teA)
    pev = c.cnt["pe"]
    sbuf_i = blkid % 2

    def veA(v, _s=sbuf_i):
        v.wait_ge(c.sem["pe"], pev)
        src = fancy(psA, [[1, 64], [64, 8], [512, 4]])
        dst = fancy(c.sa[:, _s * 2048:(_s + 1) * 2048],
                    [[32, 64], [4, 8], [1, 4]])
        _inc(c, v.tensor_copy(dst, src), "dv", 1)
    b.on("vector", veA)
    dvv = c.cnt["dv"]

    def teB(t, _s=sbuf_i):
        t.wait_ge(c.sem["dv"], dvv)
        src = c.sa[:, _s * 2048:(_s + 1) * 2048]
        for q in range(16):
            t.transpose(psBb[:, q * 128:(q + 1) * 128],
                        src[:, q * 128:(q + 1) * 128], c.ident[:, :])
        _inc(c, t.last, "pe", 1)
    b.on("tensor", teB)
    pev = c.cnt["pe"]

    def veB(v, _s=sbuf_i):
        v.wait_ge(c.sem["pe"], pev)
        _inc(c, v.tensor_copy(c.br[:, _s * 2048:(_s + 1) * 2048], psBb),
             "dv", 1)
    b.on("vector", veB)
    dvv = c.cnt["dv"]

    for rnd in range(2):
        def teC(t, _s=sbuf_i, _w=wbuf, _r=rnd, _gs=gsem):
            t.wait_ge(c.sem["dv"], dvv if _r == 0 else c.cnt["dv"])
            t.wait_ge(c.sem[_gs], gpv)
            for q8 in range(8):
                q = _r * 8 + q8
                lhsT = c.br[:, _s * 2048 + q * 128:_s * 2048 + q * 128 + 128]
                rhs = c.aupbd_s[:, _w * 320:_w * 320 + 256]
                t.matmul(psA[:, q8 * 256:(q8 + 1) * 256], lhsT, rhs,
                         start=True, stop=True)
            _inc(c, t.last, "pe", 1)
        b.on("tensor", teC)
        pev2 = c.cnt["pe"]

        st_war = c.up_st_war[blkid % 2]

        def veC(v, _r=rnd, _sw=st_war):
            v.wait_ge(c.sem["pe"], pev2)
            if _r == 0:
                v.wait_ge(c.sem["s%d" % (blkid % 2)], _sw)
            src = fancy(psA, [[256, 8], [64, 4], [1, 64]])
            ybase = c.yp[:, (blkid % 2) * DH:(blkid % 2) * DH + DH]
            dst = fancy(ybase, [[4, 8], [1, 4], [64, 64]], 32 * _r)
            _inc(c, v.tensor_copy(dst, src), "dv", 1)
        b.on("vector", veC)

    dvv2 = c.cnt["dv"]

    def veS(v):
        v.wait_ge(c.sem["dv"], dvv2)
        y = c.yp[:, (blkid % 2) * DH:(blkid % 2) * DH + DH]
        _inc(c, v.tensor_scalar(y, y, c.scupc[:, :], None, op0=OP.mult),
             "dv", 1)
    b.on("vector", veS)
    dvv3 = c.cnt["dv"]

    def sst(s, _b=blkid):
        s.wait_ge(c.sem["dv"], dvv3)
        _inc(c, s.dma_start(out=c.d_yp_up.ap()[_b * 128:(_b + 1) * 128, :],
                            in_=c.yp[:, (_b % 2) * DH:(_b % 2) * DH + DH]),
             "s%d" % (_b % 2), 16)
    b.on("scalar" if blkid % 2 == 0 else "sync", sst)
    c.up_st_war[blkid % 2] = c.cnt["s%d" % (blkid % 2)]
    c.up_w_war[blkid % 2] = c.cnt["pe"]


def emit_combine_up(b):
    nc, c = b.nc, b.c
    s0_all, s1_all = c.cnt["s0"], c.cnt["s1"]

    rb_pre = c.cnt["rb"]

    def gb_(g):
        g.wait_ge(c.sem["pe"], c.cnt["pe"])
        _inc(c, g.dma_start(out=c.bupr16, in_=c.i["biasup"]), "gp", 16)
    b.on("gpsimd", gb_)
    gpv = c.cnt["gp"]

    def tb_(t):
        t.wait_ge(c.sem["gp"], gpv)
        for kk in range(8):
            t.matmul(c.PS[:, kk * 512:(kk + 1) * 512], c.onesr[:, :],
                     c.bupr16[:, kk * 512:(kk + 1) * 512],
                     start=True, stop=True)
        _inc(c, t.last, "pe", 1)
    b.on("tensor", tb_)
    pev_b = c.cnt["pe"]

    def vb_(v):
        v.wait_ge(c.sem["pe"], pev_b)
        _inc(c, v.tensor_copy(c.bupbc[:, :], c.PS[:, 0:DH]), "dv", 1)
    b.on("vector", vb_)

    for t in range(NT):
        def gg(g, _t=t):
            g.wait_ge(c.sem["s0"], s0_all)
            g.wait_ge(c.sem["s1"], s1_all)
            g.wait_ge(c.sem["dv"], c.cnt["dv"])
            _inc(c, g.indirect_dma_start(
                out=c.gb1[:, :], out_offset=None, in_=c.d_yp_up.ap(),
                in_offset=IndirectOffsetOnAxis(ap=c.dst1[_t][:, :], axis=0)),
                "ig", 16)
            _inc(c, g.indirect_dma_start(
                out=c.gb2[:, :], out_offset=None, in_=c.d_yp_up.ap(),
                in_offset=IndirectOffsetOnAxis(ap=c.dst2[_t][:, :], axis=0)),
                "ig", 16)
        b.on("gpsimd", gg)
        igv = c.cnt["ig"]

        def vc(v, _t=t):
            v.wait_ge(c.sem["ig"], igv)
            vsync(c, v, v.tensor_scalar(c.t1f[:, :], c.gb1[:, :],
                                        c.p1[:, _t:_t + 1], None, op0=OP.mult))
            vsync(c, v, v.scalar_tensor_tensor(
                c.t2f[:, :], c.gb2[:, :], c.p2[:, _t:_t + 1], c.t1f[:, :],
                op0=OP.mult, op1=OP.add))
            vsync(c, v, v.tensor_tensor(c.t1f[:, :], c.t2f[:, :],
                                        c.bupbc[:, :], op=OP.add))
            _inc(c, v.nop(), "dv", 1)
        b.on("vector", vc)
        dvv = c.cnt["dv"]
        hb = t % 2

        def sg(s, _hb=hb, _sw=c.cu_st_war[hb], _pw=c.cu_pe_war[hb]):
            s.wait_ge(c.sem["dv"], dvv)
            s.wait_ge(c.sem["h%d" % hb], _sw)
            s.wait_ge(c.sem["pe"], _pw)
            _inc(c, s.activation(c.t2f[:, :], c.t1f[:, :], AF.Gelu), "ac", 1)
            s.wait_ge(c.sem["ac"], c.cnt["ac"])
            _inc(c, s.copy(c.hbuf[:, _hb * DH:(_hb + 1) * DH], c.t2f[:, :]),
                 "ac", 1)
        b.on("scalar", sg)
        acv = c.cnt["ac"]

        def shw(s, _t=t, _hb=hb):
            s.wait_ge(c.sem["ac"], acv)
            _inc(c, s.dma_start(out=c.d_h.ap()[_t * 128:(_t + 1) * 128, :],
                                in_=c.hbuf[:, _hb * DH:(_hb + 1) * DH]),
                 "h%d" % _hb, 16)
        b.on("scalar", shw)
        c.cu_st_war[hb] = c.cnt["h%d" % hb]

        pe_prev_teM = c.cnt["pe"]
        for half in range(2):
            def vlo(v, _hb=hb, _half=half, _pp=pe_prev_teM):
                v.wait_ge(c.sem["ac"], acv)
                v.wait_ge(c.sem["pe"], c.cnt["pe"])
                _inc(c, v.tensor_tensor(
                    c.hTc[:, 2048:4096],
                    c.t2f[:, _half * 2048:(_half + 1) * 2048],
                    c.hbuf[:, _hb * DH + _half * 2048:
                           _hb * DH + (_half + 1) * 2048],
                    op=OP.subtract), "dv", 1)
            b.on("vector", vlo)
            dv_lo = c.cnt["dv"]

            def teL(tt, _hb=hb, _half=half):
                tt.wait_ge(c.sem["ac"], acv)
                tt.wait_ge(c.sem["dv"], dv_lo)
                for ch16 in range(16):
                    ch = _half * 16 + ch16
                    tt.transpose(
                        c.PSb[:, 4096 + ch16 * 128:4096 + ch16 * 128 + 128],
                        c.hbuf[:, _hb * DH + ch * 128:_hb * DH + ch * 128 + 128],
                        c.ident[:, :])
                for ch16 in range(16):
                    tt.transpose(
                        c.PSb[:, 6144 + ch16 * 128:6144 + ch16 * 128 + 128],
                        c.hTc[:, 2048 + ch16 * 128:2048 + ch16 * 128 + 128],
                        c.ident[:, :])
                _inc(c, tt.last, "pe", 1)
            b.on("tensor", teL)
            pev = c.cnt["pe"]

            def veL(v):
                v.wait_ge(c.sem["pe"], pev)
                v.tensor_copy(c.hTc[:, 0:2048], c.PSb[:, 4096:6144])
                _inc(c, v.tensor_copy(c.hTc[:, 2048:4096],
                                      c.PSb[:, 6144:8192]), "dv", 1)
            b.on("vector", veL)
            dvv2 = c.cnt["dv"]

            def teM(tt, _t=t, _half=half):
                tt.wait_ge(c.sem["dv"], dvv2)
                out128 = c.PS[0:128, _t * 128:(_t + 1) * 128]
                out64 = c.PS[0:64, _t * 128:(_t + 1) * 128]
                for ch16 in range(16):
                    ch = _half * 16 + ch16
                    last_mm = (_half == 1 and ch16 == 15)
                    if not last_mm:
                        tt.matmul(out128,
                                  c.wdnhl[:, ch * 128:(ch + 1) * 128],
                                  c.hTc[:, ch16 * 128:(ch16 + 1) * 128],
                                  start=(_half == 0 and ch16 == 0), stop=False)
                        tt.matmul(out64,
                                  c.wdnhl[:, ch * 128:ch * 128 + 64],
                                  c.hTc[:, 2048 + ch16 * 128:
                                        2048 + (ch16 + 1) * 128],
                                  start=False, stop=False)
                    else:
                        tt.matmul(out64,
                                  c.wdnhl[:, ch * 128:ch * 128 + 64],
                                  c.hTc[:, 2048 + ch16 * 128:
                                        2048 + (ch16 + 1) * 128],
                                  start=False, stop=False)
                        tt.matmul(out128,
                                  c.wdnhl[:, ch * 128:(ch + 1) * 128],
                                  c.hTc[:, ch16 * 128:(ch16 + 1) * 128],
                                  start=False, stop=True)
                _inc(c, tt.last, "pe", 1)
            b.on("tensor", teM)
        c.cu_pe_war[hb] = c.cnt["pe"]

    pev = c.cnt["pe"]

    def vf(v):
        v.wait_ge(c.sem["pe"], pev)
        vsync(c, v, v.tensor_copy(c.logT[:, :], c.PS[64:128, 0:L]))
        _inc(c, v.tensor_tensor(c.logT[:, :], c.logT[:, :],
                                c.PS[0:64, 0:L], op=OP.add), "dv", 1)
    b.on("vector", vf)


def emit_down_dispatch(b):
    nc, c = b.nc, b.c
    dvv = c.cnt["dv"]
    h0_dh, h1_dh = c.cnt["h0"], c.cnt["h1"]
    for t in range(NT):
        hb = t % 2
        ig_prev = c.cnt["ig"]

        def sr(s, _t=t, _hb=hb):
            s.wait_ge(c.sem["h0"], h0_dh)
            s.wait_ge(c.sem["h1"], h1_dh)
            s.wait_ge(c.sem["dv"], dvv)
            s.wait_ge(c.sem["ig"], ig_prev)
            _inc(c, s.dma_start(out=c.hbuf[:, _hb * DH:(_hb + 1) * DH],
                                in_=c.d_h.ap()[_t * 128:(_t + 1) * 128, :]),
                 "rb", 16)
        b.on("sync", sr)
        rbv = c.cnt["rb"]

        def gs(g, _t=t, _hb=hb):
            g.wait_ge(c.sem["rb"], rbv)
            for dk in (c.dst1, c.dst2):
                g.wait_ge(c.sem["ig"], c.cnt["ig"])
                _inc(c, g.indirect_dma_start(
                    out=c.d_xg_dn.ap(),
                    out_offset=IndirectOffsetOnAxis(ap=dk[_t][:, :], axis=0),
                    in_=c.hbuf[:, _hb * DH:(_hb + 1) * DH], in_offset=None),
                    "ig", 16)
        b.on("gpsimd", gs)


def emit_dn_block(b, blkid, buf, bi, rbv):
    nc, c = b.nc, b.c
    e = blkid
    psA = c.PS[:, 0:2048]
    psBb = c.PSb[:, 4096:6144]
    psC = c.PS[:, 3072:4096]
    wbuf = blkid % 2

    gsem = "g%d" % (blkid % 2)

    def gw(g, _e=e, _w=wbuf, _pw=c.dn_w_war[wbuf], _gs=gsem):
        g.wait_ge(c.sem["pe"], _pw)
        g.wait_ge(c.sem["dv"], c.widx_dv)
        _inc(c, g.indirect_dma_start(
            out=c.adnbd_s[:, _w * 320:_w * 320 + 256], out_offset=None,
            in_=c.i["adnbd"],
            in_offset=IndirectOffsetOnAxis(ap=c.widxu[:, _e:_e + 1], axis=0)),
            _gs, 16)
        _inc(c, g.indirect_dma_start(
            out=c.adnbd_s[:, _w * 320 + 256:_w * 320 + 320], out_offset=None,
            in_=c.i["bdng"],
            in_offset=IndirectOffsetOnAxis(ap=c.widxu[:, _e:_e + 1], axis=0)),
            _gs, 16)
    b.on("gpsimd", gw)
    gpv = c.cnt[gsem]
    dv_bef, ac_bef = c.cnt["dv"], c.cnt["ac"]

    def teA(t, _buf=buf, _bi=bi, _e=e, _w=wbuf, _gs=gsem):
        t.wait_ge(c.sem[rbv[0]], rbv[1])
        t.wait_ge(c.sem[_gs], gpv)
        t.wait_ge(c.sem["dv"], dv_bef)
        t.wait_ge(c.sem["ac"], ac_bef)
        for fp in range(32):
            lhsT = c.ar_a[:, fp * 256 + _bi * 128:fp * 256 + _bi * 128 + 128]
            rhs = c.adnbd_s[:, _w * 320 + 256:_w * 320 + 320]
            out = psA[:, (fp & 3) * 512 + (fp >> 2) * 64:
                      (fp & 3) * 512 + (fp >> 2) * 64 + 64]
            t.matmul(out, lhsT, rhs, start=True, stop=True)
        _inc(c, t.last, "pe", 1)
    b.on("tensor", teA)
    pev = c.cnt["pe"]
    sbuf_i = blkid % 2

    def veA(v, _s=sbuf_i):
        v.wait_ge(c.sem["pe"], pev)
        for h in range(2):
            for d in range(2):
                src = fancy(psA, [[1, 32], [64, 4], [512, 4]],
                            256 * h + 32 * d)
                dst = fancy(c.sa2[:, _s * 2048:(_s + 1) * 2048],
                            [[32, 32], [8, 4], [2, 4]], 1024 * h + d)
                last = (h == 1 and d == 1)
                if last:
                    _inc(c, v.tensor_copy(dst, src), "dv", 1)
                else:
                    v.tensor_copy(dst, src)
    b.on("vector", veA)
    dvv = c.cnt["dv"]

    def teB(t, _s=sbuf_i):
        t.wait_ge(c.sem["dv"], dvv)
        src = c.sa2[:, _s * 2048:(_s + 1) * 2048]
        for u in range(16):
            t.transpose(psBb[:, u * 128:(u + 1) * 128],
                        src[:, u * 128:(u + 1) * 128], c.ident[:, :])
        _inc(c, t.last, "pe", 1)
    b.on("tensor", teB)
    pev = c.cnt["pe"]

    def veB(v, _s=sbuf_i):
        v.wait_ge(c.sem["pe"], pev)
        _inc(c, v.tensor_copy(c.br2[:, _s * 2048:(_s + 1) * 2048], psBb),
             "dv", 1)
    b.on("vector", veB)
    dvv = c.cnt["dv"]

    def teC(t, _s=sbuf_i, _w=wbuf, _gs=gsem):
        t.wait_ge(c.sem["dv"], dvv)
        t.wait_ge(c.sem[_gs], gpv)
        for q in range(8):
            for h in range(2):
                lhsT = c.br2[:, _s * 2048 + (h * 8 + q) * 128:
                             _s * 2048 + (h * 8 + q) * 128 + 128]
                rhs = c.adnbd_s[:, _w * 320 + h * 128:_w * 320 + h * 128 + 128]
                t.matmul(psC[:, q * 128:(q + 1) * 128], lhsT, rhs,
                         start=(h == 0), stop=(h == 1))
        _inc(c, t.last, "pe", 1)
    b.on("tensor", teC)
    pev = c.cnt["pe"]

    st_war = c.dn_st_war[blkid % 2]

    def veC(v, _b=blkid, _sw=st_war):
        v.wait_ge(c.sem["pe"], pev)
        v.wait_ge(c.sem["s%d" % (blkid % 2)], _sw)
        src = fancy(psC, [[128, 8], [32, 4], [1, 32]])
        ybase = c.yp2[:, (_b % 2) * D:(_b % 2) * D + D]
        dst = fancy(ybase, [[4, 8], [1, 4], [32, 32]])
        vsync(c, v, v.tensor_copy(dst, src))
        y = c.yp2[:, (_b % 2) * D:(_b % 2) * D + D]
        _inc(c, v.tensor_scalar(y, y, c.scdnc[:, :], None, op0=OP.mult),
             "dv", 1)
    b.on("vector", veC)
    dvv2 = c.cnt["dv"]

    def sst(s, _b=blkid):
        s.wait_ge(c.sem["dv"], dvv2)
        _inc(c, s.dma_start(out=c.d_yp_dn.ap()[_b * 128:(_b + 1) * 128, :],
                            in_=c.yp2[:, (_b % 2) * D:(_b % 2) * D + D]),
             "s%d" % (_b % 2), 16)
    b.on("scalar" if blkid % 2 == 0 else "sync", sst)
    c.dn_st_war[blkid % 2] = c.cnt["s%d" % (blkid % 2)]
    c.dn_w_war[blkid % 2] = c.cnt["pe"]


def emit_combine_dn(b):
    nc, c = b.nc, b.c
    s0_all, s1_all = c.cnt["s0"], c.cnt["s1"]
    for t in range(NT):
        def gg(g, _t=t):
            g.wait_ge(c.sem["s0"], s0_all)
            g.wait_ge(c.sem["s1"], s1_all)
            g.wait_ge(c.sem["dv"], c.cnt["dv"])
            _inc(c, g.indirect_dma_start(
                out=c.y1b[:, :], out_offset=None, in_=c.d_yp_dn.ap(),
                in_offset=IndirectOffsetOnAxis(ap=c.dst1[_t][:, :], axis=0)),
                "ig", 16)
            _inc(c, g.indirect_dma_start(
                out=c.y2b[:, :], out_offset=None, in_=c.d_yp_dn.ap(),
                in_offset=IndirectOffsetOnAxis(ap=c.dst2[_t][:, :], axis=0)),
                "ig", 16)
        b.on("gpsimd", gg)
        igv = c.cnt["ig"]

        par = t % 2
        st_pre = c.cnt["st"]

        def sx2(s, _t=t, _p=par):
            s.wait_ge(c.sem["st"], st_pre)
            _inc(c, s.dma_start(out=c.xb[:, _p * D:(_p + 1) * D],
                                in_=c.d_xr.ap()[_t * 128:(_t + 1) * 128, :]),
                 "rb", 16)
        b.on("sync", sx2)
        rbv2 = c.cnt["rb"]

        def vc(v, _t=t, _p=par):
            v.wait_ge(c.sem["ig"], igv)
            v.wait_ge(c.sem["rb"], rbv2)
            vsync(c, v, v.tensor_scalar(c.t3f[:, :], c.y1b[:, :],
                                        c.p1[:, _t:_t + 1], None, op0=OP.mult))
            vsync(c, v, v.scalar_tensor_tensor(
                c.t3f[:, :], c.y2b[:, :], c.p2[:, _t:_t + 1], c.t3f[:, :],
                op0=OP.mult, op1=OP.add))
            vsync(c, v, v.scalar_tensor_tensor(
                c.t3f[:, :], c.t3f[:, :], 0.5, c.bdnbc[:, :],
                op0=OP.mult, op1=OP.add))
            vsync(c, v, v.tensor_tensor(
                c.xb[:, _p * D:(_p + 1) * D], c.xb[:, _p * D:(_p + 1) * D],
                c.t3f[:, :], op=OP.add))
            _inc(c, v.nop(), "dv", 1)
        b.on("vector", vc)
        dvx = c.cnt["dv"]

        def sx3(s, _t=t, _p=par):
            s.wait_ge(c.sem["dv"], dvx)
            _inc(c, s.dma_start(out=c.d_xr.ap()[_t * 128:(_t + 1) * 128, :],
                                in_=c.xb[:, _p * D:(_p + 1) * D]), "st", 16)
        b.on("sync", sx3)


def emit_attn(b):
    nc, c = b.nc, b.c

    def vtp(v):
        v.wait_ge(c.sem["pe"], c.cnt["pe"])
        v.wait_ge(c.sem["ig"], c.cnt["ig"])
        v.memset(c.tpad[:, :], 0.0)
    b.on("vector", vtp)
    emit_ln(b)
    b.barrier()

    for t in range(NT):
        xn_dv = emit_xn_tile(b, t, war_sems=(("pe", c.cnt["pe"]),))
        par = t % 2

        def teT(tt, _t=t, _p=par, _x=xn_dv):
            tt.wait_ge(c.sem["dv"], _x)
            if _t > 0:
                tt.wait_ge(c.sem["dv"], c.cnt["dv"])
            for ch in range(8):
                tt.transpose(c.PSb[:, 4096 + ch * 128:4096 + ch * 128 + 128],
                             c.xnb[:, _p * D + ch * 128:_p * D + ch * 128 + 128],
                             c.ident[:, :])
            _inc(c, tt.last, "pe", 1)
        b.on("tensor", teT)
        pev = c.cnt["pe"]

        def veT(v, _t=t):
            v.wait_ge(c.sem["pe"], pev)
            last = None
            for ch in range(8):
                last = v.tensor_copy(
                    c.xnT[:, ch * L + _t * 128:ch * L + _t * 128 + 128],
                    c.PSb[:, 4096 + ch * 128:4096 + ch * 128 + 128])
            _inc(c, last, "dv", 1)
        b.on("vector", veT)
    b.barrier()

    emit_aug(b)
    b.barrier()

    for grp in range(4):
        emit_proj_group(b, grp)
        b.barrier()
        for j in range(4):
            emit_head(b, grp, j)
        b.barrier()

    emit_oproj(b)


def emit_aug(b):
    nc, c = b.nc, b.c
    for ch in range(8):
        pe_prev = c.cnt["pe"]
        gsem = "g%d" % (ch % 2)

        def gw(g, _ch=ch, _pp=pe_prev, _gs=gsem):
            if _ch >= 2:
                g.wait_ge(c.sem["pe"], _pp)
            _inc(c, g.dma_start(
                out=c.wstr[:, (_ch % 2) * 1024:(_ch % 2) * 1024 + 1024],
                in_=c.i["wq_r"][_ch * 128:(_ch + 1) * 128, :]), _gs, 16)
        b.on("gpsimd", gw)
        gpv = c.cnt[gsem]

        def tw(t, _ch=ch, _gs=gsem):
            t.wait_ge(c.sem[_gs], gpv)
            for half in range(2):
                t.matmul(c.PS[0:16, half * 512:(half + 1) * 512],
                         c.ddbd[:, _ch * 16:(_ch + 1) * 16],
                         c.wstr[:, (_ch % 2) * 1024 + half * 512:
                                (_ch % 2) * 1024 + half * 512 + 512],
                         start=(_ch == 0), stop=(_ch == 7),
                         skip_group_check=True)
            _inc(c, t.last, "pe", 1)
        b.on("tensor", tw)
    pev = c.cnt["pe"]

    def ve(v):
        v.wait_ge(c.sem["pe"], pev)
        _inc(c, v.tensor_copy(c.waug[:, :], c.PS[0:16, 0:1024]), "dv", 1)
    b.on("vector", ve)
    dvv = c.cnt["dv"]

    def tT(t):
        t.wait_ge(c.sem["dv"], dvv)
        for ch in range(8):
            t.transpose(c.PSb[:, 4096 + ch * 16:4096 + ch * 16 + 16],
                        c.waug[:, ch * 128:(ch + 1) * 128],
                        c.ident[0:16, 0:16])
        _inc(c, t.last, "pe", 1)
    b.on("tensor", tT)
    pev = c.cnt["pe"]

    def ve2(v):
        v.wait_ge(c.sem["pe"], pev)
        _inc(c, v.tensor_copy(c.waugT[:, :], c.PSb[:, 4096:4224]), "dv", 1)
    b.on("vector", ve2)
    dvv = c.cnt["dv"]

    def tA(t):
        t.wait_ge(c.sem["dv"], dvv)
        for lc in range(4):
            for ch in range(8):
                t.matmul(c.PS[0:16, lc * 512:(lc + 1) * 512],
                         c.waugT[:, ch * 16:(ch + 1) * 16],
                         c.xnT[:, ch * L + lc * 512:ch * L + lc * 512 + 512],
                         start=(ch == 0), stop=(ch == 7),
                         skip_group_check=True)
        _inc(c, t.last, "pe", 1)
    b.on("tensor", tA)
    pev = c.cnt["pe"]

    def vA(v):
        v.wait_ge(c.sem["pe"], pev)
        _inc(c, v.tensor_scalar(c.augT[:, :], c.PS[0:16, 0:L],
                                c.ddbq[:, :], None, op0=OP.add), "dv", 1)
    b.on("vector", vA)
    c.aug_dv = c.cnt["dv"]


def emit_proj_group(b, grp):
    nc, c = b.nc, b.c
    col0 = 256 * grp
    wvsl = c.ar_a[:, 0:2048]

    def gw(g):
        g.wait_ge(c.sem["pe"], c.cnt["pe"])
        _inc(c, g.dma_start(
            out=c.wstr[:, 0:2048],
            in_=fancy(c.i["wq_s"], [[1024, 8], [1, 256]], col0)), "gp", 16)
        _inc(c, g.dma_start(
            out=c.wksl,
            in_=fancy(c.i["wk_s"], [[1024, 8], [1, 256]], col0)), "gp", 16)
        _inc(c, g.dma_start(
            out=wvsl,
            in_=fancy(c.i["wv_s"], [[1024, 8], [1, 256]], col0)), "gp", 16)
    b.on("gpsimd", gw)
    gpv = c.cnt["gp"]

    for wslab, slabs, bias in ((c.wstr[:, 0:2048], c.qsl, c.bq),
                               (c.wksl, c.ksl, c.bk)):
        ac_bef = c.cnt["ac"]

        def tp(t, _w=wslab, _ab=ac_bef):
            t.wait_ge(c.sem["gp"], gpv)
            t.wait_ge(c.sem["ac"], _ab)
            for dqt in range(2):
                for lc in range(4):
                    reg = dqt * 4 + lc
                    for ch in range(8):
                        t.matmul(c.PS[:, reg * 512:(reg + 1) * 512],
                                 _w[:, ch * 256 + dqt * 128:
                                    ch * 256 + dqt * 128 + 128],
                                 c.xnT[:, ch * L + lc * 512:
                                       ch * L + lc * 512 + 512],
                                 start=(ch == 0), stop=(ch == 7))
            _inc(c, t.last, "pe", 1)
        b.on("tensor", tp)
        pev = c.cnt["pe"]

        def vev(v, _sl=slabs, _bias=bias):
            v.wait_ge(c.sem["pe"], pev)
            last = None
            for dqt in range(2):
                for lc in range(4):
                    reg = dqt * 4 + lc
                    for par in range(2):
                        j = 2 * dqt + par
                        last = v.tensor_scalar(
                            _sl[j][0:64, lc * 512:(lc + 1) * 512],
                            c.PS[64 * par:64 * par + 64,
                                 reg * 512:(reg + 1) * 512],
                            _bias[64 * par:64 * par + 64,
                                  2 * grp + dqt:2 * grp + dqt + 1],
                            None, op0=OP.add)
            _inc(c, last, "ac", 1)
        b.on("vector", vev)

    dv_bef = c.cnt["dv"]

    ac_aux = c.cnt["ac"]

    def vaux(v):
        v.wait_ge(c.sem["ac"], ac_aux)
        last = None
        for j in range(4):
            last = v.memset(c.ksl[j][64:65, :], 1.0)
        _inc(c, last, "dv", 1)
    b.on("vector", vaux)

    def saux(s):
        s.wait_ge(c.sem["ac"], ac_aux)
        s.wait_ge(c.sem["dv"], c.aug_dv)
        for j in range(4):
            _inc(c, s.dma_start(
                out=c.qsl[j][64:65, :],
                in_=c.augT[4 * grp + j:4 * grp + j + 1, :]), "sk", 16)
    b.on("sync", saux)

    ac_bef2 = c.cnt["ac"]

    def vround(t, rt):
        reg = rt % 8
        for ch in range(8):
            t.matmul(c.PS[:, reg * 512:reg * 512 + 256],
                     c.xnT[:, ch * L + rt * 128:ch * L + rt * 128 + 128],
                     wvsl[:, ch * 256:(ch + 1) * 256],
                     start=(ch == 0), stop=False)
        t.matmul(c.PS[:, reg * 512:reg * 512 + 256],
                 c.onesr[0:1, 0:128],
                 c.bvr[0:1, col0:col0 + 256],
                 start=False, stop=True)

    def tv1(t):
        t.wait_ge(c.sem["gp"], gpv)
        t.wait_ge(c.sem["ac"], ac_bef2)
        t.wait_ge(c.sem["dv"], c.cnt["dv"])
        for rt in range(8):
            vround(t, rt)
        _inc(c, t.last, "pe", 1)
    b.on("tensor", tv1)
    pev1 = c.cnt["pe"]

    def vevh(v, half):
        for r8 in range(8):
            rt = half * 8 + r8
            src = fancy(c.PS[:, r8 * 512:r8 * 512 + 256],
                        [[64, 4], [1, 64]])
            dst = fancy(c.vfull[:, rt * 1040:(rt + 1) * 1040],
                        [[65, 4], [1, 64]], 65 * 4 * grp)
            last = v.tensor_copy(dst, src)
        _inc(c, last, "dv", 1)

    def vh0(v):
        v.wait_ge(c.sem["pe"], pev1)
        vevh(v, 0)
    b.on("vector", vh0)
    dv_h0 = c.cnt["dv"]

    def tv2(t):
        t.wait_ge(c.sem["dv"], dv_h0)
        for rt in range(8, NT):
            vround(t, rt)
        _inc(c, t.last, "pe", 1)
    b.on("tensor", tv2)
    pev2v = c.cnt["pe"]

    def vh1(v):
        v.wait_ge(c.sem["pe"], pev2v)
        vevh(v, 1)
    b.on("vector", vh1)

    if grp == 0:
        def vone(v):
            v.wait_ge(c.sem["dv"], c.cnt["dv"])
            last = None
            for rt in range(NT):
                ap = fancy(c.vfull[:, rt * 1040:(rt + 1) * 1040],
                           [[65, 16], [1, 1]], 64)
                last = v.memset(ap, 1.0)
            _inc(c, last, "dv", 1)
        b.on("vector", vone)


def emit_head(b, grp, j):
    nc, c = b.nc, b.c
    h = 4 * grp + j

    for lt in range(NT):
        buf = lt % 2
        treg = 128 * buf
        sk_bef = c.cnt["sk"]

        dv_pre0 = c.cnt["dv"]

        def tt_mm(t, _lt=lt, _buf=buf, _tr=treg):
            t.wait_ge(c.sem["vv"], c.cnt["vv"])
            if _lt == 0:
                t.wait_ge(c.sem["dv"], dv_pre0)
            t.matmul(c.PS[:, _tr:_tr + 73],
                     c.qsl[j][0:64, _lt * 128:(_lt + 1) * 128],
                     c.demT[:, :], start=True, stop=True)
            _inc(c, t.last, "pe", 1)
        b.on("tensor", tt_mm)
        pev = c.cnt["pe"]

        def vt(v, _lt=lt, _buf=buf, _tr=treg):
            v.wait_ge(c.sem["pe"], pev)
            v.wait_ge(c.sem["sk"], sk_bef)
            tp = c.tpad[:, _buf * PADL:(_buf + 1) * PADL]
            vsync(c, v, v.tensor_scalar(
                tp[:, 127:200], c.PS[:, _tr:_tr + 73],
                c.PS[:, _tr:_tr + 1], None, op0=OP.subtract))
            vsync(c, v, v.tensor_scalar(
                c.acol[:, :], c.PS[:, _tr + 72:_tr + 73],
                c.PS[:, _tr:_tr + 1], None, op0=OP.subtract))
            vsync(c, v, v.tensor_scalar(
                tp[:, 200:PADL], tp[:, 200:PADL], 0.0, c.acol[:, :],
                op0=OP.mult, op1=OP.add))
            _inc(c, v.nop(), "vv", 1)
        b.on("vector", vt)
        vvv = c.cnt["vv"]

        def swr(s, _lt=lt, _buf=buf):
            s.wait_ge(c.sem["vv"], vvv)
            dst = c.d_skew.ap().copy()
            dst.ap = bass_rust.VecI64Pair([[PADL, 128], [1, PADL]])
            dst.offset = _buf * 128 * PADL
            _inc(c, s.dma_start(out=dst,
                                in_=c.tpad[:, _buf * PADL:(_buf + 1) * PADL]),
                 "sk", 16)
        b.on("sync", swr)
        skv = c.cnt["sk"]

        def srd(s, _lt=lt, _buf=buf):
            s.wait_ge(c.sem["sk"], skv)
            src = c.d_skew.ap().copy()
            src.ap = bass_rust.VecI64Pair([[PADL - 1, 128], [1, WIN]])
            src.offset = _buf * 128 * PADL + 127
            _inc(c, s.dma_start(out=c.bandl[:, _lt * WIN:(_lt + 1) * WIN],
                                in_=src), "sk", 16)
        b.on("sync", srd)

    sk_all = c.cnt["sk"]

    def tbt(t):
        t.wait_ge(c.sem["sk"], sk_all)
        t.wait_ge(c.sem["dv"], c.cnt["dv"])
        for lt in range(NT):
            s0 = lt * WIN
            t.transpose(c.PSb[:, 4096 + lt * 128:4096 + lt * 128 + 128],
                        c.bandl[:, s0 + 64:s0 + 192], c.ident[:, :])
            if lt >= 1:
                t.transpose(
                    c.PSb[64:128, 6144 + (lt - 1) * 128:6144 + (lt - 1) * 128 + 128],
                    c.bandl[:, s0:s0 + 64], c.ident[:, :],
                    tile_position=(0, 64))
            if lt <= 14:
                t.transpose(
                    c.PSb[0:32, 6144 + (lt + 1) * 128:6144 + (lt + 1) * 128 + 128],
                    c.bandl[:, s0 + 192:s0 + 224], c.ident[:, :])
        _inc(c, t.last, "pe", 1)
    b.on("tensor", tbt)
    pev = c.cnt["pe"]

    def vbt(v):
        v.wait_ge(c.sem["pe"], pev)
        v.tensor_copy(c.bandTB[:, :], c.PSb[:, 4096:6144])
        v.tensor_copy(c.bandTAC[0:32, 128:2048], c.PSb[0:32, 6272:8192])
        _inc(c, v.tensor_copy(c.bandTAC[64:128, 0:1920],
                              c.PSb[64:128, 6144:8064]), "dv", 1)
    b.on("vector", vbt)
    dv_band = c.cnt["dv"]

    for rt in range(NT):
        ac_bef = c.cnt["ac"]

        def ts(t, _rt=rt):
            t.wait_ge(c.sem["dv"], dv_band)
            t.wait_ge(c.sem["ac"], ac_bef)
            r0 = _rt * 128
            aug_end = max(0, (_rt - 1) * 128)

            mms = []

            def add(bank, kw):
                mms.append([bank, kw])

            w0 = 0
            while w0 < aug_end:
                w1 = min(w0 + 512 - (w0 % 512), aug_end)
                add(w0 // 512, dict(
                    out=c.PS[:, w0:w1], lhsT=c.ksl[j][0:65, r0:r0 + 128],
                    rhs=c.qsl[j][0:65, w0:w1]))
                w0 = w1
            w0 = r0
            while w0 < L:
                w1 = min(w0 + 512 - (w0 % 512), L)
                add(w0 // 512, dict(
                    out=c.PS[:, w0:w1], lhsT=c.ksl[j][0:64, r0:r0 + 128],
                    rhs=c.qsl[j][0:64, w0:w1]))
                w0 = w1
            if _rt >= 1:
                w0, w1 = aug_end, aug_end + 128
                mb = w0 // 512
                add(mb, dict(out=c.PS[0:32, w0:w1],
                             lhsT=c.ksl[j][0:64, r0:r0 + 32],
                             rhs=c.qsl[j][0:64, w0:w1]))
                add(mb, dict(out=c.PS[32:64, w0:w1],
                             lhsT=c.ksl[j][0:65, r0 + 32:r0 + 64],
                             rhs=c.qsl[j][0:65, w0:w1],
                             tile_position=(0, 32)))
                add(mb, dict(out=c.PS[64:128, w0:w1],
                             lhsT=c.ksl[j][0:65, r0 + 64:r0 + 128],
                             rhs=c.qsl[j][0:65, w0:w1],
                             tile_position=(0, 64)))
            add(_rt // 4, dict(out=c.PS[:, r0:r0 + 128], lhsT=c.ident[:, :],
                               rhs=c.bandTB[:, r0:r0 + 128]))
            if _rt <= 14:
                add((_rt + 1) // 4, dict(
                    out=c.PS[:, (_rt + 1) * 128:(_rt + 1) * 128 + 128],
                    lhsT=c.ident[64:128, :],
                    rhs=c.bandTAC[64:128, r0:r0 + 128],
                    tile_position=(64, 0)))
            if _rt >= 1:
                add((_rt - 1) // 4, dict(
                    out=c.PS[:, (_rt - 1) * 128:(_rt - 1) * 128 + 128],
                    lhsT=c.ident[0:32, :],
                    rhs=c.bandTAC[0:32, r0:r0 + 128]))

            first_i = {}
            last_i = {}
            for i, (bank, _) in enumerate(mms):
                first_i.setdefault(bank, i)
                last_i[bank] = i
            for i, (bank, kw) in enumerate(mms):
                t.matmul(kw.pop("out"), kw.pop("lhsT"), kw.pop("rhs"),
                         start=(first_i[bank] == i), stop=(last_i[bank] == i),
                         skip_group_check=True, **kw)
            _inc(c, t.last, "pe", 1)
        b.on("tensor", ts)
        pev = c.cnt["pe"]

        def sx(s):
            s.wait_ge(c.sem["pe"], pev)
            _inc(c, s.activation(c.eT[:, 0:L],
                                 c.PS[:, 0:L], AF.Exp, scale=SCALE), "ac", 1)
        b.on("scalar", sx)
        acv = c.cnt["ac"]

        def tav(t, _rt=rt):
            t.wait_ge(c.sem["ac"], acv)
            for nb in range(4):
                t.matmul(c.PS[0:65, 2048 + nb * 512:2048 + (nb + 1) * 512],
                         c.vfull[:, _rt * 1040 + 65 * h:_rt * 1040 + 65 * h + 65],
                         c.eT[:, nb * 512:(nb + 1) * 512],
                         start=(_rt == 0), stop=(_rt == 15),
                         skip_group_check=True)
            _inc(c, t.last, "pe", 1)
        b.on("tensor", tav)

    pe_av = c.cnt["pe"]

    def vzr(v):
        v.wait_ge(c.sem["pe"], pe_av)
        with nc.allow_low_precision(reason="1/Z bf16: 2^-9 rel on softmax "
                                    "normalizer, well inside tolerance"):
            _inc(c, v.reciprocal(c.zrow[:, :], c.PS[64:65, 2048:4096]),
                 "dv", 1)
    b.on("vector", vzr)
    dvv = c.cnt["dv"]

    def tbz(t):
        t.wait_ge(c.sem["dv"], dvv)
        for nb in range(4):
            t.matmul(c.PS[:, nb * 512:(nb + 1) * 512], c.onesr[0:1, 0:128],
                     c.zrow[:, nb * 512:(nb + 1) * 512],
                     start=True, stop=True)
        _inc(c, t.last, "pe", 1)
    b.on("tensor", tbz)
    pev = c.cnt["pe"]

    def vo(v):
        v.wait_ge(c.sem["pe"], pev)
        vsync(c, v, v.tensor_copy(c.zbc[:, :], c.PS[0:64, 0:L]))
        _inc(c, v.tensor_tensor(
            c.outT[64 * (h & 1):64 * (h & 1) + 64,
                   (h >> 1) * L:(h >> 1) * L + L],
            c.PS[0:64, 2048:4096], c.zbc[:, :], op=OP.mult), "dv", 1)
        v.wait_ge(c.sem["dv"], c.cnt["dv"])
        _inc(c, v.nop(), "ac", 1)
    b.on("vector", vo)


def emit_oproj(b):
    nc, c = b.nc, b.c
    wo = c.wosl

    def gw(g):
        g.wait_ge(c.sem["pe"], c.cnt["pe"])
        _inc(c, g.dma_start(out=wo, in_=c.i["wo_s"]), "gp", 16)
    b.on("gpsimd", gw)
    gpv = c.cnt["gp"]
    dv_all = c.cnt["dv"]

    for lt in range(NT):
        reg = lt % 2
        dv_bef = c.cnt["dv"]

        def to(t, _lt=lt, _reg=reg):
            t.wait_ge(c.sem["gp"], gpv)
            t.wait_ge(c.sem["dv"], dv_all if _lt < 2 else c.cnt["dv"])
            for half in range(2):
                out = c.PS[:, _reg * 2048 + half * 512:
                           _reg * 2048 + half * 512 + 512]
                for ch in range(8):
                    t.matmul(out,
                             c.outT[:, ch * L + _lt * 128:ch * L + _lt * 128 + 128],
                             wo[:, ch * 1024 + half * 512:
                                ch * 1024 + half * 512 + 512],
                             start=(ch == 0), stop=False)
                t.matmul(out, c.onesr[0:1, 0:128],
                         c.bor[0:1, half * 512:half * 512 + 512],
                         start=False, stop=True)
            _inc(c, t.last, "pe", 1)
        b.on("tensor", to)
        pev = c.cnt["pe"]

        par = lt % 2
        st_pre = c.cnt["st"]

        def sxl(s, _lt=lt, _p=par):
            s.wait_ge(c.sem["st"], st_pre)
            _inc(c, s.dma_start(out=c.xb[:, _p * D:(_p + 1) * D],
                                in_=c.d_xr.ap()[_lt * 128:(_lt + 1) * 128, :]),
                 "rb", 16)
        b.on("sync", sxl)
        rbx = c.cnt["rb"]

        def vr(v, _lt=lt, _reg=reg, _p=par):
            v.wait_ge(c.sem["pe"], pev)
            v.wait_ge(c.sem["rb"], rbx)
            _inc(c, v.scalar_tensor_tensor(
                c.xb[:, _p * D:(_p + 1) * D],
                c.PS[:, _reg * 2048:_reg * 2048 + 1024], 1.0,
                c.xb[:, _p * D:(_p + 1) * D], op0=OP.mult, op1=OP.add),
                "dv", 1)
        b.on("vector", vr)
        dvx = c.cnt["dv"]

        def sxs(s, _lt=lt, _p=par):
            s.wait_ge(c.sem["dv"], dvx)
            _inc(c, s.dma_start(out=c.d_xr.ap()[_lt * 128:(_lt + 1) * 128, :],
                                in_=c.xb[:, _p * D:(_p + 1) * D]), "st", 16)
        b.on("sync", sxs)


def emit_out(b):
    nc, c = b.nc, b.c

    def so(s):
        s.wait_ge(c.sem["st"], c.cnt["st"])
        _inc(c, s.dma_start(out=c.o_out, in_=c.d_xr.ap()), "st", 16)
        s.wait_ge(c.sem["st"], c.cnt["st"])
    b.on("sync", so)


def emit_all(b, upto=3):
    emit_p0(b)
    b.barrier()
    if upto == -2:
        emit_attn(b)
        b.barrier()
        emit_out(b)
        return
    if upto >= 1:
        emit_ffn(b)
        b.barrier()
    if upto >= 2:
        emit_attn(b)
        b.barrier()
    if upto >= 3:
        emit_ffn(b)
        b.barrier()
    emit_out(b)


def build_full(upto=3):
    nc, c = build_program()
    with c.es:
        with nc.Block() as block:
            b = Bld(nc, c, block)
            emit_all(b, upto)
    return nc


def kernel(**inputs):
    w = _prep_weights(inputs)
    x_full = np.ascontiguousarray(
        np.asarray(inputs["L0_attn_hidden_states"], dtype=np.float32))
    nc = build_full()
    z_up = np.zeros((SLOTS, D), ml_dtypes.bfloat16)
    z_dn = np.zeros((SLOTS, DH), ml_dtypes.bfloat16)
    in_maps = []
    for core in range(8):
        m = {"x": x_full[core], "d_xg_up": z_up, "d_xg_dn": z_dn}
        m.update(w)
        in_maps.append(m)
    res = run_bass_kernel_spmd(nc, in_maps, core_ids=list(range(8)))
    out = np.stack([np.asarray(r["out"], dtype=np.float32)
                    for r in res.results], axis=0)
    return out


# version 11
# speedup vs baseline: 1.1162x; 1.0164x over previous
import contextlib
import numpy as np
import ml_dtypes
import bass_rust
import concourse.bass as bass
import concourse.mybir as mybir
from concourse.bass import IndirectOffsetOnAxis
from concourse.bass_utils import run_bass_kernel_spmd

F32 = mybir.dt.float32
BF16 = mybir.dt.bfloat16
U32 = mybir.dt.uint32
AF = mybir.ActivationFunctionType
OP = mybir.AluOpType
AX = mybir.AxisListType

L = 2048
NT = 16
D = 1024
NH = 16
E = 64
CAP = 128
NBLK = 96
SLOTS = NBLK * CAP
DH = 4096
WIN = 224
PADL = WIN + 127
SCALE = 0.125


def fancy(base_ap, axes, extra_off=0):
    ap = base_ap.copy()
    part = list(ap.ap)[0]
    ap.ap = bass_rust.VecI64Pair([list(part)] + [list(a) for a in axes])
    ap.offset = ap.offset + extra_off
    return ap


def _prep_weights(inp):
    f32 = lambda a: np.ascontiguousarray(np.asarray(a), dtype=np.float32)
    w = {}
    Wq, Wk, Wv, Wo = (f32(inp[f"L0_attn_W{ch}"]) for ch in "qkvo")
    sw = lambda wt: np.ascontiguousarray(
        wt.reshape(8, 128, 1024).transpose(1, 0, 2).reshape(128, 8192))
    w["wq_s"] = sw(f32(Wq.T))
    w["wk_s"] = sw(f32(Wk.T))
    w["wv_s"] = sw(f32(Wv.T))
    w["wo_s"] = sw(f32(Wo.T))
    w["wq_r"] = Wq
    w["bq"] = f32(inp["L0_attn_bq"]).reshape(8, 128).T.copy()
    w["bk"] = f32(inp["L0_attn_bk"]).reshape(8, 128).T.copy()
    w["bv"] = f32(inp["L0_attn_bv"]).reshape(1, D)
    w["bo"] = f32(inp["L0_attn_bo"]).reshape(1, D)
    w["demT"] = f32(np.asarray(inp["L0_attn_dist_emb"]).T)

    w["wup_s"] = np.ascontiguousarray(
        f32(np.asarray(inp["L0_ffn_W_up"]).T).reshape(8, 128, 64)
        .transpose(1, 0, 2).reshape(128, 512))
    w["wdn_s"] = np.ascontiguousarray(
        f32(np.asarray(inp["L0_ffn_W_down"]).T).reshape(32, 128, 64)
        .transpose(1, 0, 2).reshape(128, 2048))
    A_up = f32(inp["L0_ffn_A_up"])
    B_up = f32(inp["L0_ffn_B_up"])
    A_dn = f32(inp["L0_ffn_A_down"])
    B_dn = f32(inp["L0_ffn_B_down"])
    w["bupg"] = np.ascontiguousarray(
        np.tile(B_up.transpose(2, 0, 1), (4, 1, 1))
        .transpose(1, 0, 2).reshape(E * 128, 64))
    z = np.zeros((E, 4, 32, 4, 64), np.float32)
    at = A_up.transpose(0, 2, 1)
    for pl in range(4):
        z[:, pl, :, pl, :] = at
    w["aupbd"] = np.ascontiguousarray(z.reshape(E * 128, 256))
    z = np.zeros((E, 2, 64, 2, 32), np.float32)
    bt = B_dn.transpose(0, 2, 1)
    for m in range(2):
        z[:, m, :, m, :] = bt
    w["bdng"] = np.ascontiguousarray(z.reshape(E * 128, 64))
    z = np.zeros((E, 2, 4, 32, 4, 32), np.float32)
    for h in range(2):
        ah = A_dn[:, :, 32 * h:32 * h + 32].transpose(0, 2, 1)
        for pl in range(4):
            z[:, h, pl, :, pl, :] = ah
    w["adnbd"] = np.ascontiguousarray(
        z.reshape(E, 2, 128, 128).transpose(0, 2, 1, 3).reshape(E * 128, 256))
    w["scup"] = np.tile(f32(inp["L0_ffn_scale_up"]).reshape(1, 1), (128, 1))
    w["scdn"] = np.tile(f32(inp["L0_ffn_scale_down"]).reshape(1, 1), (128, 1))
    w["biasup"] = f32(inp["L0_ffn_bias_up"]).reshape(1, DH)
    w["biasdn"] = f32(inp["L0_ffn_bias_down"]).reshape(1, D)
    w["ident"] = np.eye(128, dtype=np.float32)
    w["onesr"] = np.ones((1, 128), np.float32)
    w["halfr"] = np.full((1, 128), 0.5, np.float32)
    w["iotaE"] = np.tile(np.arange(E, dtype=np.float32), (128, 1))
    w["sltri"] = np.triu(np.ones((E, E), np.float32), 1)
    w["iotaB"] = np.tile(np.arange(NBLK, dtype=np.float32), (E, 1))
    w["eiota"] = np.arange(E, dtype=np.float32).reshape(E, 1)
    w["onesf"] = np.ones((1, 128), np.float32)
    w["iota128"] = np.arange(128, dtype=np.float32).reshape(128, 1)
    return w


_W_SHAPES = {
    "wq_s": [128, 8192], "wk_s": [128, 8192], "wv_s": [128, 8192],
    "wo_s": [128, 8192],
    "wq_r": [D, D], "bq": [128, 8], "bk": [128, 8], "bv": [1, D],
    "bo": [1, D], "demT": [64, 73], "wup_s": [128, 512],
    "wdn_s": [128, 2048],
    "bupg": [E * 128, 64], "aupbd": [E * 128, 256], "bdng": [E * 128, 64],
    "adnbd": [E * 128, 256], "scup": [128, 1], "scdn": [128, 1],
    "biasup": [1, DH], "biasdn": [1, D], "ident": [128, 128],
    "onesr": [1, 128], "halfr": [1, 128], "iotaE": [128, E],
    "sltri": [E, E], "iotaB": [E, NBLK], "eiota": [E, 1],
    "onesf": [1, 128], "iota128": [128, 1],
}


class Ctx:
    pass


class Rec:

    def __init__(self, eng):
        object.__setattr__(self, "_e", eng)
        object.__setattr__(self, "last", None)

    def __getattr__(self, name):
        a = getattr(object.__getattribute__(self, "_e"), name)
        if callable(a) and name not in ("wait_ge", "nop"):
            def call(*args, **kw):
                r = a(*args, **kw)
                if r is not None and hasattr(r, "then_inc"):
                    object.__setattr__(self, "last", r)
                return r
            return call
        return a


class Bld:
    def __init__(self, nc, c, blk):
        self.nc, self.c, self.blk = nc, c, blk
        self.barv = 0

    def on(self, eng, fn):
        getattr(self.blk, eng)(lambda e: fn(Rec(e)))

    def barrier(self):
        c = self.c
        self.barv += 5
        for eng in ("sync", "tensor", "vector", "scalar", "gpsimd"):
            def body(e, _v=self.barv, _eng=eng):
                if _eng == "vector":
                    e.memset(c.dmy[:, :], 0.0).then_inc(c.sem["bar"], 1)
                elif _eng == "scalar":
                    e.copy(c.dmy2[:, :], c.dmy2[:, :]).then_inc(c.sem["bar"], 1)
                else:
                    e.nop().then_inc(c.sem["bar"], 1)
                e.wait_ge(c.sem["bar"], _v)
            self.on(eng, body)


def _inc(c, instr, name, k):
    instr.then_inc(c.sem[name], k)
    c.cnt[name] += k


def vsync(c, v, instr):
    _inc(c, instr, "vv", 1)
    v.wait_ge(c.sem["vv"], c.cnt["vv"])


def build_program():
    nc = bass.Bass("TRN2", target_bir_lowering=False)
    c = Ctx()
    c.nc = nc
    c.i_x = nc.dram_tensor("x", [L, D], F32, kind="ExternalInput").ap()
    c.i = {k: nc.dram_tensor(k, shp, F32, kind="ExternalInput").ap()
           for k, shp in _W_SHAPES.items()}
    c.o_out = nc.dram_tensor("out", [L, D], F32, kind="ExternalOutput").ap()

    c.d_xg_up = nc.dram_tensor("d_xg_up", [SLOTS, D], BF16,
                               kind="ExternalInput")
    c.d_yp_up = nc.dram_tensor("d_yp_up", [SLOTS, DH], BF16, kind="Internal")
    c.d_xg_dn = nc.dram_tensor("d_xg_dn", [SLOTS, DH], BF16,
                               kind="ExternalInput")
    c.d_yp_dn = nc.dram_tensor("d_yp_dn", [SLOTS, D], BF16, kind="Internal")
    c.d_h = nc.dram_tensor("d_h", [L, DH], BF16, kind="Internal")
    c.d_skew = nc.dram_tensor("d_skew", [2 * 128 * PADL], BF16, kind="Internal")
    c.d_xr = nc.dram_tensor("d_xr", [L, D], F32, kind="Internal")

    es = contextlib.ExitStack()
    c.es = es
    sb = lambda name, shp, dt: es.enter_context(nc.sbuf_tensor(name, shp, dt))

    c.xb = sb("xb", [128, 2 * D], F32)
    c.xnb = sb("xnb", [128, 2 * D], BF16)
    c.wuphl = sb("swuphl", [128, 8 * 128], BF16)
    c.wdnhl = sb("swdnhl", [128, 32 * 128], BF16)
    c.demT = sb("sdemT", [64, 73], BF16)
    pass
    c.ident = sb("sident", [128, 128], BF16)
    c.identF = sb("sidentF", [128, 128], F32)
    c.onesr = sb("sonesr", [1, 128], BF16)
    c.halfr = sb("shalfr", [1, 128], BF16)
    c.iotaE = sb("siotaE", [128, E], F32)
    c.bq = sb("sbq", [128, 8], F32)
    c.bq16 = sb("sbq16", [128, 8], BF16)
    c.bk = sb("sbk", [128, 8], F32)
    c.bvr = sb("sbvr", [1, D], BF16)
    c.bor = sb("sbor", [1, D], BF16)
    c.scupc = sb("sscupc", [128, 1], F32)
    c.scdnc = sb("sscdnc", [128, 1], F32)
    c.bdnbc = sb("sbdnbc", [128, D], BF16)
    c.ddbd = sb("sddbd", [128, 8 * 16], BF16)
    c.ddbq = sb("sddbq", [16, 1], F32)
    c.dd = sb("sdd", [64, 1], F32)

    c.ssum = sb("ssum", [128, NT], F32)
    c.ssq = sb("ssq", [128, NT], F32)
    c.smean = sb("smean", [128, NT], F32)
    c.srstd = sb("srstd", [128, NT], F32)
    c.logL = sb("logL", [128, E], F32)
    c.mxv = sb("mxv", [128, 8], F32)
    c.mxi = sb("mxi", [128, 8], U32)
    c.dstL = sb("dstL", [128, E], F32)
    c.p1 = sb("p1", [128, NT], F32)
    c.p2 = sb("p2", [128, NT], F32)
    c.pd = sb("pd", [128, NT], F32)
    c.dst1 = [sb(f"dst1_{t}", [128, 1], U32) for t in range(NT)]
    c.dst2 = [sb(f"dst2_{t}", [128, 1], U32) for t in range(NT)]
    c.idxf = sb("idxf", [128, 2], F32)

    c.hTc = sb("hTc", [128, 32 * 128], BF16)
    c.sqdump = c.hTc[:, 0:1024]
    c.bupr16 = c.hTc[0:1, 0:DH]
    c.bdnr16 = c.hTc[0:1, 0:D]
    c.m1L = c.hTc[:, 2048:2048 + NT * E]
    c.m2L = c.hTc[:, 3072:3072 + NT * E]
    c.wksl = c.hTc[:, 0:2048]
    c.wvsl = None
    c.augT = c.hTc[0:16, 2048:4096]

    c.ar_a = sb("ar_a", [128, 8192], BF16)
    c.xgtg = c.ar_a[:, :]
    c.wstr = c.ar_a[:, 2048:4096]
    c.xg2 = c.ar_a[:, :]
    c.bupbc = c.ar_a[:, 0:DH]
    c.ar_b = sb("ar_b", [128, 16384], BF16)
    c.xnT = c.ar_b[:, :]
    _bf = c.ar_b[:, :].bitcast(F32)
    c.t1f = _bf[:, 0:DH]
    c.t2f = _bf[:, DH:2 * DH]
    c.t3f = _bf[:, 0:D]

    c.ar_c = sb("ar_c", [128, 16384], BF16)
    c.outT = c.ar_c[:, :]
    c.demTf = c.ar_c[:, :].bitcast(F32)[0:64, 0:73]
    c.gb1 = c.ar_c[:, 0:DH]
    c.gb2 = c.ar_c[:, DH:2 * DH]
    c.hbuf = c.ar_c[:, 2 * DH:4 * DH]
    c.waug = c.ar_c[0:16, 0:1024]
    c.ar_d = sb("ar_d", [128, NT * 1040], BF16)
    _fd = c.ar_d[:, :].bitcast(F32)
    c.sltri = _fd[0:64, 6144:6208]
    c.iotaB96 = _fd[0:64, 6208:6208 + NBLK]
    c.eiotac = _fd[0:64, 6306:6307]
    c.onesrF = _fd[0:1, 6307:6435]
    c.iota128c = _fd[:, 6435:6436]
    c.rsc = _fd[0:64, 6440:6456]
    c.indb = _fd[0:64, 6456:6456 + NBLK]
    c.indb2 = _fd[0:64, 6552:6552 + NBLK]
    c.ebr = _fd[0:1, 6648:6648 + NBLK]
    c.widxu = c.ar_d[:, :].bitcast(U32)[:, 6744:6744 + NBLK]
    c.rscU = c.ar_d[:, :].bitcast(U32)[0:64, 6440:6456]
    c.vfull = c.ar_d[:, :]
    c.yp = c.ar_d[:, 0:2 * DH]
    c.yp2 = c.ar_d[:, 2 * DH:2 * DH + 2 * D]
    c.y1b = c.ar_d[:, 2 * DH + 2 * D:2 * DH + 3 * D]
    c.y2b = c.ar_d[:, 2 * DH + 3 * D:2 * DH + 4 * D]
    c.ar_e = sb("ar_e", [128, 16384], BF16)
    c.sa = c.ar_e[:, 0:4096]
    c.xlb = c.ar_e[:, 4096:6144]
    c.br = c.ar_e[:, 4096:8192]
    c.sa2 = c.ar_e[:, 8192:12288]
    c.br2 = c.ar_e[:, 12288:16384]
    c.qsl = [c.ar_e[0:65, jj * 2048:(jj + 1) * 2048] for jj in range(4)]
    c.ksl = [c.ar_e[0:65, 8192 + jj * 2048:8192 + (jj + 1) * 2048]
             for jj in range(4)]
    c.wosl = c.ar_e[:, 0:8192]
    c.ar_f = sb("ar_f", [128, 10240], BF16)
    _ff = c.ar_f[:, :].bitcast(F32)
    c.logT = _ff[0:64, 0:L]
    c.scanT = _ff[0:64, L:2 * L]
    c.dstT = c.scanT
    c.mT = c.ar_f[0:64, 8192:10240]
    c.zrow = c.ar_f[0:1, 8192:10240]
    c.zbc = c.hTc[64:128, 2048:4096]
    c.eT = c.ar_f[:, 0:2048]
    c.bandl = c.ar_f[:, 2048:2048 + NT * WIN]
    c.bandTB = c.ar_f[:, 5632:7680]
    c.bandTAC = c.ar_f[:, 7680:9728]

    c.aupbd_s = sb("aupbd_s", [128, 2 * 320], BF16)
    c.adnbd_s = sb("adnbd_s", [128, 2 * 320], BF16)

    c.waugT = sb("waugT", [128, 16 * 8], BF16)
    c.tpad = c.ar_a[:, 4096:4096 + 2 * PADL]
    c.avl = c.ar_a[:, 4800:4864]
    c.zrec = sb("zrec", [128, 1], F32)
    c.tcol = sb("tcol", [128, 1], F32)
    c.acol = sb("acol", [128, 1], F32)
    c.dmy = sb("dmy", [128, 1], F32)
    c.dmy2 = sb("dmy2", [128, 1], F32)
    c.PS = es.enter_context(nc.psum_tensor("PS", [128, 4096], F32))
    c.PSb = c.PS.ap().bitcast(BF16)

    sems = ["ld", "pe", "dv", "ac", "gp", "st", "bar", "sk", "ig", "rb", "vv",
            "iz", "s0", "s1", "h0", "h1", "g0", "g1", "r0", "r1"]
    c.sem = {s: es.enter_context(nc.semaphore(s)) for s in sems}
    c.cnt = {s: 0 for s in sems}
    c.up_st_war = {0: 0, 1: 0}
    c.up_rb_war = {0: 0, 1: 0}
    c.dn_rb_war = {0: 0, 1: 0}
    c.up_w_war = {0: 0, 1: 0}
    c.dn_w_war = {0: 0, 1: 0}
    c.dn_st_war = {0: 0, 1: 0}
    c.cu_st_war = {0: 0, 1: 0}
    c.cu_pe_war = {0: 0, 1: 0}
    return nc, c


def emit_p0(b):
    nc, c = b.nc, b.c

    def sync(s):
        _inc(c, s.dma_start(out=c.d_xr.ap(), in_=c.i_x), "rb", 16)
        arbF = c.ar_b[:, :].bitcast(F32)
        _inc(c, s.dma_start(out=arbF[:, 0:512], in_=c.i["wup_s"]), "rb", 16)
        _inc(c, s.dma_start(out=arbF[:, 512:2560], in_=c.i["wdn_s"]),
             "rb", 16)
    b.on("sync", sync)
    c.x_rb = c.cnt["rb"]

    def gp(g):
        i = c.i
        _inc(c, g.memset(c.dmy2[:, :], 0.0), "iz", 1)
        loads = [
            (c.demT[:, :], i["demT"]),
            (c.demTf[:, :], i["demT"]),
            (c.ident[:, :], i["ident"]),
            (c.identF[:, :], i["ident"]),
            (c.onesr[:, :], i["onesr"]),
            (c.halfr[:, :], i["halfr"]),
            (c.iotaE[:, :], i["iotaE"]),
            (c.bq[:, :], i["bq"]),
            (c.bq16[:, :], i["bq"]),
            (c.bk[:, :], i["bk"]),
            (c.bvr[:, :], i["bv"]),
            (c.bor[:, :], i["bo"]),
            (c.scupc[:, :], i["scup"]),
            (c.scdnc[:, :], i["scdn"]),
            (c.bdnr16, i["biasdn"]),
        ]
        for dst, srcap in loads:
            _inc(c, g.dma_start(out=dst, in_=srcap), "ld", 16)
    b.on("gpsimd", gp)

    def sc0(s):
        s.wait_ge(c.sem["iz"], 1)
    b.on("scalar", sc0)
    ldv = c.cnt["ld"]

    def te(t):
        t.wait_ge(c.sem["ld"], ldv)
        for kk in range(2):
            t.matmul(c.PS[:, 512 + kk * 512:1024 + kk * 512], c.halfr[:, :],
                     c.bdnr16[:, kk * 512:(kk + 1) * 512], start=True, stop=True)
        _inc(c, t.last, "pe", 1)
    b.on("tensor", te)
    pev = c.cnt["pe"]

    def ve(v):
        v.wait_ge(c.sem["pe"], pev)
        v.wait_ge(c.sem["rb"], c.x_rb)
        arbF = c.ar_b[:, :].bitcast(F32)
        wupf, wdnf = arbF[:, 0:512], arbF[:, 512:2560]
        vsync(c, v, v.tensor_copy(fancy(c.wuphl[:, :], [[128, 8], [1, 64]]), wupf))
        vsync(c, v, v.tensor_tensor(
            fancy(c.wuphl[:, :], [[128, 8], [1, 64]], 64), wupf,
            fancy(c.wuphl[:, :], [[128, 8], [1, 64]]), op=OP.subtract))
        vsync(c, v, v.tensor_copy(fancy(c.wdnhl[:, :], [[128, 32], [1, 64]]), wdnf))
        vsync(c, v, v.tensor_tensor(
            fancy(c.wdnhl[:, :], [[128, 32], [1, 64]], 64), wdnf,
            fancy(c.wdnhl[:, :], [[128, 32], [1, 64]]), op=OP.subtract))
        vsync(c, v, v.tensor_copy(c.bdnbc[:, :], c.PS[:, 512:1536]))
        vsync(c, v, v.tensor_sub(c.dd[:, :], c.demTf[:, 72:73],
                                 c.demTf[:, 0:1]))
        vsync(c, v, v.memset(c.ddbd[:, :], 0.0))
        last = None
        for h in range(16):
            rb0 = 0 if (h % 2 == 0) else 64
            ch = h // 2
            last = v.tensor_copy(
                c.ddbd[rb0:rb0 + 64, ch * 16 + h:ch * 16 + h + 1], c.dd[:, :])
        _inc(c, last, "dv", 1)
    b.on("vector", ve)
    dvv = c.cnt["dv"]

    def te3(t):
        t.wait_ge(c.sem["dv"], dvv)
        for kk in range(8):
            t.matmul(c.PS[0:16, 1536:1537],
                     c.ddbd[:, kk * 16:(kk + 1) * 16], c.bq16[:, kk:kk + 1],
                     start=(kk == 0), stop=(kk == 7))
        _inc(c, t.last, "pe", 1)
    b.on("tensor", te3)
    pev = c.cnt["pe"]

    def ve3(v):
        v.wait_ge(c.sem["pe"], pev)
        _inc(c, v.tensor_copy(c.ddbq[:, :], c.PS[0:16, 1536:1537]), "dv", 1)
    b.on("vector", ve3)


def emit_ln(b):
    nc, c = b.nc, b.c

    st_pre0 = c.cnt["st"]
    rb_pre0 = c.cnt["rb"]
    for t in range(NT):
        par = t % 2
        dv_pre, ac_pre = c.cnt["dv"], c.cnt["ac"]

        def sl(s, _t=t, _p=par):
            if _t == 0:
                s.wait_ge(c.sem["rb"], rb_pre0)
                s.wait_ge(c.sem["st"], st_pre0)
            s.wait_ge(c.sem["dv"], dv_pre)
            s.wait_ge(c.sem["ac"], ac_pre)
            _inc(c, s.dma_start(out=c.xb[:, _p * D:(_p + 1) * D],
                                in_=c.d_xr.ap()[_t * 128:(_t + 1) * 128, :]),
                 "rb", 16)
        b.on("sync", sl)
        rbv = c.cnt["rb"]

        def ve(v, _t=t, _p=par):
            v.wait_ge(c.sem["rb"], rbv)
            _inc(c, v.tensor_reduce(c.ssum[:, _t:_t + 1],
                                    c.xb[:, _p * D:(_p + 1) * D],
                                    axis=AX.X, op=OP.add), "dv", 1)
        b.on("vector", ve)

        def sc(s, _t=t, _p=par):
            s.wait_ge(c.sem["rb"], rbv)
            _inc(c, s.activation(c.sqdump[:, :], c.xb[:, _p * D:(_p + 1) * D],
                                 AF.Square, accum_out=c.ssq[:, _t:_t + 1]),
                 "ac", 1)
        b.on("scalar", sc)

    dvv, acv = c.cnt["dv"], c.cnt["ac"]

    def ve2(v):
        v.wait_ge(c.sem["dv"], dvv)
        v.wait_ge(c.sem["ac"], acv)
        vsync(c, v, v.tensor_scalar(c.smean[:, :], c.ssum[:, :], 1.0 / D, None,
                                    op0=OP.mult))
        vsync(c, v, v.tensor_scalar(c.srstd[:, :], c.ssq[:, :], 1.0 / D, None,
                                    op0=OP.mult))
        vsync(c, v, v.tensor_tensor(c.ssum[:, :], c.smean[:, :], c.smean[:, :],
                                    op=OP.mult))
        vsync(c, v, v.tensor_tensor(c.srstd[:, :], c.srstd[:, :],
                                    c.ssum[:, :], op=OP.subtract))
        vsync(c, v, v.tensor_scalar(c.srstd[:, :], c.srstd[:, :], 1e-5, None,
                                    op0=OP.add))
        _inc(c, v.nop(), "dv", 1)
    b.on("vector", ve2)
    dvv = c.cnt["dv"]

    def sc2(s):
        s.wait_ge(c.sem["dv"], dvv)
        _inc(c, s.activation(c.srstd[:, :], c.srstd[:, :], AF.Sqrt), "ac", 1)
    b.on("scalar", sc2)
    acv = c.cnt["ac"]

    def ve3(v):
        v.wait_ge(c.sem["ac"], acv)
        _inc(c, v.reciprocal(c.srstd[:, :], c.srstd[:, :]), "dv", 1)
    b.on("vector", ve3)


def emit_xn_tile(b, t, war_sems=(), want_lo=False):
    nc, c = b.nc, b.c
    par = t % 2
    waits = list(war_sems)

    def sl(s, _t=t, _p=par, _w=tuple(waits)):
        for nm, val in _w:
            s.wait_ge(c.sem[nm], val)
        _inc(c, s.dma_start(out=c.xb[:, _p * D:(_p + 1) * D],
                            in_=c.d_xr.ap()[_t * 128:(_t + 1) * 128, :]),
             "rb", 16)
    b.on("sync", sl)
    rbv = c.cnt["rb"]

    def vn(v, _t=t, _p=par, _lo=want_lo):
        v.wait_ge(c.sem["rb"], rbv)
        if not _lo:
            _inc(c, v.scalar_tensor_tensor(
                c.xnb[:, _p * D:(_p + 1) * D], c.xb[:, _p * D:(_p + 1) * D],
                c.smean[:, _t:_t + 1],
                c.srstd[:, _t:_t + 1].to_broadcast((128, D)),
                op0=OP.subtract, op1=OP.mult), "dv", 1)
        else:
            xnf = c.ar_b[:, :].bitcast(F32)[:, 0:D]
            vsync(c, v, v.scalar_tensor_tensor(
                xnf, c.xb[:, _p * D:(_p + 1) * D],
                c.smean[:, _t:_t + 1],
                c.srstd[:, _t:_t + 1].to_broadcast((128, D)),
                op0=OP.subtract, op1=OP.mult))
            vsync(c, v, v.tensor_copy(c.xnb[:, _p * D:(_p + 1) * D], xnf))
            _inc(c, v.tensor_tensor(
                c.xlb[:, _p * D:(_p + 1) * D], xnf,
                c.xnb[:, _p * D:(_p + 1) * D], op=OP.subtract), "dv", 1)
    b.on("vector", vn)
    return c.cnt["dv"]


def emit_router(b, width, wslab, logits_ready):
    nc, c = b.nc, b.c
    nch = width // 128

    if not logits_ready:
        for t in range(NT):
            xn_dv = emit_xn_tile(b, t, war_sems=(("pe", c.cnt["pe"]),),
                                 want_lo=True)
            par = t % 2

            def teT(tt, _t=t, _p=par, _x=xn_dv):
                tt.wait_ge(c.sem["dv"], _x)
                if _t > 0:
                    tt.wait_ge(c.sem["dv"], c.cnt["dv"])
                for ch in range(8):
                    tt.transpose(c.PSb[:, 4096 + ch * 128:4096 + ch * 128 + 128],
                                 c.xnb[:, _p * D + ch * 128:_p * D + ch * 128 + 128],
                                 c.ident[:, :])
                for ch in range(8):
                    tt.transpose(c.PSb[:, 5120 + ch * 128:5120 + ch * 128 + 128],
                                 c.xlb[:, _p * D + ch * 128:_p * D + ch * 128 + 128],
                                 c.ident[:, :])
                _inc(c, tt.last, "pe", 1)
            b.on("tensor", teT)
            pev = c.cnt["pe"]

            def veT(v):
                v.wait_ge(c.sem["pe"], pev)
                v.tensor_copy(c.hTc[:, 0:1024], c.PSb[:, 4096:5120])
                _inc(c, v.tensor_copy(c.hTc[:, 1024:2048],
                                      c.PSb[:, 5120:6144]), "dv", 1)
            b.on("vector", veT)
            dvv = c.cnt["dv"]

            def teM(tt, _t=t):
                tt.wait_ge(c.sem["dv"], dvv)
                out128 = c.PS[0:128, _t * 128:(_t + 1) * 128]
                out64 = c.PS[0:64, _t * 128:(_t + 1) * 128]
                for ch in range(8):
                    if ch < 7:
                        tt.matmul(out128,
                                  c.wuphl[:, ch * 128:(ch + 1) * 128],
                                  c.hTc[:, ch * 128:(ch + 1) * 128],
                                  start=(ch == 0), stop=False)
                        tt.matmul(out64,
                                  c.wuphl[:, ch * 128:ch * 128 + 64],
                                  c.hTc[:, 1024 + ch * 128:1024 + (ch + 1) * 128],
                                  start=False, stop=False)
                    else:
                        tt.matmul(out64,
                                  c.wuphl[:, ch * 128:ch * 128 + 64],
                                  c.hTc[:, 1024 + ch * 128:1024 + (ch + 1) * 128],
                                  start=False, stop=False)
                        tt.matmul(out128,
                                  c.wuphl[:, ch * 128:(ch + 1) * 128],
                                  c.hTc[:, ch * 128:(ch + 1) * 128],
                                  start=False, stop=True)
                _inc(c, tt.last, "pe", 1)
            b.on("tensor", teM)
        pev = c.cnt["pe"]

        def vel(v):
            v.wait_ge(c.sem["pe"], pev)
            vsync(c, v, v.tensor_copy(c.logT[:, :], c.PS[64:128, 0:L]))
            _inc(c, v.tensor_tensor(c.logT[:, :], c.logT[:, :],
                                    c.PS[0:64, 0:L], op=OP.add), "dv", 1)
        b.on("vector", vel)

    dvv = c.cnt["dv"]

    vv_mark = [0]

    def teL(tt):
        tt.wait_ge(c.sem["dv"], dvv)
        for t in range(8):
            tt.transpose(c.PS[:, 2048 + t * 64:2048 + t * 64 + 64],
                         c.logT[:, t * 128:(t + 1) * 128],
                         c.identF[0:64, 0:64])
        _inc(c, tt.last, "pe", 1)
    b.on("tensor", teL)
    pe_base = c.cnt["pe"] - 1

    def vtop(v):
        for half in range(2):
            v.wait_ge(c.sem["pe"], pe_base + 1 + half)
            for t8 in range(8):
                t = half * 8 + t8
                vsync(c, v, v.tensor_copy(
                    c.logL[:, :], c.PS[:, 2048 + t8 * 64:2048 + t8 * 64 + 64]))
                vsync(c, v, v.max(out=c.mxv[:, :], in_=c.logL[:, :]))
                vsync(c, v, v.max_index(out=c.mxi[:, :], in_max=c.mxv[:, :],
                                        in_values=c.logL[:, :]))
                vsync(c, v, v.tensor_copy(c.idxf[:, 0:2], c.mxi[:, 0:2]))
                vsync(c, v, v.tensor_scalar(
                    c.m1L[:, t * E:(t + 1) * E], c.iotaE[:, :],
                    c.idxf[:, 0:1], None, op0=OP.is_equal))
                vsync(c, v, v.tensor_scalar(
                    c.m2L[:, t * E:(t + 1) * E], c.iotaE[:, :],
                    c.idxf[:, 1:2], None, op0=OP.is_equal))
                vsync(c, v, v.tensor_sub(c.pd[:, t:t + 1], c.mxv[:, 1:2],
                                         c.mxv[:, 0:1]))
            if half == 0:
                _inc(c, v.nop(), "dv", 1)
        _inc(c, v.nop(), "dv", 1)
    b.on("vector", vtop)
    dv_half = c.cnt["dv"] - 1

    def teL2(tt):
        tt.wait_ge(c.sem["dv"], dv_half)
        for t in range(8, NT):
            tt.transpose(c.PS[:, 2048 + (t % 8) * 64:2048 + (t % 8) * 64 + 64],
                         c.logT[:, t * 128:(t + 1) * 128],
                         c.identF[0:64, 0:64])
        _inc(c, tt.last, "pe", 1)
    b.on("tensor", teL2)
    dvv = c.cnt["dv"]

    def sca(s):
        s.wait_ge(c.sem["dv"], dvv)
        _inc(c, s.activation(c.pd[:, :], c.pd[:, :], AF.Exp), "ac", 1)
    b.on("scalar", sca)
    acv = c.cnt["ac"]

    def vp(v):
        v.wait_ge(c.sem["ac"], acv)
        vsync(c, v, v.tensor_scalar(c.p1[:, :], c.pd[:, :], 1.0, None,
                                    op0=OP.add))
        vsync(c, v, v.reciprocal(c.p1[:, :], c.p1[:, :]))
        vsync(c, v, v.tensor_scalar(c.p2[:, :], c.p1[:, :], -1.0, 1.0,
                                    op0=OP.mult, op1=OP.add))
        _inc(c, v.nop(), "dv", 1)
    b.on("vector", vp)
    dvv = c.cnt["dv"]

    def teQ(tt):
        tt.wait_ge(c.sem["dv"], dvv)
        for t in range(NT):
            tt.transpose(c.PSb[0:64, 4096 + t * 128:4096 + t * 128 + 128],
                         c.m1L[:, t * E:(t + 1) * E], c.ident[:, :])
            tt.transpose(c.PSb[0:64, 6144 + t * 128:6144 + t * 128 + 128],
                         c.m2L[:, t * E:(t + 1) * E], c.ident[:, :])
        _inc(c, tt.last, "pe", 1)
    b.on("tensor", teQ)
    pev = c.cnt["pe"]

    R = c.rsc

    def veQ(v):
        v.wait_ge(c.sem["pe"], pev)
        v.wait_ge(c.sem["ld"], c.ffn_ld)
        vsync(c, v, v.tensor_copy(c.mT[0:64, :], c.PSb[0:64, 4096:6144]))
        vsync(c, v, v.tensor_tensor(c.mT[0:64, :], c.mT[0:64, :],
                                    c.PSb[0:64, 6144:8192], op=OP.add))
        vsync(c, v, v.tensor_tensor_scan(c.scanT[:, :], c.mT[0:64, :],
                                         c.mT[0:64, :], 0.0, op0=OP.add,
                                         op1=OP.bypass))
        U = c.rscU
        vsync(c, v, v.tensor_copy(R[:, 0:1], c.scanT[:, L - 1:L]))
        vsync(c, v, v.tensor_scalar(R[:, 1:2], R[:, 0:1], 127.0, None,
                                    op0=OP.add))
        vsync(c, v, v.tensor_copy(U[:, 2:3], R[:, 1:2]))
        vsync(c, v, v.tensor_scalar(U[:, 2:3], U[:, 2:3], 7, None,
                                    op0=OP.logical_shift_right))
        vsync(c, v, v.tensor_scalar(U[:, 2:3], U[:, 2:3], 7, None,
                                    op0=OP.logical_shift_left))
        vsync(c, v, v.tensor_copy(R[:, 3:4], U[:, 2:3]))
        _inc(c, v.nop(), "dv", 1)
    b.on("vector", veQ)
    dvv = c.cnt["dv"]

    def teP(tt):
        tt.wait_ge(c.sem["dv"], dvv)
        tt.matmul(c.PS[0:64, 2048:2049], c.sltri, R[:, 3:4],
                  start=True, stop=True)
        _inc(c, tt.last, "pe", 1)
    b.on("tensor", teP)
    pev = c.cnt["pe"]

    def veP(v):
        v.wait_ge(c.sem["pe"], pev)
        vsync(c, v, v.tensor_copy(R[:, 4:5], c.PS[0:64, 2048:2049]))
        vsync(c, v, v.tensor_scalar(c.scanT[:, :], c.scanT[:, :], -1.0, None,
                                    op0=OP.add))
        vsync(c, v, v.tensor_scalar(c.dstT[:, :], c.scanT[:, :],
                                    R[:, 4:5], None, op0=OP.add))
        vsync(c, v, v.tensor_scalar(R[:, 5:6], R[:, 4:5], 1.0 / 128, None,
                                    op0=OP.mult))
        vsync(c, v, v.tensor_scalar(R[:, 6:7], R[:, 3:4], 1.0 / 128, None,
                                    op0=OP.mult))
        vsync(c, v, v.tensor_tensor(R[:, 6:7], R[:, 5:6], R[:, 6:7],
                                    op=OP.add))
        vsync(c, v, v.tensor_scalar(c.indb[:, :], c.iotaB96, R[:, 5:6],
                                    None, op0=OP.is_ge))
        vsync(c, v, v.tensor_scalar(c.indb2[:, :], c.iotaB96, R[:, 6:7],
                                    None, op0=OP.is_lt))
        vsync(c, v, v.tensor_tensor(c.indb[:, :], c.indb[:, :],
                                    c.indb2[:, :], op=OP.mult))
        _inc(c, v.nop(), "dv", 1)
    b.on("vector", veP)
    dvv = c.cnt["dv"]

    def teW(tt):
        tt.wait_ge(c.sem["dv"], dvv)
        tt.matmul(c.PS[0:1, 2064:2064 + NBLK], c.eiotac, c.indb[:, :],
                  start=True, stop=True)
        _inc(c, tt.last, "pe", 1)
    b.on("tensor", teW)
    pev = c.cnt["pe"]

    def veW(v):
        v.wait_ge(c.sem["pe"], pev)
        _inc(c, v.tensor_scalar(c.ebr[:, :], c.PS[0:1, 2064:2064 + NBLK],
                                128.0, None, op0=OP.mult), "dv", 1)
    b.on("vector", veW)
    dvv = c.cnt["dv"]

    def teW2(tt):
        tt.wait_ge(c.sem["dv"], dvv)
        tt.matmul(c.PS[0:128, 2176:2176 + NBLK], c.onesrF, c.ebr[:, :],
                  start=True, stop=True)
        _inc(c, tt.last, "pe", 1)
    b.on("tensor", teW2)
    pev = c.cnt["pe"]

    def veW2(v):
        v.wait_ge(c.sem["pe"], pev)
        _inc(c, v.tensor_scalar(c.widxu[:, :], c.PS[:, 2176:2176 + NBLK],
                                c.iota128c, None, op0=OP.add), "dv", 1)
    b.on("vector", veW2)
    c.widx_dv = c.cnt["dv"]
    dvv = c.cnt["dv"]

    def teD(tt):
        tt.wait_ge(c.sem["dv"], dvv)
        for t in range(8):
            tt.transpose(c.PS[:, 2048 + t * 64:2048 + t * 64 + 64],
                         c.dstT[0:64, t * 128:(t + 1) * 128],
                         c.identF[0:64, 0:64])
        _inc(c, tt.last, "pe", 1)
    b.on("tensor", teD)
    pe_base = c.cnt["pe"] - 1

    def veD(v):
        for half in range(2):
            v.wait_ge(c.sem["pe"], pe_base + 1 + half)
            for t8 in range(8):
                t = half * 8 + t8
                vsync(c, v, v.tensor_copy(
                    c.dstL[:, :], c.PS[:, 2048 + t8 * 64:2048 + t8 * 64 + 64]))
                for mk, dk in ((c.m1L, c.dst1), (c.m2L, c.dst2)):
                    vsync(c, v, v.tensor_tensor(
                        c.logL[:, :], mk[:, t * E:(t + 1) * E], c.dstL[:, :],
                        op=OP.mult))
                    vsync(c, v, v.tensor_reduce(
                        c.idxf[:, 0:1], c.logL[:, :], axis=AX.X, op=OP.add))
                    vsync(c, v, v.tensor_copy(dk[t][:, :], c.idxf[:, 0:1]))
            if half == 0:
                _inc(c, v.nop(), "dv", 1)
        _inc(c, v.nop(), "dv", 1)
    b.on("vector", veD)
    dv_half2 = c.cnt["dv"] - 1

    def teD2(tt):
        tt.wait_ge(c.sem["dv"], dv_half2)
        for t in range(8, NT):
            tt.transpose(c.PS[:, 2048 + (t % 8) * 64:2048 + (t % 8) * 64 + 64],
                         c.dstT[0:64, t * 128:(t + 1) * 128],
                         c.identF[0:64, 0:64])
        _inc(c, tt.last, "pe", 1)
    b.on("tensor", teD2)


def emit_ffn(b):
    nc, c = b.nc, b.c

    def gcl(g):
        g.wait_ge(c.sem["pe"], c.cnt["pe"])
        g.wait_ge(c.sem["dv"], c.cnt["dv"])
        for dst, key in ((c.sltri, "sltri"), (c.iotaB96, "iotaB"),
                         (c.eiotac, "eiota"), (c.onesrF, "onesf"),
                         (c.iota128c, "iota128")):
            _inc(c, g.dma_start(out=dst, in_=c.i[key]), "ld", 16)
    b.on("gpsimd", gcl)
    c.ffn_ld = c.cnt["ld"]
    emit_ln(b)
    b.barrier()
    emit_router(b, D, None, logits_ready=False)
    b.barrier()

    dvv = c.cnt["dv"]
    for t in range(NT):
        xn_dv = emit_xn_tile(b, t, war_sems=(("ig", c.cnt["ig"]), ("dv", dvv)))
        par = t % 2

        def gsc(g, _t=t, _p=par, _x=xn_dv):
            g.wait_ge(c.sem["dv"], _x)
            for dk in (c.dst1, c.dst2):
                g.wait_ge(c.sem["ig"], c.cnt["ig"])
                _inc(c, g.indirect_dma_start(
                    out=c.d_xg_up.ap(),
                    out_offset=IndirectOffsetOnAxis(ap=dk[_t][:, :], axis=0),
                    in_=c.xnb[:, _p * D:(_p + 1) * D], in_offset=None),
                    "ig", 16)
        b.on("gpsimd", gsc)
    igv_disp = c.cnt["ig"]

    for grp in range(NBLK // 8):
        rsem = "r%d" % (grp % 2)
        pe_war = c.cnt["pe"]

        def mk_srd(c0, c1):
            def srd(s, _g=grp, _pw=pe_war, _rs=rsem, _c0=c0, _c1=c1):
                s.wait_ge(c.sem["ig"], igv_disp)
                s.wait_ge(c.sem["pe"], _pw)
                for ch in range(_c0, _c1):
                    _inc(c, s.dma_start(
                        out=c.ar_a[:, ch * 1024:ch * 1024 + 1024],
                        in_=c.d_xg_up.ap()[_g * 1024:(_g + 1) * 1024,
                                           ch * 128:(ch + 1) * 128],
                        transpose=True), _rs, 16)
            return srd
        b.on("sync", mk_srd(0, 4))
        b.on("scalar", mk_srd(4, 8))
        rbv = (rsem, c.cnt[rsem])
        for bi in range(8):
            emit_up_block(b, grp * 8 + bi, 0, bi, rbv)
    b.barrier()

    emit_combine_up(b)
    b.barrier()
    emit_router(b, DH, None, logits_ready=True)
    b.barrier()
    emit_down_dispatch(b)
    igv2 = c.cnt["ig"]
    b.barrier()

    for grp in range(NBLK // 2):
        rsem = "r%d" % (grp % 2)
        pe_war = c.cnt["pe"]

        def mk_srd2(c0, c1):
            def srd2(s, _g=grp, _pw=pe_war, _rs=rsem, _c0=c0, _c1=c1):
                s.wait_ge(c.sem["ig"], igv2)
                s.wait_ge(c.sem["pe"], _pw)
                for ch in range(_c0, _c1):
                    _inc(c, s.dma_start(
                        out=c.ar_a[:, ch * 256:ch * 256 + 256],
                        in_=c.d_xg_dn.ap()[_g * 256:(_g + 1) * 256,
                                           ch * 128:(ch + 1) * 128],
                        transpose=True), _rs, 16)
            return srd2
        b.on("sync", mk_srd2(0, 16))
        b.on("scalar", mk_srd2(16, 32))
        rbv = (rsem, c.cnt[rsem])
        for bi in range(2):
            emit_dn_block(b, grp * 2 + bi, 0, bi, rbv)
        c.dn_rb_war[grp % 2] = c.cnt["pe"]
    b.barrier()
    emit_combine_dn(b)


def emit_up_block(b, blkid, buf, bi, rbv):
    nc, c = b.nc, b.c
    e = blkid
    psA = c.PS[:, 0:2048]
    psBb = c.PSb[:, 4096:6144]
    wbuf = blkid % 2

    gsem = "g%d" % (blkid % 2)

    def gw(g, _e=e, _w=wbuf, _pw=c.up_w_war[wbuf], _gs=gsem):
        g.wait_ge(c.sem["pe"], _pw)
        g.wait_ge(c.sem["dv"], c.widx_dv)
        _inc(c, g.indirect_dma_start(
            out=c.aupbd_s[:, _w * 320:_w * 320 + 256], out_offset=None,
            in_=c.i["aupbd"],
            in_offset=IndirectOffsetOnAxis(ap=c.widxu[:, _e:_e + 1], axis=0)),
            _gs, 16)
        _inc(c, g.indirect_dma_start(
            out=c.aupbd_s[:, _w * 320 + 256:_w * 320 + 320], out_offset=None,
            in_=c.i["bupg"],
            in_offset=IndirectOffsetOnAxis(ap=c.widxu[:, _e:_e + 1], axis=0)),
            _gs, 16)
    b.on("gpsimd", gw)
    gpv = c.cnt[gsem]

    dv_bef, ac_bef = c.cnt["dv"], c.cnt["ac"]

    def teA(t, _buf=buf, _bi=bi, _e=e, _w=wbuf, _gs=gsem):
        t.wait_ge(c.sem[rbv[0]], rbv[1])
        t.wait_ge(c.sem[_gs], gpv)
        t.wait_ge(c.sem["dv"], dv_bef)
        t.wait_ge(c.sem["ac"], ac_bef)
        for i in range(32):
            u, a = i & 3, i >> 2
            lhsT = c.ar_a[32 * u:32 * u + 32,
                          a * 1024 + _bi * 128:a * 1024 + _bi * 128 + 128]
            rhs = c.aupbd_s[32 * u:32 * u + 32,
                            _w * 320 + 256:_w * 320 + 320]
            out = psA[:, u * 512 + a * 64:u * 512 + a * 64 + 64]
            t.matmul(out, lhsT, rhs, start=True, stop=True,
                     tile_position=(32 * u, 0))
        _inc(c, t.last, "pe", 1)
    b.on("tensor", teA)
    pev = c.cnt["pe"]
    sbuf_i = blkid % 2

    def veA(v, _s=sbuf_i):
        v.wait_ge(c.sem["pe"], pev)
        src = fancy(psA, [[1, 64], [64, 8], [512, 4]])
        dst = fancy(c.sa[:, _s * 2048:(_s + 1) * 2048],
                    [[32, 64], [4, 8], [1, 4]])
        _inc(c, v.tensor_copy(dst, src), "dv", 1)
    b.on("vector", veA)
    dvv = c.cnt["dv"]

    def teB(t, _s=sbuf_i):
        t.wait_ge(c.sem["dv"], dvv)
        src = c.sa[:, _s * 2048:(_s + 1) * 2048]
        for q in range(16):
            t.transpose(psBb[:, q * 128:(q + 1) * 128],
                        src[:, q * 128:(q + 1) * 128], c.ident[:, :])
        _inc(c, t.last, "pe", 1)
    b.on("tensor", teB)
    pev = c.cnt["pe"]

    def veB(v, _s=sbuf_i):
        v.wait_ge(c.sem["pe"], pev)
        _inc(c, v.tensor_copy(c.br[:, _s * 2048:(_s + 1) * 2048], psBb),
             "dv", 1)
    b.on("vector", veB)
    dvv = c.cnt["dv"]

    for rnd in range(2):
        def teC(t, _s=sbuf_i, _w=wbuf, _r=rnd, _gs=gsem):
            t.wait_ge(c.sem["dv"], dvv if _r == 0 else c.cnt["dv"])
            t.wait_ge(c.sem[_gs], gpv)
            for q8 in range(8):
                q = _r * 8 + q8
                lhsT = c.br[:, _s * 2048 + q * 128:_s * 2048 + q * 128 + 128]
                rhs = c.aupbd_s[:, _w * 320:_w * 320 + 256]
                t.matmul(psA[:, q8 * 256:(q8 + 1) * 256], lhsT, rhs,
                         start=True, stop=True)
            _inc(c, t.last, "pe", 1)
        b.on("tensor", teC)
        pev2 = c.cnt["pe"]

        st_war = c.up_st_war[blkid % 2]

        def veC(v, _r=rnd, _sw=st_war):
            v.wait_ge(c.sem["pe"], pev2)
            if _r == 0:
                v.wait_ge(c.sem["s%d" % (blkid % 2)], _sw)
            src = fancy(psA, [[256, 8], [64, 4], [1, 64]])
            ybase = c.yp[:, (blkid % 2) * DH:(blkid % 2) * DH + DH]
            dst = fancy(ybase, [[4, 8], [1, 4], [64, 64]], 32 * _r)
            _inc(c, v.tensor_copy(dst, src), "dv", 1)
        b.on("vector", veC)

    dvv2 = c.cnt["dv"]

    def veS(v):
        v.wait_ge(c.sem["dv"], dvv2)
        y = c.yp[:, (blkid % 2) * DH:(blkid % 2) * DH + DH]
        _inc(c, v.tensor_scalar(y, y, c.scupc[:, :], None, op0=OP.mult),
             "dv", 1)
    b.on("vector", veS)
    dvv3 = c.cnt["dv"]

    def sst(s, _b=blkid):
        s.wait_ge(c.sem["dv"], dvv3)
        _inc(c, s.dma_start(out=c.d_yp_up.ap()[_b * 128:(_b + 1) * 128, :],
                            in_=c.yp[:, (_b % 2) * DH:(_b % 2) * DH + DH]),
             "s%d" % (_b % 2), 16)
    b.on("scalar" if blkid % 2 == 0 else "sync", sst)
    c.up_st_war[blkid % 2] = c.cnt["s%d" % (blkid % 2)]
    c.up_w_war[blkid % 2] = c.cnt["pe"]


def emit_combine_up(b):
    nc, c = b.nc, b.c
    s0_all, s1_all = c.cnt["s0"], c.cnt["s1"]

    rb_pre = c.cnt["rb"]

    def gb_(g):
        g.wait_ge(c.sem["pe"], c.cnt["pe"])
        _inc(c, g.dma_start(out=c.bupr16, in_=c.i["biasup"]), "gp", 16)
    b.on("gpsimd", gb_)
    gpv = c.cnt["gp"]

    def tb_(t):
        t.wait_ge(c.sem["gp"], gpv)
        for kk in range(8):
            t.matmul(c.PS[:, kk * 512:(kk + 1) * 512], c.onesr[:, :],
                     c.bupr16[:, kk * 512:(kk + 1) * 512],
                     start=True, stop=True)
        _inc(c, t.last, "pe", 1)
    b.on("tensor", tb_)
    pev_b = c.cnt["pe"]

    def vb_(v):
        v.wait_ge(c.sem["pe"], pev_b)
        _inc(c, v.tensor_copy(c.bupbc[:, :], c.PS[:, 0:DH]), "dv", 1)
    b.on("vector", vb_)

    for t in range(NT):
        def gg(g, _t=t):
            g.wait_ge(c.sem["s0"], s0_all)
            g.wait_ge(c.sem["s1"], s1_all)
            g.wait_ge(c.sem["dv"], c.cnt["dv"])
            _inc(c, g.indirect_dma_start(
                out=c.gb1[:, :], out_offset=None, in_=c.d_yp_up.ap(),
                in_offset=IndirectOffsetOnAxis(ap=c.dst1[_t][:, :], axis=0)),
                "ig", 16)
            _inc(c, g.indirect_dma_start(
                out=c.gb2[:, :], out_offset=None, in_=c.d_yp_up.ap(),
                in_offset=IndirectOffsetOnAxis(ap=c.dst2[_t][:, :], axis=0)),
                "ig", 16)
        b.on("gpsimd", gg)
        igv = c.cnt["ig"]

        def vc(v, _t=t):
            v.wait_ge(c.sem["ig"], igv)
            vsync(c, v, v.tensor_scalar(c.t1f[:, :], c.gb1[:, :],
                                        c.p1[:, _t:_t + 1], None, op0=OP.mult))
            vsync(c, v, v.scalar_tensor_tensor(
                c.t2f[:, :], c.gb2[:, :], c.p2[:, _t:_t + 1], c.t1f[:, :],
                op0=OP.mult, op1=OP.add))
            vsync(c, v, v.tensor_tensor(c.t1f[:, :], c.t2f[:, :],
                                        c.bupbc[:, :], op=OP.add))
            _inc(c, v.nop(), "dv", 1)
        b.on("vector", vc)
        dvv = c.cnt["dv"]
        hb = t % 2

        def sg(s, _hb=hb, _sw=c.cu_st_war[hb], _pw=c.cu_pe_war[hb]):
            s.wait_ge(c.sem["dv"], dvv)
            s.wait_ge(c.sem["h%d" % hb], _sw)
            s.wait_ge(c.sem["pe"], _pw)
            _inc(c, s.activation(c.t2f[:, :], c.t1f[:, :], AF.Gelu), "ac", 1)
            s.wait_ge(c.sem["ac"], c.cnt["ac"])
            _inc(c, s.copy(c.hbuf[:, _hb * DH:(_hb + 1) * DH], c.t2f[:, :]),
                 "ac", 1)
        b.on("scalar", sg)
        acv = c.cnt["ac"]

        def shw(s, _t=t, _hb=hb):
            s.wait_ge(c.sem["ac"], acv)
            _inc(c, s.dma_start(out=c.d_h.ap()[_t * 128:(_t + 1) * 128, :],
                                in_=c.hbuf[:, _hb * DH:(_hb + 1) * DH]),
                 "h%d" % _hb, 16)
        b.on("scalar", shw)
        c.cu_st_war[hb] = c.cnt["h%d" % hb]

        pe_prev_teM = c.cnt["pe"]
        for half in range(2):
            def vlo(v, _hb=hb, _half=half, _pp=pe_prev_teM):
                v.wait_ge(c.sem["ac"], acv)
                v.wait_ge(c.sem["pe"], c.cnt["pe"])
                _inc(c, v.tensor_tensor(
                    c.hTc[:, 2048:4096],
                    c.t2f[:, _half * 2048:(_half + 1) * 2048],
                    c.hbuf[:, _hb * DH + _half * 2048:
                           _hb * DH + (_half + 1) * 2048],
                    op=OP.subtract), "dv", 1)
            b.on("vector", vlo)
            dv_lo = c.cnt["dv"]

            def teL(tt, _hb=hb, _half=half):
                tt.wait_ge(c.sem["ac"], acv)
                tt.wait_ge(c.sem["dv"], dv_lo)
                for ch16 in range(16):
                    ch = _half * 16 + ch16
                    tt.transpose(
                        c.PSb[:, 4096 + ch16 * 128:4096 + ch16 * 128 + 128],
                        c.hbuf[:, _hb * DH + ch * 128:_hb * DH + ch * 128 + 128],
                        c.ident[:, :])
                for ch16 in range(16):
                    tt.transpose(
                        c.PSb[:, 6144 + ch16 * 128:6144 + ch16 * 128 + 128],
                        c.hTc[:, 2048 + ch16 * 128:2048 + ch16 * 128 + 128],
                        c.ident[:, :])
                _inc(c, tt.last, "pe", 1)
            b.on("tensor", teL)
            pev = c.cnt["pe"]

            def veL(v):
                v.wait_ge(c.sem["pe"], pev)
                v.tensor_copy(c.hTc[:, 0:2048], c.PSb[:, 4096:6144])
                _inc(c, v.tensor_copy(c.hTc[:, 2048:4096],
                                      c.PSb[:, 6144:8192]), "dv", 1)
            b.on("vector", veL)
            dvv2 = c.cnt["dv"]

            def teM(tt, _t=t, _half=half):
                tt.wait_ge(c.sem["dv"], dvv2)
                out128 = c.PS[0:128, _t * 128:(_t + 1) * 128]
                out64 = c.PS[0:64, _t * 128:(_t + 1) * 128]
                for ch16 in range(16):
                    ch = _half * 16 + ch16
                    last_mm = (_half == 1 and ch16 == 15)
                    if not last_mm:
                        tt.matmul(out128,
                                  c.wdnhl[:, ch * 128:(ch + 1) * 128],
                                  c.hTc[:, ch16 * 128:(ch16 + 1) * 128],
                                  start=(_half == 0 and ch16 == 0), stop=False)
                        tt.matmul(out64,
                                  c.wdnhl[:, ch * 128:ch * 128 + 64],
                                  c.hTc[:, 2048 + ch16 * 128:
                                        2048 + (ch16 + 1) * 128],
                                  start=False, stop=False)
                    else:
                        tt.matmul(out64,
                                  c.wdnhl[:, ch * 128:ch * 128 + 64],
                                  c.hTc[:, 2048 + ch16 * 128:
                                        2048 + (ch16 + 1) * 128],
                                  start=False, stop=False)
                        tt.matmul(out128,
                                  c.wdnhl[:, ch * 128:(ch + 1) * 128],
                                  c.hTc[:, ch16 * 128:(ch16 + 1) * 128],
                                  start=False, stop=True)
                _inc(c, tt.last, "pe", 1)
            b.on("tensor", teM)
        c.cu_pe_war[hb] = c.cnt["pe"]

    pev = c.cnt["pe"]

    def vf(v):
        v.wait_ge(c.sem["pe"], pev)
        vsync(c, v, v.tensor_copy(c.logT[:, :], c.PS[64:128, 0:L]))
        _inc(c, v.tensor_tensor(c.logT[:, :], c.logT[:, :],
                                c.PS[0:64, 0:L], op=OP.add), "dv", 1)
    b.on("vector", vf)


def emit_down_dispatch(b):
    nc, c = b.nc, b.c
    dvv = c.cnt["dv"]
    h0_dh, h1_dh = c.cnt["h0"], c.cnt["h1"]
    for t in range(NT):
        hb = t % 2
        ig_prev = c.cnt["ig"]

        def sr(s, _t=t, _hb=hb):
            s.wait_ge(c.sem["h0"], h0_dh)
            s.wait_ge(c.sem["h1"], h1_dh)
            s.wait_ge(c.sem["dv"], dvv)
            s.wait_ge(c.sem["ig"], ig_prev)
            _inc(c, s.dma_start(out=c.hbuf[:, _hb * DH:(_hb + 1) * DH],
                                in_=c.d_h.ap()[_t * 128:(_t + 1) * 128, :]),
                 "rb", 16)
        b.on("sync", sr)
        rbv = c.cnt["rb"]

        def gs(g, _t=t, _hb=hb):
            g.wait_ge(c.sem["rb"], rbv)
            for dk in (c.dst1, c.dst2):
                g.wait_ge(c.sem["ig"], c.cnt["ig"])
                _inc(c, g.indirect_dma_start(
                    out=c.d_xg_dn.ap(),
                    out_offset=IndirectOffsetOnAxis(ap=dk[_t][:, :], axis=0),
                    in_=c.hbuf[:, _hb * DH:(_hb + 1) * DH], in_offset=None),
                    "ig", 16)
        b.on("gpsimd", gs)


def emit_dn_block(b, blkid, buf, bi, rbv):
    nc, c = b.nc, b.c
    e = blkid
    psA = c.PS[:, 0:2048]
    psBb = c.PSb[:, 4096:6144]
    psC = c.PS[:, 3072:4096]
    wbuf = blkid % 2

    gsem = "g%d" % (blkid % 2)

    def gw(g, _e=e, _w=wbuf, _pw=c.dn_w_war[wbuf], _gs=gsem):
        g.wait_ge(c.sem["pe"], _pw)
        g.wait_ge(c.sem["dv"], c.widx_dv)
        _inc(c, g.indirect_dma_start(
            out=c.adnbd_s[:, _w * 320:_w * 320 + 256], out_offset=None,
            in_=c.i["adnbd"],
            in_offset=IndirectOffsetOnAxis(ap=c.widxu[:, _e:_e + 1], axis=0)),
            _gs, 16)
        _inc(c, g.indirect_dma_start(
            out=c.adnbd_s[:, _w * 320 + 256:_w * 320 + 320], out_offset=None,
            in_=c.i["bdng"],
            in_offset=IndirectOffsetOnAxis(ap=c.widxu[:, _e:_e + 1], axis=0)),
            _gs, 16)
    b.on("gpsimd", gw)
    gpv = c.cnt[gsem]
    dv_bef, ac_bef = c.cnt["dv"], c.cnt["ac"]

    def teA(t, _buf=buf, _bi=bi, _e=e, _w=wbuf, _gs=gsem):
        t.wait_ge(c.sem[rbv[0]], rbv[1])
        t.wait_ge(c.sem[_gs], gpv)
        t.wait_ge(c.sem["dv"], dv_bef)
        t.wait_ge(c.sem["ac"], ac_bef)
        for fp in range(32):
            lhsT = c.ar_a[:, fp * 256 + _bi * 128:fp * 256 + _bi * 128 + 128]
            rhs = c.adnbd_s[:, _w * 320 + 256:_w * 320 + 320]
            out = psA[:, (fp & 3) * 512 + (fp >> 2) * 64:
                      (fp & 3) * 512 + (fp >> 2) * 64 + 64]
            t.matmul(out, lhsT, rhs, start=True, stop=True)
        _inc(c, t.last, "pe", 1)
    b.on("tensor", teA)
    pev = c.cnt["pe"]
    sbuf_i = blkid % 2

    def veA(v, _s=sbuf_i):
        v.wait_ge(c.sem["pe"], pev)
        for h in range(2):
            for d in range(2):
                src = fancy(psA, [[1, 32], [64, 4], [512, 4]],
                            256 * h + 32 * d)
                dst = fancy(c.sa2[:, _s * 2048:(_s + 1) * 2048],
                            [[32, 32], [8, 4], [2, 4]], 1024 * h + d)
                last = (h == 1 and d == 1)
                if last:
                    _inc(c, v.tensor_copy(dst, src), "dv", 1)
                else:
                    v.tensor_copy(dst, src)
    b.on("vector", veA)
    dvv = c.cnt["dv"]

    def teB(t, _s=sbuf_i):
        t.wait_ge(c.sem["dv"], dvv)
        src = c.sa2[:, _s * 2048:(_s + 1) * 2048]
        for u in range(16):
            t.transpose(psBb[:, u * 128:(u + 1) * 128],
                        src[:, u * 128:(u + 1) * 128], c.ident[:, :])
        _inc(c, t.last, "pe", 1)
    b.on("tensor", teB)
    pev = c.cnt["pe"]

    def veB(v, _s=sbuf_i):
        v.wait_ge(c.sem["pe"], pev)
        _inc(c, v.tensor_copy(c.br2[:, _s * 2048:(_s + 1) * 2048], psBb),
             "dv", 1)
    b.on("vector", veB)
    dvv = c.cnt["dv"]

    def teC(t, _s=sbuf_i, _w=wbuf, _gs=gsem):
        t.wait_ge(c.sem["dv"], dvv)
        t.wait_ge(c.sem[_gs], gpv)
        for q in range(8):
            for h in range(2):
                lhsT = c.br2[:, _s * 2048 + (h * 8 + q) * 128:
                             _s * 2048 + (h * 8 + q) * 128 + 128]
                rhs = c.adnbd_s[:, _w * 320 + h * 128:_w * 320 + h * 128 + 128]
                t.matmul(psC[:, q * 128:(q + 1) * 128], lhsT, rhs,
                         start=(h == 0), stop=(h == 1))
        _inc(c, t.last, "pe", 1)
    b.on("tensor", teC)
    pev = c.cnt["pe"]

    st_war = c.dn_st_war[blkid % 2]

    def veC(v, _b=blkid, _sw=st_war):
        v.wait_ge(c.sem["pe"], pev)
        v.wait_ge(c.sem["s%d" % (blkid % 2)], _sw)
        src = fancy(psC, [[128, 8], [32, 4], [1, 32]])
        ybase = c.yp2[:, (_b % 2) * D:(_b % 2) * D + D]
        dst = fancy(ybase, [[4, 8], [1, 4], [32, 32]])
        vsync(c, v, v.tensor_copy(dst, src))
        y = c.yp2[:, (_b % 2) * D:(_b % 2) * D + D]
        _inc(c, v.tensor_scalar(y, y, c.scdnc[:, :], None, op0=OP.mult),
             "dv", 1)
    b.on("vector", veC)
    dvv2 = c.cnt["dv"]

    def sst(s, _b=blkid):
        s.wait_ge(c.sem["dv"], dvv2)
        _inc(c, s.dma_start(out=c.d_yp_dn.ap()[_b * 128:(_b + 1) * 128, :],
                            in_=c.yp2[:, (_b % 2) * D:(_b % 2) * D + D]),
             "s%d" % (_b % 2), 16)
    b.on("scalar" if blkid % 2 == 0 else "sync", sst)
    c.dn_st_war[blkid % 2] = c.cnt["s%d" % (blkid % 2)]
    c.dn_w_war[blkid % 2] = c.cnt["pe"]


def emit_combine_dn(b):
    nc, c = b.nc, b.c
    s0_all, s1_all = c.cnt["s0"], c.cnt["s1"]
    for t in range(NT):
        def gg(g, _t=t):
            g.wait_ge(c.sem["s0"], s0_all)
            g.wait_ge(c.sem["s1"], s1_all)
            g.wait_ge(c.sem["dv"], c.cnt["dv"])
            _inc(c, g.indirect_dma_start(
                out=c.y1b[:, :], out_offset=None, in_=c.d_yp_dn.ap(),
                in_offset=IndirectOffsetOnAxis(ap=c.dst1[_t][:, :], axis=0)),
                "ig", 16)
            _inc(c, g.indirect_dma_start(
                out=c.y2b[:, :], out_offset=None, in_=c.d_yp_dn.ap(),
                in_offset=IndirectOffsetOnAxis(ap=c.dst2[_t][:, :], axis=0)),
                "ig", 16)
        b.on("gpsimd", gg)
        igv = c.cnt["ig"]

        par = t % 2
        st_pre = c.cnt["st"]

        def sx2(s, _t=t, _p=par):
            s.wait_ge(c.sem["st"], st_pre)
            _inc(c, s.dma_start(out=c.xb[:, _p * D:(_p + 1) * D],
                                in_=c.d_xr.ap()[_t * 128:(_t + 1) * 128, :]),
                 "rb", 16)
        b.on("sync", sx2)
        rbv2 = c.cnt["rb"]

        def vc(v, _t=t, _p=par):
            v.wait_ge(c.sem["ig"], igv)
            v.wait_ge(c.sem["rb"], rbv2)
            vsync(c, v, v.tensor_scalar(c.t3f[:, :], c.y1b[:, :],
                                        c.p1[:, _t:_t + 1], None, op0=OP.mult))
            vsync(c, v, v.scalar_tensor_tensor(
                c.t3f[:, :], c.y2b[:, :], c.p2[:, _t:_t + 1], c.t3f[:, :],
                op0=OP.mult, op1=OP.add))
            vsync(c, v, v.scalar_tensor_tensor(
                c.t3f[:, :], c.t3f[:, :], 0.5, c.bdnbc[:, :],
                op0=OP.mult, op1=OP.add))
            vsync(c, v, v.tensor_tensor(
                c.xb[:, _p * D:(_p + 1) * D], c.xb[:, _p * D:(_p + 1) * D],
                c.t3f[:, :], op=OP.add))
            _inc(c, v.nop(), "dv", 1)
        b.on("vector", vc)
        dvx = c.cnt["dv"]

        def sx3(s, _t=t, _p=par):
            s.wait_ge(c.sem["dv"], dvx)
            _inc(c, s.dma_start(out=c.d_xr.ap()[_t * 128:(_t + 1) * 128, :],
                                in_=c.xb[:, _p * D:(_p + 1) * D]), "st", 16)
        b.on("sync", sx3)


def emit_attn(b):
    nc, c = b.nc, b.c

    def vtp(v):
        v.wait_ge(c.sem["pe"], c.cnt["pe"])
        v.wait_ge(c.sem["ig"], c.cnt["ig"])
        v.memset(c.tpad[:, :], 0.0)
    b.on("vector", vtp)
    emit_ln(b)
    b.barrier()

    for t in range(NT):
        xn_dv = emit_xn_tile(b, t, war_sems=(("pe", c.cnt["pe"]),))
        par = t % 2

        def teT(tt, _t=t, _p=par, _x=xn_dv):
            tt.wait_ge(c.sem["dv"], _x)
            if _t > 0:
                tt.wait_ge(c.sem["dv"], c.cnt["dv"])
            for ch in range(8):
                tt.transpose(c.PSb[:, 4096 + ch * 128:4096 + ch * 128 + 128],
                             c.xnb[:, _p * D + ch * 128:_p * D + ch * 128 + 128],
                             c.ident[:, :])
            _inc(c, tt.last, "pe", 1)
        b.on("tensor", teT)
        pev = c.cnt["pe"]

        def veT(v, _t=t):
            v.wait_ge(c.sem["pe"], pev)
            last = None
            for ch in range(8):
                last = v.tensor_copy(
                    c.xnT[:, ch * L + _t * 128:ch * L + _t * 128 + 128],
                    c.PSb[:, 4096 + ch * 128:4096 + ch * 128 + 128])
            _inc(c, last, "dv", 1)
        b.on("vector", veT)
    b.barrier()

    emit_aug(b)
    b.barrier()

    for grp in range(4):
        emit_proj_group(b, grp)
        b.barrier()
        for j in range(4):
            emit_head(b, grp, j)
        b.barrier()

    emit_oproj(b)


def emit_aug(b):
    nc, c = b.nc, b.c
    for ch in range(8):
        pe_prev = c.cnt["pe"]
        gsem = "g%d" % (ch % 2)

        def gw(g, _ch=ch, _pp=pe_prev, _gs=gsem):
            if _ch >= 2:
                g.wait_ge(c.sem["pe"], _pp)
            _inc(c, g.dma_start(
                out=c.wstr[:, (_ch % 2) * 1024:(_ch % 2) * 1024 + 1024],
                in_=c.i["wq_r"][_ch * 128:(_ch + 1) * 128, :]), _gs, 16)
        b.on("gpsimd", gw)
        gpv = c.cnt[gsem]

        def tw(t, _ch=ch, _gs=gsem):
            t.wait_ge(c.sem[_gs], gpv)
            for half in range(2):
                t.matmul(c.PS[0:16, half * 512:(half + 1) * 512],
                         c.ddbd[:, _ch * 16:(_ch + 1) * 16],
                         c.wstr[:, (_ch % 2) * 1024 + half * 512:
                                (_ch % 2) * 1024 + half * 512 + 512],
                         start=(_ch == 0), stop=(_ch == 7),
                         skip_group_check=True)
            _inc(c, t.last, "pe", 1)
        b.on("tensor", tw)
    pev = c.cnt["pe"]

    def ve(v):
        v.wait_ge(c.sem["pe"], pev)
        _inc(c, v.tensor_copy(c.waug[:, :], c.PS[0:16, 0:1024]), "dv", 1)
    b.on("vector", ve)
    dvv = c.cnt["dv"]

    def tT(t):
        t.wait_ge(c.sem["dv"], dvv)
        for ch in range(8):
            t.transpose(c.PSb[:, 4096 + ch * 16:4096 + ch * 16 + 16],
                        c.waug[:, ch * 128:(ch + 1) * 128],
                        c.ident[0:16, 0:16])
        _inc(c, t.last, "pe", 1)
    b.on("tensor", tT)
    pev = c.cnt["pe"]

    def ve2(v):
        v.wait_ge(c.sem["pe"], pev)
        _inc(c, v.tensor_copy(c.waugT[:, :], c.PSb[:, 4096:4224]), "dv", 1)
    b.on("vector", ve2)
    dvv = c.cnt["dv"]

    def tA(t):
        t.wait_ge(c.sem["dv"], dvv)
        for lc in range(4):
            for ch in range(8):
                t.matmul(c.PS[0:16, lc * 512:(lc + 1) * 512],
                         c.waugT[:, ch * 16:(ch + 1) * 16],
                         c.xnT[:, ch * L + lc * 512:ch * L + lc * 512 + 512],
                         start=(ch == 0), stop=(ch == 7),
                         skip_group_check=True)
        _inc(c, t.last, "pe", 1)
    b.on("tensor", tA)
    pev = c.cnt["pe"]

    def vA(v):
        v.wait_ge(c.sem["pe"], pev)
        _inc(c, v.tensor_scalar(c.augT[:, :], c.PS[0:16, 0:L],
                                c.ddbq[:, :], None, op0=OP.add), "dv", 1)
    b.on("vector", vA)
    c.aug_dv = c.cnt["dv"]


def emit_proj_group(b, grp):
    nc, c = b.nc, b.c
    col0 = 256 * grp
    wvsl = c.ar_a[:, 0:2048]

    def gw(g):
        g.wait_ge(c.sem["pe"], c.cnt["pe"])
        _inc(c, g.dma_start(
            out=c.wstr[:, 0:2048],
            in_=fancy(c.i["wq_s"], [[1024, 8], [1, 256]], col0)), "gp", 16)
        _inc(c, g.dma_start(
            out=c.wksl,
            in_=fancy(c.i["wk_s"], [[1024, 8], [1, 256]], col0)), "gp", 16)
        _inc(c, g.dma_start(
            out=wvsl,
            in_=fancy(c.i["wv_s"], [[1024, 8], [1, 256]], col0)), "gp", 16)
    b.on("gpsimd", gw)
    gpv = c.cnt["gp"]

    for wslab, slabs, bias in ((c.wstr[:, 0:2048], c.qsl, c.bq),
                               (c.wksl, c.ksl, c.bk)):
        ac_bef = c.cnt["ac"]

        def tp(t, _w=wslab, _ab=ac_bef):
            t.wait_ge(c.sem["gp"], gpv)
            t.wait_ge(c.sem["ac"], _ab)
            for dqt in range(2):
                for lc in range(4):
                    reg = dqt * 4 + lc
                    for ch in range(8):
                        t.matmul(c.PS[:, reg * 512:(reg + 1) * 512],
                                 _w[:, ch * 256 + dqt * 128:
                                    ch * 256 + dqt * 128 + 128],
                                 c.xnT[:, ch * L + lc * 512:
                                       ch * L + lc * 512 + 512],
                                 start=(ch == 0), stop=(ch == 7))
            _inc(c, t.last, "pe", 1)
        b.on("tensor", tp)
        pev = c.cnt["pe"]

        def vev(v, _sl=slabs, _bias=bias):
            v.wait_ge(c.sem["pe"], pev)
            last = None
            for dqt in range(2):
                for lc in range(4):
                    reg = dqt * 4 + lc
                    for par in range(2):
                        j = 2 * dqt + par
                        last = v.tensor_scalar(
                            _sl[j][0:64, lc * 512:(lc + 1) * 512],
                            c.PS[64 * par:64 * par + 64,
                                 reg * 512:(reg + 1) * 512],
                            _bias[64 * par:64 * par + 64,
                                  2 * grp + dqt:2 * grp + dqt + 1],
                            None, op0=OP.add)
            _inc(c, last, "ac", 1)
        b.on("vector", vev)

    dv_bef = c.cnt["dv"]

    ac_aux = c.cnt["ac"]

    def vaux(v):
        v.wait_ge(c.sem["ac"], ac_aux)
        last = None
        for j in range(4):
            last = v.memset(c.ksl[j][64:65, :], 1.0)
        _inc(c, last, "dv", 1)
    b.on("vector", vaux)

    def saux(s):
        s.wait_ge(c.sem["ac"], ac_aux)
        s.wait_ge(c.sem["dv"], c.aug_dv)
        for j in range(4):
            _inc(c, s.dma_start(
                out=c.qsl[j][64:65, :],
                in_=c.augT[4 * grp + j:4 * grp + j + 1, :]), "sk", 16)
    b.on("sync", saux)

    ac_bef2 = c.cnt["ac"]

    def vround(t, rt):
        reg = rt % 8
        for ch in range(8):
            t.matmul(c.PS[:, reg * 512:reg * 512 + 256],
                     c.xnT[:, ch * L + rt * 128:ch * L + rt * 128 + 128],
                     wvsl[:, ch * 256:(ch + 1) * 256],
                     start=(ch == 0), stop=False)
        t.matmul(c.PS[:, reg * 512:reg * 512 + 256],
                 c.onesr[0:1, 0:128],
                 c.bvr[0:1, col0:col0 + 256],
                 start=False, stop=True)

    def tv1(t):
        t.wait_ge(c.sem["gp"], gpv)
        t.wait_ge(c.sem["ac"], ac_bef2)
        t.wait_ge(c.sem["dv"], c.cnt["dv"])
        for rt in range(8):
            vround(t, rt)
        _inc(c, t.last, "pe", 1)
    b.on("tensor", tv1)
    pev1 = c.cnt["pe"]

    def vevh(v, half):
        for r8 in range(8):
            rt = half * 8 + r8
            src = fancy(c.PS[:, r8 * 512:r8 * 512 + 256],
                        [[64, 4], [1, 64]])
            dst = fancy(c.vfull[:, rt * 1040:(rt + 1) * 1040],
                        [[65, 4], [1, 64]], 65 * 4 * grp)
            last = v.tensor_copy(dst, src)
        _inc(c, last, "dv", 1)

    def vh0(v):
        v.wait_ge(c.sem["pe"], pev1)
        vevh(v, 0)
    b.on("vector", vh0)
    dv_h0 = c.cnt["dv"]

    def tv2(t):
        t.wait_ge(c.sem["dv"], dv_h0)
        for rt in range(8, NT):
            vround(t, rt)
        _inc(c, t.last, "pe", 1)
    b.on("tensor", tv2)
    pev2v = c.cnt["pe"]

    def vh1(v):
        v.wait_ge(c.sem["pe"], pev2v)
        vevh(v, 1)
    b.on("vector", vh1)

    if grp == 0:
        def vone(v):
            v.wait_ge(c.sem["dv"], c.cnt["dv"])
            last = None
            for rt in range(NT):
                ap = fancy(c.vfull[:, rt * 1040:(rt + 1) * 1040],
                           [[65, 16], [1, 1]], 64)
                last = v.memset(ap, 1.0)
            _inc(c, last, "dv", 1)
        b.on("vector", vone)


def emit_head(b, grp, j):
    nc, c = b.nc, b.c
    h = 4 * grp + j

    for lt in range(NT):
        buf = lt % 2
        treg = 128 * buf
        sk_bef = c.cnt["sk"]

        dv_pre0 = c.cnt["dv"]

        def tt_mm(t, _lt=lt, _buf=buf, _tr=treg):
            t.wait_ge(c.sem["vv"], c.cnt["vv"])
            if _lt == 0:
                t.wait_ge(c.sem["dv"], dv_pre0)
            t.matmul(c.PS[:, _tr:_tr + 73],
                     c.qsl[j][0:64, _lt * 128:(_lt + 1) * 128],
                     c.demT[:, :], start=True, stop=True)
            _inc(c, t.last, "pe", 1)
        b.on("tensor", tt_mm)
        pev = c.cnt["pe"]

        def vt(v, _lt=lt, _buf=buf, _tr=treg):
            v.wait_ge(c.sem["pe"], pev)
            v.wait_ge(c.sem["sk"], sk_bef)
            tp = c.tpad[:, _buf * PADL:(_buf + 1) * PADL]
            vsync(c, v, v.tensor_scalar(
                tp[:, 127:200], c.PS[:, _tr:_tr + 73],
                c.PS[:, _tr:_tr + 1], None, op0=OP.subtract))
            vsync(c, v, v.tensor_scalar(
                c.acol[:, :], c.PS[:, _tr + 72:_tr + 73],
                c.PS[:, _tr:_tr + 1], None, op0=OP.subtract))
            vsync(c, v, v.tensor_scalar(
                tp[:, 200:PADL], tp[:, 200:PADL], 0.0, c.acol[:, :],
                op0=OP.mult, op1=OP.add))
            _inc(c, v.nop(), "vv", 1)
        b.on("vector", vt)
        vvv = c.cnt["vv"]

        def swr(s, _lt=lt, _buf=buf):
            s.wait_ge(c.sem["vv"], vvv)
            dst = c.d_skew.ap().copy()
            dst.ap = bass_rust.VecI64Pair([[PADL, 128], [1, PADL]])
            dst.offset = _buf * 128 * PADL
            _inc(c, s.dma_start(out=dst,
                                in_=c.tpad[:, _buf * PADL:(_buf + 1) * PADL]),
                 "sk", 16)
        b.on("sync", swr)
        skv = c.cnt["sk"]

        def srd(s, _lt=lt, _buf=buf):
            s.wait_ge(c.sem["sk"], skv)
            src = c.d_skew.ap().copy()
            src.ap = bass_rust.VecI64Pair([[PADL - 1, 128], [1, WIN]])
            src.offset = _buf * 128 * PADL + 127
            _inc(c, s.dma_start(out=c.bandl[:, _lt * WIN:(_lt + 1) * WIN],
                                in_=src), "sk", 16)
        b.on("sync", srd)

    sk_all = c.cnt["sk"]

    def tbt(t):
        t.wait_ge(c.sem["sk"], sk_all)
        t.wait_ge(c.sem["dv"], c.cnt["dv"])
        for lt in range(NT):
            s0 = lt * WIN
            t.transpose(c.PSb[:, 4096 + lt * 128:4096 + lt * 128 + 128],
                        c.bandl[:, s0 + 64:s0 + 192], c.ident[:, :])
            if lt >= 1:
                t.transpose(
                    c.PSb[64:128, 6144 + (lt - 1) * 128:6144 + (lt - 1) * 128 + 128],
                    c.bandl[:, s0:s0 + 64], c.ident[:, :],
                    tile_position=(0, 64))
            if lt <= 14:
                t.transpose(
                    c.PSb[0:32, 6144 + (lt + 1) * 128:6144 + (lt + 1) * 128 + 128],
                    c.bandl[:, s0 + 192:s0 + 224], c.ident[:, :])
        _inc(c, t.last, "pe", 1)
    b.on("tensor", tbt)
    pev = c.cnt["pe"]

    def vbt(v):
        v.wait_ge(c.sem["pe"], pev)
        v.tensor_copy(c.bandTB[:, :], c.PSb[:, 4096:6144])
        v.tensor_copy(c.bandTAC[0:32, 128:2048], c.PSb[0:32, 6272:8192])
        _inc(c, v.tensor_copy(c.bandTAC[64:128, 0:1920],
                              c.PSb[64:128, 6144:8064]), "dv", 1)
    b.on("vector", vbt)
    dv_band = c.cnt["dv"]

    for rt in range(NT):
        ac_bef = c.cnt["ac"]

        def ts(t, _rt=rt):
            t.wait_ge(c.sem["dv"], dv_band)
            t.wait_ge(c.sem["ac"], ac_bef)
            r0 = _rt * 128
            aug_end = max(0, (_rt - 1) * 128)

            mms = []

            def add(bank, kw):
                mms.append([bank, kw])

            w0 = 0
            while w0 < aug_end:
                w1 = min(w0 + 512 - (w0 % 512), aug_end)
                add(w0 // 512, dict(
                    out=c.PS[:, w0:w1], lhsT=c.ksl[j][0:65, r0:r0 + 128],
                    rhs=c.qsl[j][0:65, w0:w1]))
                w0 = w1
            w0 = r0
            while w0 < L:
                w1 = min(w0 + 512 - (w0 % 512), L)
                add(w0 // 512, dict(
                    out=c.PS[:, w0:w1], lhsT=c.ksl[j][0:64, r0:r0 + 128],
                    rhs=c.qsl[j][0:64, w0:w1]))
                w0 = w1
            if _rt >= 1:
                w0, w1 = aug_end, aug_end + 128
                mb = w0 // 512
                add(mb, dict(out=c.PS[0:32, w0:w1],
                             lhsT=c.ksl[j][0:64, r0:r0 + 32],
                             rhs=c.qsl[j][0:64, w0:w1]))
                add(mb, dict(out=c.PS[32:64, w0:w1],
                             lhsT=c.ksl[j][0:65, r0 + 32:r0 + 64],
                             rhs=c.qsl[j][0:65, w0:w1],
                             tile_position=(0, 32)))
                add(mb, dict(out=c.PS[64:128, w0:w1],
                             lhsT=c.ksl[j][0:65, r0 + 64:r0 + 128],
                             rhs=c.qsl[j][0:65, w0:w1],
                             tile_position=(0, 64)))
            add(_rt // 4, dict(out=c.PS[:, r0:r0 + 128], lhsT=c.ident[:, :],
                               rhs=c.bandTB[:, r0:r0 + 128]))
            if _rt <= 14:
                add((_rt + 1) // 4, dict(
                    out=c.PS[:, (_rt + 1) * 128:(_rt + 1) * 128 + 128],
                    lhsT=c.ident[64:128, :],
                    rhs=c.bandTAC[64:128, r0:r0 + 128],
                    tile_position=(64, 0)))
            if _rt >= 1:
                add((_rt - 1) // 4, dict(
                    out=c.PS[:, (_rt - 1) * 128:(_rt - 1) * 128 + 128],
                    lhsT=c.ident[0:32, :],
                    rhs=c.bandTAC[0:32, r0:r0 + 128]))

            first_i = {}
            last_i = {}
            for i, (bank, _) in enumerate(mms):
                first_i.setdefault(bank, i)
                last_i[bank] = i
            for i, (bank, kw) in enumerate(mms):
                t.matmul(kw.pop("out"), kw.pop("lhsT"), kw.pop("rhs"),
                         start=(first_i[bank] == i), stop=(last_i[bank] == i),
                         skip_group_check=True, **kw)
            _inc(c, t.last, "pe", 1)
        b.on("tensor", ts)
        pev = c.cnt["pe"]

        def sx(s):
            s.wait_ge(c.sem["pe"], pev)
            _inc(c, s.activation(c.eT[:, 0:L],
                                 c.PS[:, 0:L], AF.Exp, scale=SCALE), "ac", 1)
        b.on("scalar", sx)
        acv = c.cnt["ac"]

        def tav(t, _rt=rt):
            t.wait_ge(c.sem["ac"], acv)
            for nb in range(4):
                t.matmul(c.PS[0:65, 2048 + nb * 512:2048 + (nb + 1) * 512],
                         c.vfull[:, _rt * 1040 + 65 * h:_rt * 1040 + 65 * h + 65],
                         c.eT[:, nb * 512:(nb + 1) * 512],
                         start=(_rt == 0), stop=(_rt == 15),
                         skip_group_check=True)
            _inc(c, t.last, "pe", 1)
        b.on("tensor", tav)

    pe_av = c.cnt["pe"]

    def vzr(v):
        v.wait_ge(c.sem["pe"], pe_av)
        with nc.allow_low_precision(reason="1/Z bf16: 2^-9 rel on softmax "
                                    "normalizer, well inside tolerance"):
            _inc(c, v.reciprocal(c.zrow[:, :], c.PS[64:65, 2048:4096]),
                 "dv", 1)
    b.on("vector", vzr)
    dvv = c.cnt["dv"]

    def tbz(t):
        t.wait_ge(c.sem["dv"], dvv)
        for nb in range(4):
            t.matmul(c.PS[:, nb * 512:(nb + 1) * 512], c.onesr[0:1, 0:128],
                     c.zrow[:, nb * 512:(nb + 1) * 512],
                     start=True, stop=True)
        _inc(c, t.last, "pe", 1)
    b.on("tensor", tbz)
    pev = c.cnt["pe"]

    def vo(v):
        v.wait_ge(c.sem["pe"], pev)
        vsync(c, v, v.tensor_copy(c.zbc[:, :], c.PS[0:64, 0:L]))
        _inc(c, v.tensor_tensor(
            c.outT[64 * (h & 1):64 * (h & 1) + 64,
                   (h >> 1) * L:(h >> 1) * L + L],
            c.PS[0:64, 2048:4096], c.zbc[:, :], op=OP.mult), "dv", 1)
        v.wait_ge(c.sem["dv"], c.cnt["dv"])
        _inc(c, v.nop(), "ac", 1)
    b.on("vector", vo)


def emit_oproj(b):
    nc, c = b.nc, b.c
    wo = c.wosl

    def gw(g):
        g.wait_ge(c.sem["pe"], c.cnt["pe"])
        _inc(c, g.dma_start(out=wo, in_=c.i["wo_s"]), "gp", 16)
    b.on("gpsimd", gw)
    gpv = c.cnt["gp"]
    dv_all = c.cnt["dv"]

    for lt in range(NT):
        reg = lt % 2
        dv_bef = c.cnt["dv"]

        def to(t, _lt=lt, _reg=reg):
            t.wait_ge(c.sem["gp"], gpv)
            t.wait_ge(c.sem["dv"], dv_all if _lt < 2 else c.cnt["dv"])
            for half in range(2):
                out = c.PS[:, _reg * 2048 + half * 512:
                           _reg * 2048 + half * 512 + 512]
                for ch in range(8):
                    t.matmul(out,
                             c.outT[:, ch * L + _lt * 128:ch * L + _lt * 128 + 128],
                             wo[:, ch * 1024 + half * 512:
                                ch * 1024 + half * 512 + 512],
                             start=(ch == 0), stop=False)
                t.matmul(out, c.onesr[0:1, 0:128],
                         c.bor[0:1, half * 512:half * 512 + 512],
                         start=False, stop=True)
            _inc(c, t.last, "pe", 1)
        b.on("tensor", to)
        pev = c.cnt["pe"]

        par = lt % 2
        st_pre = c.cnt["st"]

        def sxl(s, _lt=lt, _p=par):
            s.wait_ge(c.sem["st"], st_pre)
            _inc(c, s.dma_start(out=c.xb[:, _p * D:(_p + 1) * D],
                                in_=c.d_xr.ap()[_lt * 128:(_lt + 1) * 128, :]),
                 "rb", 16)
        b.on("sync", sxl)
        rbx = c.cnt["rb"]

        def vr(v, _lt=lt, _reg=reg, _p=par):
            v.wait_ge(c.sem["pe"], pev)
            v.wait_ge(c.sem["rb"], rbx)
            _inc(c, v.scalar_tensor_tensor(
                c.xb[:, _p * D:(_p + 1) * D],
                c.PS[:, _reg * 2048:_reg * 2048 + 1024], 1.0,
                c.xb[:, _p * D:(_p + 1) * D], op0=OP.mult, op1=OP.add),
                "dv", 1)
        b.on("vector", vr)
        dvx = c.cnt["dv"]

        def sxs(s, _lt=lt, _p=par):
            s.wait_ge(c.sem["dv"], dvx)
            _inc(c, s.dma_start(out=c.d_xr.ap()[_lt * 128:(_lt + 1) * 128, :],
                                in_=c.xb[:, _p * D:(_p + 1) * D]), "st", 16)
        b.on("sync", sxs)


def emit_out(b):
    nc, c = b.nc, b.c

    def so(s):
        s.wait_ge(c.sem["st"], c.cnt["st"])
        _inc(c, s.dma_start(out=c.o_out, in_=c.d_xr.ap()), "st", 16)
        s.wait_ge(c.sem["st"], c.cnt["st"])
    b.on("sync", so)


def emit_all(b, upto=3):
    emit_p0(b)
    b.barrier()
    if upto == -2:
        emit_attn(b)
        b.barrier()
        emit_out(b)
        return
    if upto >= 1:
        emit_ffn(b)
        b.barrier()
    if upto >= 2:
        emit_attn(b)
        b.barrier()
    if upto >= 3:
        emit_ffn(b)
        b.barrier()
    emit_out(b)


def build_full(upto=3):
    nc, c = build_program()
    with c.es:
        with nc.Block() as block:
            b = Bld(nc, c, block)
            emit_all(b, upto)
    return nc


def kernel(**inputs):
    w = _prep_weights(inputs)
    x_full = np.ascontiguousarray(
        np.asarray(inputs["L0_attn_hidden_states"], dtype=np.float32))
    nc = build_full()
    z_up = np.zeros((SLOTS, D), ml_dtypes.bfloat16)
    z_dn = np.zeros((SLOTS, DH), ml_dtypes.bfloat16)
    in_maps = []
    for core in range(8):
        m = {"x": x_full[core], "d_xg_up": z_up, "d_xg_dn": z_dn}
        m.update(w)
        in_maps.append(m)
    res = run_bass_kernel_spmd(nc, in_maps, core_ids=list(range(8)))
    out = np.stack([np.asarray(r["out"], dtype=np.float32)
                    for r in res.results], axis=0)
    return out
